# version 13
# baseline (speedup 1.0000x reference)
import numpy as np
import ml_dtypes
import concourse.bass as bass
import concourse.mybir as mybir
from concourse.bass_utils import run_bass_kernel_spmd

F32 = mybir.dt.float32
BF16 = mybir.dt.bfloat16
I32 = mybir.dt.int32
AF = mybir.ActivationFunctionType
ALU = mybir.AluOpType
AX = mybir.AxisListType

B = 8
L = 4096
D = 1024
DFF = 4096
NS = 16
KC = 4
T = 512
NT = L // T
HT = 256
NHT = L // HT
DC = D // 128
FC = DFF // 128
EW = T + 3
EPS_LN = 1e-5
EPS_A = 1e-4
RG = [[0, 1, 2, 3, 4, 5, 6, 7]]

ROUND_MODE = "rint"
SKIP_CC = False
PH1_HALVES = NHT
DO_WQ = 5
DO_AMAXF = True
WQ_NOQUANT = False


class Sched:

    def __init__(self):
        self.items = []
        self.counts = {}

    def val(self, sem):
        return self.counts.get(sem, 0)

    def wait(self, eng, sem, v):
        if v > 0:
            self.items.append(("w", eng, sem, v))

    def mark(self, eng, sem, semh_ref=None):
        self.counts[sem] = self.counts.get(sem, 0) + 1
        v = self.counts[sem]
        self.items.append(("m", eng, sem, v))
        return v

    def op(self, eng, fn, inc=None, waits=None):
        if waits:
            for (s, v) in waits:
                self.wait(eng, s, v)
        incs = []
        if inc:
            for (s, a) in inc:
                if a == "dflt":
                    self.counts[s] = self.counts.get(s, 0) + 1
                else:
                    assert a > 0
                    self.counts[s] = self.counts.get(s, 0) + a
                incs.append((s, a))
        self.items.append(("o", eng, fn, incs))
        out = {}
        for (s, a) in incs:
            out[s] = self.counts[s]
        return out


def build_program(stage=4, debug_taps=False):
    nc = bass.Bass("TRN2", target_bir_lowering=False)

    dx = nc.dram_tensor("x", [L, D], F32, kind="ExternalInput").ap()
    dconvw = nc.dram_tensor("conv_w", [128, DC * KC], F32, kind="ExternalInput").ap()
    dconvb = nc.dram_tensor("conv_b", [DC, 128], F32, kind="ExternalInput").ap()
    dwin = nc.dram_tensor("Win", [128, DC * 80], F32, kind="ExternalInput").ap()
    dbin = nc.dram_tensor("b_in", [1, 80], F32, kind="ExternalInput").ap()
    dwouts = nc.dram_tensor("Wout_s", [NS, D], F32, kind="ExternalInput").ap()
    dbouts = nc.dram_tensor("b_out_s", [DC, 128], F32, kind="ExternalInput").ap()
    dalog = nc.dram_tensor("A_log", [1, NS], F32, kind="ExternalInput").ap()
    dwinf = nc.dram_tensor("W_in_f", [D, DFF], F32, kind="ExternalInput").ap()
    dbinf = nc.dram_tensor("b_in_f", [FC, 128], F32, kind="ExternalInput").ap()
    dwoutf = nc.dram_tensor("W_out_f", [DFF, D], F32, kind="ExternalInput").ap()
    dboutf = nc.dram_tensor("b_out_f", [DC, 128], F32, kind="ExternalInput").ap()
    dident = nc.dram_tensor("ident", [128, 128], F32, kind="ExternalInput").ap()
    dident_bf = nc.dram_tensor("ident_bf", [128, 128], BF16, kind="ExternalInput").ap()
    dones = nc.dram_tensor("ones_r", [1, 512], F32, kind="ExternalInput").ap()
    dR = nc.dram_tensor("Rmat", [80, 48], F32, kind="ExternalInput").ap()

    dout = nc.dram_tensor("out", [L, D], F32, kind="ExternalOutput").ap()
    tap = "ExternalOutput" if debug_taps else "Internal"
    dxn = nc.dram_tensor("xn_spill", [NT, 128, DC * T], F32, kind=tap).ap()
    dhid = nc.dram_tensor("hid_spill", [NT, FC // 2, 128, 1024], F32, kind=tap).ap()
    dwsim = nc.dram_tensor("wsim_stage", [FC, 128, 1024], BF16, kind=tap).ap()
    dwinT = nc.dram_tensor("winq_T", [DC, DFF, 128], BF16).ap()
    dwq = nc.dram_tensor("winq_tap", [1, 128, 32768], BF16,
                         kind=tap if debug_taps else "Internal").ap()
    dccxi = nc.dram_tensor("ccx_in", [1, 128], F32)
    dccxo = nc.dram_tensor("ccx_out", [1, 128], F32)
    dcchi = nc.dram_tensor("cch_in", [1, 128], F32)
    dccho = nc.dram_tensor("cch_out", [1, 128], F32)
    dscal = nc.dram_tensor("scal_tap", [1, 8], F32,
                           kind=tap if debug_taps else "Internal").ap()
    dchb = nc.dram_tensor("chb_tap", [NS, L], mybir.dt.bfloat16,
                          kind=tap if debug_taps else "Internal").ap()

    import contextlib
    ctx = contextlib.ExitStack()
    sb = lambda name, shape, dt: ctx.enter_context(nc.sbuf_tensor(name, shape, dt))

    wbuf = sb("wbuf", [128, 32768], BF16)
    chb = sb("chb", [NS, L], BF16)
    xbuf = [sb(f"xbuf{i}", [128, 2048], F32) for i in range(2)]
    xnf = [sb(f"xnf{i}", [128, DC * T], F32) for i in range(2)]
    ar2 = sb("ar2", [128, 33024], mybir.dt.uint8)
    ar3 = sb("ar3", [128, 22528], mybir.dt.uint8)
    hidb = [sb(f"hidb{i}", [128, 1024], F32) for i in range(2)]
    wsim_bf = [sb(f"wsimbf{i}", [128, 1024], BF16) for i in range(2)]
    u_s = sb("u_s", [128, 1024], F32)
    i_s = sb("i_s", [128, 1024], I32)
    f_s = sb("f_s", [128, 1024], F32) if ROUND_MODE == "floor_cmp" else u_s
    ident = sb("identt", [128, 128], F32)
    ident_bf = sb("ident_bft", [128, 128], BF16)
    ones_r = sb("ones_rt", [1, 512], F32)
    rmat = sb("rmatt", [80, 48], F32)
    wt_bf = [sb(f"wt{k}", [128, DC * 80], BF16) for k in range(KC)]
    wc_sb = sb("wc_sbt", [128, DC * KC], F32)
    binf_sb = sb("binf_sbt", [128, FC], F32)
    boutf_sb = sb("boutf_sbt", [128, DC], F32)
    bouts_sb = sb("bouts_sbt", [128, DC], F32)
    cbf_sb = sb("cbf_sbt", [128, DC], F32)
    wouts_bf = sb("wouts_bft", [NS, D], BF16)
    binT = sb("binT", [80, 1], F32)
    bcomb = sb("bcomb", [80, 1], F32)
    bd = sb("bd", [NS, 6], F32)
    bdrow = sb("bdrow", [1, 3 * NS], F32)
    aset = sb("aset", [NS, 8], F32)
    bca = sb("bca", [128, 4], F32)
    scal = sb("scal", [1, 8], F32)
    amax_acc = sb("amax_acc", [128, 2], F32)
    stats = sb("stats", [128, 12], F32)
    statsW = sb("statsW", [128, 64], F32)
    statsW2 = sb("statsW2", [128, 32], F32)
    hstrip = sb("hstrip", [128, 128], F32)

    ar2b = ar2[:, :].bitcast(BF16)
    ext = [ar2b[:, i * (EW * DC):(i + 1) * (EW * DC)] for i in range(2)]
    qb = [ar2b[:, 2 * EW * DC + i * 4096: 2 * EW * DC + (i + 1) * 4096] for i in range(2)]
    ar2f = ar2[:, :].bitcast(F32)
    outb = [ar2f[:, i * 4096:(i + 1) * 4096] for i in range(2)]
    win_sb = ar2f[:, 0:640]
    sqs = ar2f[:, 640:1664]
    ld8a = ar2f[0:8, 1664:1792]
    ld8b = ar2f[0:8, 1920:2048]
    ld8c = ar2f[0:8, 2048:2176]
    ld32 = ar2f[0:32, 1792:1920]

    ar3f = ar3[:, :].bitcast(F32)
    wslc = [ar3f[:, i * 128:(i + 1) * 128] for i in range(2)]
    wtk = ar3f[:, 256:1280]
    wsimT = ar3f[:, 1280:1792].bitcast(BF16)
    p80_sb = ar3f[0:80, 0:512]
    _o = 2048
    dl_sb = ar3f[0:NS, _o:_o + 512]
    bi_sb = ar3f[0:NS, _o + 512:_o + 1024]
    c_sb = ar3f[0:NS, _o + 1024:_o + 1536]
    s1_sb = ar3f[0:NS, _o + 1536:_o + 2048]
    s2_sb = ar3f[0:NS, _o + 2048:_o + 2560]
    hb_sb = [ar3f[0:NS, _o + 2560 + i * 512:_o + 2560 + (i + 1) * 512] for i in range(2)]
    sg_sb = [ar3f[:, i * 512:(i + 1) * 512] for i in range(8)]
    g_sb = [ar3f[:, 4096 + i * 512:4096 + (i + 1) * 512] for i in range(2)]
    ccst = ar3f[0:1, 1792:1920]
    b80 = ar3f[0:1, 1920:2000]
    alog_row = ar3f[0:1, 2000:2016]

    wof = [xbuf[i][:, 0:1024] for i in range(2)]
    xr3 = [xbuf[0][:, 1024:2048], xbuf[1][:, 1024:2048],
           xbuf[0][:, 0:1024], xbuf[1][:, 0:1024]]
    qh = [xnf[i][:, :].bitcast(BF16) for i in range(2)]
    wtk_b = [None, None]
    wsimT_b = [None, None]

    wtk_b[0] = wtk
    wtk_b[1] = ar2f[:, 2176:3200]
    wsimT_b[0] = wsimT
    wsimT_b[1] = ar2f[:, 3200:3712].bitcast(BF16)
    wslc8 = [hidb[0][:, :], hidb[1][:, :]]
    pbk = [ctx.enter_context(nc.psum_tensor(f"pb{i}", [128, 512], F32)) for i in range(8)]
    pb = [p[:, :] for p in pbk]
    pb4bf = pbk[4][:, :].bitcast(BF16)
    pb5bf = pbk[5][:, :].bitcast(BF16)

    sem_names = [
        "cc", "cc1", "cc2", "dv_fl", "ac_fl", "dv_xn",
        "pe_t", "pe_w", "pe_wb", "pe_proj", "pe_r", "pe_mm1", "pe_mm2",
        "pe_y", "pe_tr", "pe_sc",
        "dv_ln", "dv_ax", "dv_q", "dv_w", "dv_wo", "dv_scan", "dv_sg",
        "dv_res", "dv_qh", "dv_h", "dv_sc", "dv_ext", "pq",
        "ac_xn", "ac_ev", "ac_g", "ac_sl", "ac_r", "ac_w", "ac_p80",
    ]
    streams = {"ldx": 2, "ldxn": 2, "ldh": 2, "ldw": 2, "ldwo": 2, "ldws": 1,
               "ldsm": 1, "ldx3": 4, "stxn": 2, "sth": 2, "stws": 2, "sto": 2,
               "ldwb": 1, "sttap": 1, "stwq": 2, "ldwq": 1}
    for nm, nb in streams.items():
        for b in range(nb):
            sem_names.append(f"{nm}{b}")
    semh = {n: ctx.enter_context(nc.semaphore(n)) for n in sem_names}
    S = Sched()
    dcount = {}

    def xop(eng, fn, marks, waits=None):

        sem = "dv_fl" if eng == "vector" else "ac_fl"
        r = S.op(eng, fn, inc=[(sem, 1)], waits=waits)
        S.wait(eng, sem, r[sem])
        out = {}
        for m in marks:
            out[m] = S.mark(eng, m)
        return out

    def warm_mm(bank):
        S.op("tensor", lambda e, bank=bank: e.matmul(out=bank[0:64, 448:512],
                                                     lhsT=ident[0:1, 0:64],
                                                     rhs=ident[0:1, 64:128],
                                                     start=True, stop=True))

    def fop(eng, fn, inc=None, waits=None):
        if inc:
            r = S.op(eng, fn, inc=inc, waits=waits)
            s0 = inc[0][0]
            S.wait(eng, s0, r[s0])
            return r
        sem = "dv_fl" if eng == "vector" else "ac_fl"
        r = S.op(eng, fn, inc=[(sem, 1)], waits=waits)
        S.wait(eng, sem, r[sem])
        return r

    def dma(eng, stream, fn, waits=None):
        nb = streams[stream]
        i = dcount.get(stream, 0)
        dcount[stream] = i + 1
        sem = f"{stream}{i % nb}"
        S.op(eng, fn, inc=[(sem, 16)], waits=waits)
        return (sem, 16 * (i // nb + 1))

    def dtotal(stream):
        nb = streams[stream]
        c = dcount.get(stream, 0)
        return [(f"{stream}{b}", 16 * ((c + nb - 1 - b) // nb))
                for b in range(nb) if (c + nb - 1 - b) // nb > 0]

    def _ld(dst, src):
        return lambda e, d=dst, s=src: e.dma_start(out=d, in_=s)

    for dst, src in [(ident[:, :], dident), (ident_bf[:, :], dident_bf),
                     (ones_r[:, :], dones), (rmat[:, :], dR),
                     (b80, dbin), (alog_row, dalog),
                     (win_sb, dwin), (wc_sb[:, :], dconvw),
                     (ld8a, dconvb), (ld8b, dbouts), (ld8c, dboutf),
                     (ld32, dbinf)]:
        dma("sync", "ldsm", _ld(dst, src))
    w_ldsm = dtotal("ldsm")
    v_wouts = dma("gpsimd", "ldwb",
                  lambda e: e.dma_start(out=wouts_bf[:, :], in_=dwouts))

    bias_jobs = [(ld8a, cbf_sb, 8), (ld8b, bouts_sb, 8), (ld8c, boutf_sb, 8),
                 (ld32, binf_sb, 32)]
    for (stg, dst, rows) in bias_jobs:
        S.op("tensor",
             lambda e, st=stg, r=rows: e.transpose(pb[5][0:128, 0:r], st[0:r, :],
                                                   ident[0:r, 0:r]),
             inc=[("pe_sc", 1)],
             waits=w_ldsm + [("ac_w", S.val("ac_w"))])
        xop("scalar",
            lambda e, d=dst, r=rows: e.copy(out=d[:, 0:r], in_=pb[5][0:128, 0:r]),
            marks=["ac_w"], waits=[("pe_sc", S.val("pe_sc"))])
    S.op("tensor", lambda e: e.transpose(pb[5][0:80, 0:1], b80,
                                         ident[0:1, 0:1]),
         inc=[("pe_sc", 1)], waits=w_ldsm + [("ac_w", S.val("ac_w"))])
    xop("scalar", lambda e: e.copy(out=binT[:, :], in_=pb[5][0:80, 0:1]),
        marks=["ac_w"], waits=[("pe_sc", S.val("pe_sc"))])
    S.op("tensor", lambda e: e.transpose(pb[5][0:NS, 0:1], alog_row,
                                         ident[0:1, 0:1]),
         inc=[("pe_sc", 1)], waits=[("ac_w", S.val("ac_w"))])
    xop("scalar", lambda e: e.copy(out=aset[:, 0:1], in_=pb[5][0:NS, 0:1]),
        marks=["ac_w"], waits=[("pe_sc", S.val("pe_sc"))])
    v_ac_bias = S.val("ac_w")

    for c in range(DC):
        for kk in range(KC):
            S.op("vector",
                 lambda e, c=c, kk=kk: e.tensor_scalar(
                     out=wt_bf[kk][:, c * 80:(c + 1) * 80],
                     in0=win_sb[:, c * 80:(c + 1) * 80],
                     scalar1=wc_sb[:, c * KC + kk:c * KC + kk + 1], scalar2=None,
                     op0=ALU.mult),
                 waits=(w_ldsm if (c == 0 and kk == 0) else None))

    xop("scalar", lambda e: e.activation(out=aset[:, 1:2], in_=aset[:, 0:1],
                                         func=AF.Exp),
        marks=["ac_w"], waits=[("ac_w", v_ac_bias)])
    vA = S.val("ac_w")
    fop("vector", lambda e: e.tensor_scalar(out=aset[:, 2:3], in0=aset[:, 1:2],
                                            scalar1=-1.0, scalar2=None, op0=ALU.mult),
        waits=[("ac_w", vA)])
    fop("vector", lambda e: e.tensor_scalar(out=aset[:, 3:4], in0=aset[:, 1:2],
                                            scalar1=EPS_A, scalar2=None, op0=ALU.is_lt))
    fop("vector", lambda e: e.tensor_scalar(out=aset[:, 4:5], in0=aset[:, 3:4],
                                            scalar1=-1.0, scalar2=1.0,
                                            op0=ALU.mult, op1=ALU.add))
    vfl = S.op("vector", lambda e: e.scalar_tensor_tensor(out=aset[:, 5:6],
                                                          in0=aset[:, 2:3],
                                                          scalar=aset[:, 4:5],
                                                          in1=aset[:, 3:4],
                                                          op0=ALU.mult, op1=ALU.add),
               inc=[("dv_fl", 1)])["dv_fl"]
    vfl = S.op("vector", lambda e: e.reciprocal(out=aset[:, 6:7], in_=aset[:, 5:6]),
               inc=[("dv_fl", 1)], waits=[("dv_fl", vfl)])["dv_fl"]
    fop("vector", lambda e: e.tensor_scalar(out=aset[:, 6:7], in0=aset[:, 6:7],
                                            scalar1=aset[:, 4:5], scalar2=None,
                                            op0=ALU.mult),
        waits=[("dv_fl", vfl)])

    for c in range(DC):
        S.op("tensor",
             lambda e, c=c: e.matmul(out=pb[5][0:80, 0:1],
                                     lhsT=win_sb[:, c * 80:(c + 1) * 80],
                                     rhs=cbf_sb[:, c:c + 1],
                                     start=(c == 0), stop=(c == DC - 1)),
             inc=([("pe_sc", 1)] if c == DC - 1 else None),
             waits=(w_ldsm + [("ac_w", v_ac_bias)] if c == 0 else None))
    fop("vector", lambda e: e.tensor_tensor(out=bcomb[:, :], in0=pb[5][0:80, 0:1],
                                            in1=binT[:, :], op=ALU.add),
        inc=[("dv_sc", 1)], waits=[("pe_sc", S.val("pe_sc")), ("ac_w", S.val("ac_w"))])
    v_bcomb = S.val("dv_sc")
    for (j3, rhs_t) in ((0, bcomb), (3, binT)):
        for i3 in range(3):
            S.op("tensor",
                 lambda e, i3=i3, rhs_t=rhs_t: e.matmul(
                     out=pb[5][0:NS, 0:1], lhsT=rmat[:, i3 * NS:(i3 + 1) * NS],
                     rhs=rhs_t[:, :], start=True, stop=True),
                 inc=[("pe_sc", 1)],
                 waits=[("dv_sc", v_bcomb), ("ac_w", S.val("ac_w"))])
            xop("scalar",
                lambda e, j3=j3, i3=i3: e.copy(out=bd[:, j3 + i3:j3 + i3 + 1],
                                               in_=pb[5][0:NS, 0:1]),
                marks=["ac_w"], waits=[("pe_sc", S.val("pe_sc"))])

    for i3 in range(3):
        S.op("tensor",
             lambda e, i3=i3: e.transpose(pb[5][0:1, 0:NS], bd[:, i3:i3 + 1],
                                          ident[0:NS, 0:NS]),
             inc=[("pe_sc", 1)], waits=[("ac_w", S.val("ac_w"))])
        xop("scalar",
            lambda e, i3=i3: e.copy(out=bdrow[0:1, i3 * NS:(i3 + 1) * NS],
                                    in_=pb[5][0:1, 0:NS]),
            marks=["ac_w"], waits=[("pe_sc", S.val("pe_sc"))])
    fop("vector", lambda e: e.memset(stats[:, 8:9], EPS_LN))
    S.op("vector", lambda e: e.memset(ccst[:, :], 0.0))
    S.op("vector", lambda e: e.memset(amax_acc[:, :], 0.0), inc=[("dv_sc", 1)])
    v_init = S.val("dv_sc")

    def round_scaled(pref, width, in_ap, out_ap, scale_ap, post_scale_ap,
                     first_waits, first_inc=None):
        vsem, psem = pref
        w = width
        fi = list(first_inc) if first_inc else [(vsem, 1)]
        fsem = fi[0][0]
        if ROUND_MODE == "floor_cmp":
            r = S.op("vector",
                     lambda e: e.tensor_scalar(out=u_s[:, 0:w], in0=in_ap,
                                               scalar1=scale_ap, scalar2=0.5,
                                               op0=ALU.mult, op1=ALU.add),
                     inc=fi, waits=first_waits)
            va = r[fsem]
            S.op("gpsimd", lambda e: e.tensor_copy(out=i_s[:, 0:w], in_=u_s[:, 0:w]),
                 waits=[(fsem, va)])
            vb = S.op("gpsimd", lambda e: e.tensor_copy(out=f_s[:, 0:w],
                                                        in_=i_s[:, 0:w]),
                      inc=[(psem, 1)])[psem]
            S.op("vector", lambda e: e.tensor_tensor(out=u_s[:, 0:w], in0=f_s[:, 0:w],
                                                     in1=u_s[:, 0:w], op=ALU.is_gt),
                 waits=[(psem, vb)])
            S.op("vector", lambda e: e.tensor_tensor(out=f_s[:, 0:w], in0=f_s[:, 0:w],
                                                     in1=u_s[:, 0:w], op=ALU.subtract))
            ps = post_scale_ap if post_scale_ap is not None else 1.0
            vz = S.op("vector",
                      lambda e, ps=ps: e.tensor_scalar(out=out_ap, in0=f_s[:, 0:w],
                                                       scalar1=ps, scalar2=None,
                                                       op0=ALU.mult),
                      inc=[(vsem, 1)])[vsem]
        else:
            S.op("vector",
                 lambda e: e.tensor_scalar(out=u_s[:, 0:w], in0=in_ap,
                                           scalar1=scale_ap, scalar2=None,
                                           op0=ALU.mult),
                 inc=fi if first_inc else None, waits=first_waits)
            S.op("vector", lambda e: e.tensor_copy(out=i_s[:, 0:w], in_=u_s[:, 0:w]))
            ps = post_scale_ap if post_scale_ap is not None else 1.0
            vz = S.op("vector",
                      lambda e, ps=ps: e.tensor_scalar(out=out_ap, in0=i_s[:, 0:w],
                                                       scalar1=ps, scalar2=None,
                                                       op0=ALU.mult),
                      inc=[(vsem, 1)])[vsem]
        return vz

    v_ldw_pc = {}
    v_pe_w_pc = {}
    v_wqst = {}
    v_ac_wtk = {}
    v_dvw_u = {}
    v_wq_done = {}

    def sched_winq_loads(k):
        v_ldw_pc[k] = dma(
            "sync", "ldw",
            lambda e, k=k: e.dma_start(
                out=wslc8[k % 2][:, :].rearrange("p (c w) -> p c w", w=128),
                in_=dwinf[:, k * 128:(k + 1) * 128].rearrange(
                    "(c p) w -> p c w", p=128)),
            waits=[("pe_w", max(0, (k - 1) * DC))])

    def sched_winq_pair(k):
        if DO_WQ < 2:
            return
        banks = {k: (pb[2], pb[3], pb4bf), k + 1: (pb[6], pb[7], pb5bf)}
        wtks = {k: wtk_b[0], k + 1: wtk_b[1]}
        wsims = {k: wsimT_b[0], k + 1: wsimT_b[1]}
        for kk in (k, k + 1):
            bA, bB, _ = banks[kk]
            for c in range(DC):
                g = kk * DC + c
                wl = [v_ldw_pc[kk]]
                if c == 0:
                    wl.append(("ac_w", S.val("ac_w")))
                S.op("tensor",
                     lambda e, kk=kk, c=c, bA=bA, bB=bB: e.transpose(
                         (bA if c < 4 else bB)[:, (c % 4) * 128:(c % 4 + 1) * 128],
                         wslc8[kk % 2][:, c * 128:(c + 1) * 128], ident[:, :]),
                     inc=[("pe_w", 1)], waits=wl)
                v_pe_w_pc[g] = S.val("pe_w")
                if c == 1:
                    warm_mm(bA)
        for kk in (k, k + 1):
            bA, bB, _ = banks[kk]
            xop("scalar", lambda e, bA=bA, kk=kk: e.copy(out=wtks[kk][:, 0:512], in_=bA),
                marks=["ac_w"],
                waits=[("pe_w", v_pe_w_pc[kk * DC + 3]), ("dv_w", S.val("dv_w"))])
            xop("scalar", lambda e, bB=bB, kk=kk: e.copy(out=wtks[kk][:, 512:1024], in_=bB),
                marks=["ac_w"], waits=[("pe_w", v_pe_w_pc[kk * DC + 7])])
        vac = S.val("ac_w")
        if DO_WQ < 3:
            return
        S.op("vector", lambda e: e.tensor_reduce(out=stats[:, 6:7], in_=wtks[k][:, :],
                                                 axis=AX.X, op=ALU.max,
                                                 apply_absolute_value=True),
             waits=[("ac_w", vac)])
        vfl = xop("vector", lambda e: e.tensor_reduce(out=stats[:, 7:8],
                                                      in_=wtks[k + 1][:, :],
                                                      axis=AX.X, op=ALU.max,
                                                      apply_absolute_value=True),
                  marks=["dv_fl"])["dv_fl"]
        vfl = xop("vector", lambda e: e.reciprocal(out=stats[:, 9:11],
                                                   in_=stats[:, 6:8]),
                  marks=["dv_fl"], waits=[("dv_fl", vfl)])["dv_fl"]
        fop("vector", lambda e: e.tensor_scalar(out=stats[:, 9:11], in0=stats[:, 9:11],
                                                scalar1=127.0, scalar2=None,
                                                op0=ALU.mult),
            waits=[("dv_fl", vfl)])
        fop("vector", lambda e: e.tensor_scalar(out=stats[:, 6:8], in0=stats[:, 6:8],
                                                scalar1=1.0 / 127.0, scalar2=None,
                                                op0=ALU.mult),
            inc=[("dv_w", 1)])
        for kk in (k, k + 1):
            j = kk - k
            v_dvw_u[kk] = round_scaled(
                ("dv_w", "pq"), 1024, wtks[kk][:, :], wsims[kk][:, :],
                stats[:, 9 + j:10 + j], stats[:, 6 + j:7 + j],
                first_waits=([v_wqst[kk - 2]] if kk - 2 in v_wqst else []))
        if DO_WQ < 4:
            return
        for kk in (k, k + 1):
            v_wqst[kk] = dma(
                "gpsimd", "stwq",
                lambda e, kk=kk: e.dma_start(
                    out=dwinT[:, kk * 128:(kk + 1) * 128, :].rearrange(
                        "c p w -> p c w"),
                    in_=wsims[kk][:, :].rearrange("p (c w) -> p c w", w=128)),
                waits=[("dv_w", v_dvw_u[kk])])
        v_wq_done[k] = S.val("ac_w")

    v_ldx = [0] * (2 * NHT)
    v_spill = [0] * NT
    v_ax = [0] * NHT
    ev_pair_vals = []
    v_sumq = {}

    def sched_ph1_passA(h):
        hb = h % 2
        wl = [("dv_ln", v_sumq[(h - 2, 1)] if h >= 2 else 0)]
        v_ldx[h] = dma(
            "sync", "ldx",
            lambda e, h=h, hb=hb: e.dma_start(
                out=xbuf[hb][:, :].rearrange("p (u d) -> p u d", u=2),
                in_=dx[h * HT:(h + 1) * HT, :].rearrange("(u p) d -> p u d", p=128)),
            waits=wl)
        for u in range(2):
            j = 2 * h + u
            xin = xbuf[hb][:, u * 1024:(u + 1) * 1024]
            S.op("vector",
                 lambda e, xin=xin, j=j: e.tensor_reduce(
                     out=statsW[:, j:j + 1], in_=xin, axis=AX.X, op=ALU.add),
                 waits=[v_ldx[h]])
            wl_sq = [v_ldx[h]]
            prev = (h, 0) if u == 1 else (h - 1, 1)
            if prev in v_sumq:
                wl_sq.append(("dv_ln", v_sumq[prev]))
            vsq = S.op("scalar",
                       lambda e, xin=xin: e.activation(out=sqs[:, :], in_=xin,
                                                       func=AF.Square),
                       inc=[("ac_xn", 1)], waits=wl_sq)["ac_xn"]
            v_sumq[(h, u)] = S.op(
                "vector",
                lambda e, j=j: e.tensor_reduce(out=statsW[:, 32 + j:33 + j],
                                               in_=sqs[:, :], axis=AX.X, op=ALU.add),
                inc=[("dv_ln", 1)], waits=[("ac_xn", vsq)])["dv_ln"]

    def sched_ph1_wide():
        S.op("vector", lambda e: e.tensor_scalar(out=statsW[:, 0:32], in0=statsW[:, 0:32],
                                                 scalar1=1.0 / D, scalar2=None,
                                                 op0=ALU.mult))
        S.op("vector", lambda e: e.tensor_scalar(out=statsW[:, 32:64], in0=statsW[:, 32:64],
                                                 scalar1=1.0 / D, scalar2=None,
                                                 op0=ALU.mult))
        S.op("vector", lambda e: e.tensor_tensor(out=statsW2[:, :], in0=statsW[:, 0:32],
                                                 in1=statsW[:, 0:32], op=ALU.mult))
        vv = xop("vector", lambda e: e.tensor_tensor(out=statsW[:, 32:64],
                                                     in0=statsW[:, 32:64],
                                                     in1=statsW2[:, :], op=ALU.subtract),
                 marks=["dv_ln"])["dv_ln"]
        fop("scalar", lambda e: e.activation(out=statsW[:, 32:64], in_=statsW[:, 32:64],
                                             func=AF.Ln, bias=stats[:, 8:9]),
            waits=[("dv_ln", vv)])
        vr = xop("scalar", lambda e: e.activation(out=statsW[:, 32:64],
                                                  in_=statsW[:, 32:64],
                                                  func=AF.Exp, scale=-0.5),
                 marks=["ac_xn"])["ac_xn"]
        return S.op("vector",
                    lambda e: e.scalar_tensor_tensor(out=statsW[:, 0:32],
                                                     in0=statsW[:, 0:32], scalar=-1.0,
                                                     in1=statsW[:, 32:64],
                                                     op0=ALU.mult, op1=ALU.mult),
                    inc=[("dv_ln", 1)], waits=[("ac_xn", vr)])["dv_ln"]

    def sched_ph1_passB(h, v_wide):
        t = h // 2
        hb = h % 2
        wl = [("dv_ax", v_ax[h - 2] if h >= 2 else 0),
              ("pe_t", max(0, (h - 1) * 4))]
        if h < 2:
            wl.append(("dv_ln", v_sumq[(NHT - 2 + h, 1)]))
        v_ldx[NHT + h] = dma(
            "sync", "ldx",
            lambda e, h=h, hb=hb: e.dma_start(
                out=xbuf[hb][:, :].rearrange("p (u d) -> p u d", u=2),
                in_=dx[h * HT:(h + 1) * HT, :].rearrange("(u p) d -> p u d", p=128)),
            waits=wl)
        for u in range(2):
            j = 2 * h + u
            xin = xbuf[hb][:, u * 1024:(u + 1) * 1024]
            S.op("vector",
                 lambda e, xin=xin, j=j: e.tensor_scalar(
                     out=xin, in0=xin, scalar1=statsW[:, 32 + j:33 + j],
                     scalar2=statsW[:, j:j + 1], op0=ALU.mult, op1=ALU.add),
                 waits=[v_ldx[NHT + h]] + ([("dv_ln", v_wide)] if h == 0 and u == 0 else []))
        S.op("vector", lambda e, hb=hb: e.tensor_reduce(out=stats[:, 6:7],
                                                        in_=xbuf[hb][:, :],
                                                        axis=AX.X, op=ALU.max,
                                                        apply_absolute_value=True))
        v_ax[h] = xop("vector", lambda e: e.tensor_tensor(out=amax_acc[:, 0:1],
                                                          in0=amax_acc[:, 0:1],
                                                          in1=stats[:, 6:7], op=ALU.max),
                      marks=["dv_ax"])["dv_ax"]
        v_xn = S.mark("vector", "dv_xn")
        for pr in range(4):
            gpair = h * 4 + pr
            bank = pr % 2
            for ci in range(2):
                c = pr * 2 + ci
                for u in range(2):
                    wl = []
                    if ci == 0 and u == 0:
                        wl = [("dv_xn", v_xn)]
                        if gpair >= 2:
                            wl.append(("ac_ev", ev_pair_vals[gpair - 2]))
                    S.op("tensor",
                         lambda e, hb=hb, c=c, u=u, bank=bank: e.transpose(
                             pb[bank][:, (c % 2) * 256 + u * 128:
                                      (c % 2) * 256 + (u + 1) * 128],
                             xbuf[hb][:, u * 1024 + c * 128:u * 1024 + (c + 1) * 128],
                             ident[:, :]),
                         inc=([("pe_t", 1)] if (ci, u) == (1, 1) else None),
                         waits=wl)
                    if (ci, u) == (0, 0):
                        warm_mm(pb[bank])
            vpe = S.val("pe_t")
            c0 = pr * 2
            wl = [("pe_t", vpe)]
            if h % 2 == 0 and pr == 0 and t >= 2:
                wl.append(v_spill[t - 2])
            S.op("scalar",
                 lambda e, t=t, h=h, c0=c0, bank=bank: e.copy(
                     out=xnf[t % 2][:, :].rearrange("p (c w) -> p c w", w=T)
                         [:, c0:c0 + 2, (h % 2) * 256:(h % 2) * 256 + 256],
                     in_=pb[bank].rearrange("p (c w) -> p c w", w=256)),
                 inc=[("ac_ev", 1)], waits=wl)
            ev_pair_vals.append(S.val("ac_ev"))
        if h % 2 == 1:
            v_spill[t] = dma(
                "gpsimd", "stxn",
                lambda e, t=t: e.dma_start(out=dxn[t], in_=xnf[t % 2][:, :]),
                waits=[("ac_ev", ev_pair_vals[-1])])

    wqk = 0
    for h in range(max(PH1_HALVES, NHT if DO_WQ else 0)):
        if h < PH1_HALVES:
            sched_ph1_passA(h)
        if wqk < FC and DO_WQ:
            sched_winq_loads(wqk)
            sched_winq_loads(wqk + 1)
            sched_winq_pair(wqk)
            wqk += 2
    v_wide = sched_ph1_wide() if PH1_HALVES else 0
    for h in range(PH1_HALVES):
        sched_ph1_passB(h, v_wide)
    v_wq_all = S.val("ac_w")
    w_stxn_all = dtotal("stxn")

    def sched_amax_final(col, cc_in_d, cc_out_d, sidx, bcol, in_sem, in_val,
                         ccd="cc1"):
        S.op("tensor",
             lambda e, col=col: e.transpose(pb[5][0:1, 0:128], amax_acc[:, col:col + 1],
                                            ident[:, :]),
             inc=[("pe_sc", 1)], waits=[(in_sem, in_val), ("dv_sc", S.val("dv_sc"))])
        xop("vector",
            lambda e: e.tensor_reduce(out=ccst[0:1, 0:1], in_=pb[5][0:1, 0:128],
                                      axis=AX.X, op=ALU.max),
            marks=["dv_sc"], waits=[("pe_sc", S.val("pe_sc"))])
        S.op("gpsimd", lambda e, d=cc_in_d: e.dma_start(out=d[:, :], in_=ccst[:, :]),
             inc=[("cc", 16)], waits=[("dv_sc", S.val("dv_sc"))])
        if SKIP_CC:
            S.op("gpsimd", lambda e, a=cc_in_d, b=cc_out_d: e.dma_start(
                 out=b[:, :], in_=a[:, :]),
                 inc=[(ccd, "dflt")], waits=[("cc", S.val("cc"))])
        else:
            S.op("gpsimd",
                 lambda e, a=cc_in_d, b=cc_out_d: e.collective_compute(
                     "AllReduce", ALU.max, replica_groups=RG,
                     ins=[a.ap().opt()], outs=[b.ap().opt()]),
                 inc=[(ccd, "dflt")], waits=[("cc", S.val("cc"))])
        S.op("gpsimd", lambda e, d=cc_out_d: e.dma_start(out=ccst[:, :], in_=d[:, :]),
             inc=[("cc", 16)], waits=[(ccd, 1)])
        vcc = S.val("cc")
        vfl = S.op("vector", lambda e, sidx=sidx: e.reciprocal(
             out=scal[:, sidx:sidx + 1], in_=ccst[0:1, 0:1]),
             inc=[("dv_fl", 1)], waits=[("cc", vcc)])["dv_fl"]
        fop("vector", lambda e, sidx=sidx: e.tensor_scalar(
            out=scal[:, sidx:sidx + 1], in0=scal[:, sidx:sidx + 1],
            scalar1=32767.0, scalar2=None, op0=ALU.mult),
            waits=[("dv_fl", vfl)])
        vds = xop("vector", lambda e, sidx=sidx: e.tensor_scalar(
            out=scal[:, sidx + 1:sidx + 2], in0=ccst[0:1, 0:1],
            scalar1=1.0 / 32767.0, scalar2=None, op0=ALU.mult),
            marks=["dv_sc"])["dv_sc"]
        S.op("tensor", lambda e, sidx=sidx: e.matmul(out=pb[5][0:128, 0:2],
                                                     lhsT=ones_r[0:1, 0:128],
                                                     rhs=scal[:, sidx:sidx + 2],
                                                     start=True, stop=True),
             inc=[("pe_sc", 1)], waits=[("dv_sc", vds)])
        vb = xop("vector", lambda e, bcol=bcol: e.tensor_copy(
            out=bca[:, bcol:bcol + 2], in_=pb[5][0:128, 0:2]),
            marks=["dv_sc"], waits=[("pe_sc", S.val("pe_sc"))])["dv_sc"]
        return vb

    w_stwq = dtotal("stwq")
    if DO_WQ:
        for c in range(DC):
            dma("sync", "ldwq",
                lambda e, c=c: e.dma_start(
                    out=wbuf[:, c * 4096:(c + 1) * 4096],
                    in_=dwinT[c], transpose=True),
                waits=(w_stwq if c == 0 else None))
    w_ldwq = dtotal("ldwq")
    v_sx = 0
    if DO_AMAXF:
        v_sx = sched_amax_final(0, dccxi, dccxo, 0, 0, "dv_ax", S.val("dv_ax"))
    if debug_taps:
        dma("gpsimd", "sttap", lambda e: e.dma_start(out=dscal, in_=scal[:, :]),
            waits=[("dv_sc", v_sx)])
        dma("gpsimd", "sttap", lambda e: e.dma_start(out=dwq[0], in_=wbuf[:, :]),
            waits=[("ac_w", v_wq_all)])
        for (sm, v) in dtotal("sttap"):
            S.wait("gpsimd", sm, v)

    if stage <= 1:
        for (sm, v) in w_stxn_all:
            S.wait("gpsimd", sm, v)
        S.wait("gpsimd", "cc", S.val("cc"))
        return finish(nc, ctx, S, semh)

    v_ldxn = [0] * NT
    v_q = [0] * NT
    v_ext = [0] * NT
    v_proj = [0] * NT
    v_mm1_t = [0] * NT
    v_gel = {}
    v_scan = [0] * NT
    v_dvh_pair = {}
    v_sth_pair = {}

    def sched_ssm_front(tf, ext_ready_val):
        first = [True]
        for kk in range(KC):
            for c in range(DC):
                wl = None
                if first[0]:
                    wl = [("dv_ext", ext_ready_val), ("ac_p80", S.val("ac_p80")),
                          ("dv_sc", v_init)]
                    first[0] = False
                S.op("tensor",
                     lambda e, kk=kk, c=c, tf=tf: e.matmul(
                         out=pb[2][0:80, 0:T],
                         lhsT=wt_bf[kk][:, c * 80:(c + 1) * 80],
                         rhs=ext[tf % 2][:, c * EW + kk:c * EW + kk + T],
                         start=(kk == 0 and c == 0),
                         stop=(kk == KC - 1 and c == DC - 1)),
                     inc=([("pe_proj", 1)] if (kk == KC - 1 and c == DC - 1) else None),
                     waits=wl)
        v_proj[tf] = S.val("pe_proj")
        S.op("scalar", lambda e: e.copy(out=p80_sb[:, :], in_=pb[2][0:80, 0:T]),
             inc=[("ac_p80", 1)],
             waits=[("pe_proj", v_proj[tf]), ("pe_r", S.val("pe_r")),
                    ("ac_w", v_wq_all)])
        vp80 = S.val("ac_p80")
        outs = [dl_sb, bi_sb, c_sb]
        for i3 in range(3):
            S.op("tensor",
                 lambda e, i3=i3: e.matmul(out=pb[3][0:NS, 0:T],
                                           lhsT=rmat[:, i3 * NS:(i3 + 1) * NS],
                                           rhs=p80_sb[:, :], start=True, stop=False),
                 waits=[("ac_p80", vp80), ("ac_r", S.val("ac_r"))] +
                       ([("dv_scan", S.val("dv_scan")),
                         ("dv_sc", v_init)] if i3 == 0 else []))
            S.op("tensor",
                 lambda e, i3=i3: e.matmul(out=pb[3][0:NS, 0:T],
                                           lhsT=bdrow[0:1, i3 * NS:(i3 + 1) * NS],
                                           rhs=ones_r[0:1, 0:T],
                                           start=False, stop=True),
                 inc=[("pe_r", 1)])
            S.op("scalar",
                 lambda e, i3=i3: e.copy(out=outs[i3][:, :], in_=pb[3][0:NS, 0:T]),
                 inc=[("ac_r", 1)], waits=[("pe_r", S.val("pe_r"))])
        vac_r = S.val("ac_r")
        if tf == NT - 1:
            for i3 in range(3):
                S.op("vector",
                     lambda e, i3=i3, o=outs[i3]: e.tensor_copy(
                         out=o[:, T - 1:T], in_=bd[:, 3 + i3:4 + i3]),
                     waits=([("ac_r", vac_r)] if i3 == 0 else None))
        S.op("scalar", lambda e: e.activation(out=s1_sb[:, :], in_=dl_sb[:, :],
                                              func=AF.Abs),
             waits=[("ac_r", vac_r), ("dv_scan", S.val("dv_scan"))])
        S.op("scalar", lambda e: e.activation(out=s1_sb[:, :], in_=s1_sb[:, :],
                                              func=AF.Exp, scale=-1.0))
        vsp = S.op("scalar", lambda e: e.activation(out=s1_sb[:, :], in_=s1_sb[:, :],
                                                    func=AF.Ln, bias=1.0),
                   inc=[("ac_r", 1)])["ac_r"]
        S.op("vector", lambda e: e.tensor_scalar(out=s2_sb[:, :], in0=dl_sb[:, :],
                                                 scalar1=0.0, scalar2=None,
                                                 op0=ALU.max),
             waits=[("ac_r", vsp)])
        S.op("vector", lambda e: e.tensor_tensor(out=s2_sb[:, :], in0=s2_sb[:, :],
                                                 in1=s1_sb[:, :], op=ALU.add))
        vda = S.op("vector", lambda e: e.tensor_scalar(out=s2_sb[:, :], in0=s2_sb[:, :],
                                                       scalar1=aset[:, 2:3],
                                                       scalar2=None, op0=ALU.mult),
                   inc=[("dv_scan", 1)])["dv_scan"]
        vab = S.op("scalar", lambda e: e.activation(out=s1_sb[:, :], in_=s2_sb[:, :],
                                                    func=AF.Exp),
                   inc=[("ac_r", 1)], waits=[("dv_scan", vda)])["ac_r"]
        S.op("vector", lambda e: e.tensor_scalar(out=dl_sb[:, :], in0=s2_sb[:, :],
                                                 scalar1=0.5, scalar2=1.0,
                                                 op0=ALU.mult, op1=ALU.add))
        S.op("vector", lambda e: e.tensor_tensor(out=s2_sb[:, :], in0=s2_sb[:, :],
                                                 in1=s2_sb[:, :], op=ALU.mult))
        S.op("vector", lambda e: e.scalar_tensor_tensor(out=dl_sb[:, :],
                                                        in0=s2_sb[:, :],
                                                        scalar=1.0 / 6.0,
                                                        in1=dl_sb[:, :],
                                                        op0=ALU.mult, op1=ALU.add))
        S.op("vector", lambda e: e.tensor_scalar(out=s2_sb[:, :], in0=s1_sb[:, :],
                                                 scalar1=-1.0, scalar2=aset[:, 6:7],
                                                 op0=ALU.add, op1=ALU.mult),
             waits=[("ac_r", vab)])
        S.op("vector", lambda e: e.scalar_tensor_tensor(out=s2_sb[:, :],
                                                        in0=dl_sb[:, :],
                                                        scalar=aset[:, 3:4],
                                                        in1=s2_sb[:, :],
                                                        op0=ALU.mult, op1=ALU.add))
        vfl = S.op("vector", lambda e: e.tensor_tensor(out=s2_sb[:, :],
                                                       in0=s2_sb[:, :],
                                                       in1=bi_sb[:, :], op=ALU.mult),
                   inc=[("dv_fl", 1)])["dv_fl"]
        hcur = hb_sb[tf % 2]
        S.wait("vector", "dv_fl", vfl)
        if tf == 0:
            vfl = S.op("vector", lambda e, hcur=hcur: e.tensor_tensor_scan(
                 out=hcur[:, :], data0=s1_sb[:, :], data1=s2_sb[:, :], initial=0.0,
                 op0=ALU.mult, op1=ALU.add), inc=[("dv_fl", 1)])["dv_fl"]
        else:
            hprev = hb_sb[(tf - 1) % 2]
            vfl = S.op("vector", lambda e, hcur=hcur, hprev=hprev: e.tensor_tensor_scan(
                 out=hcur[:, :], data0=s1_sb[:, :], data1=s2_sb[:, :],
                 initial=hprev[:, T - 1:T], op0=ALU.mult, op1=ALU.add),
                 inc=[("dv_fl", 1)])["dv_fl"]
        S.op("vector", lambda e, tf=tf, hcur=hcur: e.tensor_tensor(
             out=chb[:, tf * T:(tf + 1) * T], in0=c_sb[:, :], in1=hcur[:, :],
             op=ALU.mult),
             inc=[("dv_scan", 1)], waits=[("dv_fl", vfl)])
        v_scan[tf] = S.val("dv_scan")

    v_wo_u = [0] * FC
    v_stws_k = {}

    def sched_woq(k):
        tok = dma(
            "sync", "ldwo",
            lambda e, k=k: e.dma_start(out=wof[k % 2][:, :],
                                       in_=dwoutf[k * 128:(k + 1) * 128, :]),
            waits=[("dv_wo", v_wo_u[k - 2] if k >= 2 else 0)] +
                  ([("dv_ax", 16), ("pe_t", 4 * NHT)] if k == 0 else []))
        vfl = S.op("vector", lambda e, k=k: e.tensor_reduce(out=stats[:, 6:7],
                                                            in_=wof[k % 2][:, :],
                                                            axis=AX.X, op=ALU.max,
                                                            apply_absolute_value=True),
                   inc=[("dv_fl", 1)],
                   waits=[tok] + ([v_stws_k[k - 2]] if k >= 2 else []))["dv_fl"]
        vfl = S.op("vector", lambda e: e.reciprocal(out=stats[:, 7:8],
                                                    in_=stats[:, 6:7]),
                   inc=[("dv_fl", 1)], waits=[("dv_fl", vfl)])["dv_fl"]
        fop("vector", lambda e: e.tensor_scalar(out=stats[:, 7:8], in0=stats[:, 7:8],
                                                scalar1=127.0, scalar2=None,
                                                op0=ALU.mult),
            waits=[("dv_fl", vfl)])
        fop("vector", lambda e: e.tensor_scalar(out=stats[:, 6:7], in0=stats[:, 6:7],
                                                scalar1=1.0 / 127.0, scalar2=None,
                                                op0=ALU.mult))
        vz = round_scaled(("dv_w", "pq"), 1024, wof[k % 2][:, :],
                          wsim_bf[k % 2][:, :], stats[:, 7:8], stats[:, 6:7],
                          first_waits=None, first_inc=[("dv_wo", 1)])
        v_wo_u[k] = S.val("dv_wo")
        v_stws_k[k] = dma(
            "gpsimd", "stws",
            lambda e, k=k: e.dma_start(out=dwsim[k], in_=wsim_bf[k % 2][:, :]),
            waits=[("dv_w", vz)])

    def ext_view(i):
        return ext[i].rearrange("p (c w) -> p c w", w=EW)

    def sched_ph2_front(t):
        v_ldxn[t] = dma(
            "sync", "ldxn",
            lambda e, t=t: e.dma_start(out=xnf[t % 2][:, :], in_=dxn[t]),
            waits=w_stxn_all + [("dv_q", v_q[t - 2] if t >= 2 else 0)])
        S.op("vector",
             lambda e, t=t: e.tensor_copy(
                 out=ext_view(t % 2)[:, :, 1:1 + T],
                 in_=xnf[t % 2][:, :].rearrange("p (c w) -> p c w", w=T)),
             waits=[v_ldxn[t],
                    ("pe_proj", v_proj[t - 2] if t >= 2 else 0)] +
                   ([("pe_sc", S.val("pe_sc")), ("ac_xn", S.val("ac_xn")),
                     ("dv_ln", S.val("dv_ln"))]
                    if t == 0 else []) +
                   ([("ac_w", v_wq_all), ("dv_w", S.val("dv_w")),
                     ("pe_wb", S.val("pe_wb"))] if t < 2 else []))
        if t == 0:
            S.op("vector", lambda e: e.memset(ext_view(0)[:, :, 0:1], 0.0),
                 inc=[("dv_ext", 1)])
        else:
            S.op("vector", lambda e, t=t: e.tensor_copy(
                 out=ext_view(t % 2)[:, :, 0:1],
                 in_=ext_view((t - 1) % 2)[:, :, T:T + 1]))
            S.op("vector", lambda e, t=t: e.tensor_copy(
                 out=ext_view((t - 1) % 2)[:, :, T + 1:T + 3],
                 in_=ext_view(t % 2)[:, :, 1:3]),
                 inc=[("dv_ext", 1)])
        if t == NT - 1:
            S.op("vector", lambda e: e.memset(
                 ext_view((NT - 1) % 2)[:, :, T + 1:T + 3], 0.0),
                 inc=[("dv_ext", 1)])
        v_ext[t] = S.val("dv_ext")

        for pc in range(4):
            w = 1024
            xin = xnf[t % 2][:, pc * w:(pc + 1) * w]
            qout = qb[t % 2][:, pc * w:(pc + 1) * w]
            wl = []
            if t == 0 and pc == 0:
                wl.append(("dv_sc", v_sx))
            if t >= 2 and pc == 0:
                wl.append(("pe_mm1", v_mm1_t[t - 2]))
            round_scaled(("dv_q", "pq"), w, xin, qout, bca[:, 0:1], None,
                         first_waits=wl)
        v_q[t] = S.val("dv_q")

    def sched_ph2_body(t):
        if t >= 1:
            sched_ssm_front(t - 1, v_ext[t])
        if t + 1 < NT:
            sched_ph2_front(t + 1)

        for s in range(FC):
            bank = (0, 1, 6, 7)[s % 4]
            gi = t * FC + s
            for c in range(DC):
                wl = None
                if c == 0:
                    wl = [("dv_q", v_q[t]),
                          ("ac_g", v_gel[gi - 4] if gi >= 4 else 0)]
                    if gi == 0:
                        wl.extend(w_ldwq)
                S.op("tensor",
                     lambda e, t=t, s=s, c=c, bank=bank: e.matmul(
                         out=pb[bank],
                         lhsT=wbuf[:, c * 4096 + s * 128:c * 4096 + (s + 1) * 128],
                         rhs=qb[t % 2][:, c * T:(c + 1) * T],
                         start=(c == 0), stop=(c == DC - 1)),
                     inc=([("pe_mm1", 1)] if c == DC - 1 else None), waits=wl)
            vmm = S.val("pe_mm1")
            hb2 = (gi // 2) % 2
            wl = [("pe_mm1", vmm)]
            if gi == 0:
                wl.append(("dv_sc", v_sx))
            if gi <= 3:
                wl.append(("pe_w", FC * DC))
            if gi >= 4:
                wl.append(v_sth_pair[gi // 2 - 2])
                wl.append(("dv_h", v_dvh_pair[gi // 2 - 2]))
            v_gel[gi] = S.op("scalar",
                             lambda e, s=s, bank=bank, hb2=hb2: e.activation(
                                 out=hidb[hb2][:, (s % 2) * T:(s % 2 + 1) * T],
                                 in_=pb[bank], func=AF.Gelu,
                                 bias=binf_sb[:, s:s + 1], scale=bca[:, 1:2]),
                             inc=[("ac_g", 1)], waits=wl)["ac_g"]
            if s % 2 == 1:
                pc2 = gi // 2
                S.op("vector",
                     lambda e, hb2=hb2, pc2=pc2: e.tensor_reduce(
                         out=hstrip[:, pc2:pc2 + 1], in_=hidb[hb2][:, :],
                         axis=AX.X, op=ALU.max, apply_absolute_value=True),
                     inc=[("dv_h", 1)], waits=[("ac_g", v_gel[gi])])
                v_dvh_pair[gi // 2] = S.val("dv_h")
                v_sth_pair[gi // 2] = dma(
                    "gpsimd", "sth",
                    lambda e, t=t, s=s, hb2=hb2: e.dma_start(out=dhid[t, s // 2],
                                                             in_=hidb[hb2][:, :]),
                    waits=[("ac_g", v_gel[gi])])
        v_mm1_t[t] = S.val("pe_mm1")
        for kq in range(4 * t, min(4 * t + 4, FC)):
            sched_woq(kq)

    sched_ph2_front(0)
    for t in range(NT):
        sched_ph2_body(t)
    sched_ssm_front(NT - 1, v_ext[NT - 1])

    xop("vector", lambda e: e.tensor_reduce(out=amax_acc[:, 1:2],
                                            in_=hstrip[:, :], axis=AX.X,
                                            op=ALU.max),
        marks=["dv_h"], waits=[("dv_h", S.val("dv_h"))])
    v_sh = sched_amax_final(1, dcchi, dccho, 3, 2, "dv_h", S.val("dv_h"), ccd="cc2")
    if debug_taps:
        dma("gpsimd", "sttap", lambda e: e.dma_start(out=dscal, in_=scal[:, :]),
            waits=[("dv_sc", v_sh)])
        dma("gpsimd", "sttap", lambda e: e.dma_start(out=dchb, in_=chb[:, :]),
            waits=[("dv_scan", v_scan[NT - 1])])
        for (sm, v) in dtotal("sttap"):
            S.wait("gpsimd", sm, v)
    w_sth_all = dtotal("sth")
    w_stws_all = dtotal("stws")

    if stage <= 2:
        for (sm, v) in w_sth_all + w_stws_all:
            S.wait("gpsimd", sm, v)
        S.wait("gpsimd", "cc", S.val("cc"))
        return finish(nc, ctx, S, semh)

    for g4 in range(8):
        dma("sync", "ldws",
            lambda e, g4=g4: e.dma_start(
                out=wbuf[:, g4 * 4096:(g4 + 1) * 4096].rearrange(
                    "p (k w) -> p k w", k=4),
                in_=dwsim[g4 * 4:(g4 + 1) * 4].rearrange("k p w -> p k w")),
            waits=([("pe_mm1", NT * FC)] + w_stws_all if g4 == 0 else None))
    w_ldws = dtotal("ldws")

    v_qh_k = {}
    v_mm2_k = {}
    v_sto = {}
    v_y_s = {}
    v_tr_tile = {}
    v_sg = {}
    v_res = {}
    v_sl = {}
    v_ldx3 = {}

    v_ac_g_all = v_gel[NT * FC - 1]
    v_mm2a_k = {}
    v_trA = {}
    v_trB = {}
    for t in range(NT):
        for u in range(4):
            if t == 0:
                if u < 2:
                    wl = [("dv_ax", 16), ("pe_t", 4 * NHT),
                          ("dv_ln", S.val("dv_ln"))]
                else:
                    wl = [("dv_wo", v_wo_u[FC - 4 + u]),
                          ("dv_ln", S.val("dv_ln"))]
            else:
                wl = [("dv_res", v_res[(t - 1, u, 1)])]
            v_ldx3[(t, u)] = dma(
                "sync", "ldx3",
                lambda e, t=t, u=u: e.dma_start(
                    out=xr3[u][:, :],
                    in_=dx[t * T + u * 128:t * T + (u + 1) * 128, :]),
                waits=wl)

        for g in range(FC // 2):
            if 2 * g >= 4:
                wl = [("dv_qh", v_qh_k[(t, 2 * g - 4 + 1)])]
            elif t >= 1:
                wl = [("dv_qh", v_qh_k[(t - 1, FC + 2 * g - 4 + 1)])]
            else:
                wl = w_sth_all + [("dv_h", S.val("dv_h"))]
            tok_h = dma(
                "sync", "ldh",
                lambda e, t=t, g=g: e.dma_start(out=hidb[g % 2][:, :],
                                                in_=dhid[t, g]),
                waits=wl)
            for kk2 in range(2):
                k = 2 * g + kk2
                qdst = qh[k // 16][:, (k % 16) * T:(k % 16 + 1) * T]
                hin = hidb[g % 2][:, kk2 * T:(kk2 + 1) * T]
                wl = [tok_h]
                if t == 0 and k == 0:
                    wl.append(("dv_sc", v_sh))
                    wl.append(("dv_q", v_q[NT - 1]))
                if t >= 1:
                    wl.append(("pe_mm2", v_mm2_k[(t - 1, k)]))
                v_qh_k[(t, k)] = round_scaled(("dv_qh", "pq"), T, hin, qdst,
                                              bca[:, 2:3], None, first_waits=wl)
        for k in range(FC):
            for sdx in range(4):
                wl = None
                if sdx == 0:
                    wl = [("dv_qh", v_qh_k[(t, k)])]
                    if k == 0:
                        wl.extend(w_ldws)
                        if t == 0:
                            wl.append(("dv_scan", v_scan[NT - 1]))
                            wl.append(("ac_g", v_ac_g_all))
                            wl.append(("ac_p80", S.val("ac_p80")))
                            wl.append(("ac_r", S.val("ac_r")))
                            wl.append(("ac_w", v_wq_all))
                            wl.append(("dv_sc", v_sh))
                        else:
                            wl.append(("ac_sl", v_sl[(t - 1, 3)]))
                            wl.append(("dv_sg", v_sg[(t - 1, 3)]))
                            wl.append(("dv_res", v_res[(t - 1, 3, 0)]))
                S.op("tensor",
                     lambda e, t=t, k=k, sdx=sdx: e.matmul(
                         out=pb[sdx],
                         lhsT=wbuf[:, k * 1024 + sdx * 128:k * 1024 + (sdx + 1) * 128],
                         rhs=qh[k // 16][:, (k % 16) * T:(k % 16 + 1) * T],
                         start=(k == 0), stop=(k == FC - 1)),
                     inc=([("pe_mm2", 1)] if sdx == 3 else None), waits=wl)
            v_mm2a_k[(t, k)] = S.val("pe_mm2")
        for k in range(FC):
            for sdx in range(4, 8):
                wl = None
                if sdx == 4 and k == 0:
                    if t == 0:
                        wl = [("ac_g", v_ac_g_all)]
                    else:
                        wl = [("ac_sl", v_sl[(t - 1, 7)]),
                              ("dv_sg", v_sg[(t - 1, 7)]),
                              ("dv_res", v_res[(t - 1, 3, 1)])]
                S.op("tensor",
                     lambda e, t=t, k=k, sdx=sdx: e.matmul(
                         out=pb[sdx],
                         lhsT=wbuf[:, k * 1024 + sdx * 128:k * 1024 + (sdx + 1) * 128],
                         rhs=qh[k // 16][:, (k % 16) * T:(k % 16 + 1) * T],
                         start=(k == 0), stop=(k == FC - 1)),
                     inc=([("pe_mm2", 1)] if sdx == 7 else None), waits=wl)
            v_mm2_k[(t, k)] = S.val("pe_mm2")
        for sdx in range(8):
            half_last = v_mm2a_k[(t, FC - 1)] if sdx < 4 else v_mm2_k[(t, FC - 1)]
            wl = [("pe_mm2", half_last)]
            if sdx == 0:
                if t >= 1:
                    wl.append(("pe_tr", v_trA[t - 1]))
                else:
                    wl.append(("dv_sc", v_sh))
                    wl.append(("dv_scan", v_scan[NT - 1]))
            if sdx == 4 and t >= 1:
                wl.append(("pe_tr", v_trB[t - 1]))
            v_sl[(t, sdx)] = S.op("scalar",
                                  lambda e, sdx=sdx: e.activation(
                                      out=sg_sb[sdx][:, :], in_=pb[sdx],
                                      func=AF.Silu,
                                      bias=boutf_sb[:, sdx:sdx + 1],
                                      scale=bca[:, 3:4]),
                                  inc=[("ac_sl", 1)], waits=wl)["ac_sl"]

        def emit_half(h2):
            for si in range(4):
                sdx = h2 * 4 + si
                S.op("tensor",
                     lambda e, t=t, sdx=sdx: e.matmul(
                         out=pb[sdx],
                         lhsT=wouts_bf[0:NS, sdx * 128:(sdx + 1) * 128],
                         rhs=chb[0:NS, t * T:(t + 1) * T], start=True, stop=True),
                     inc=[("pe_y", 1)],
                     waits=[("ac_sl", v_sl[(t, sdx)])] +
                           ([v_wouts] if (t == 0 and sdx == 0) else []))
                v_y_s[(t, sdx)] = S.val("pe_y")
            for si in range(4):
                sdx = h2 * 4 + si
                v_sg[(t, sdx)] = S.op("vector",
                                      lambda e, sdx=sdx: e.scalar_tensor_tensor(
                                          out=sg_sb[sdx][:, :], in0=pb[sdx],
                                          scalar=bouts_sb[:, sdx:sdx + 1],
                                          in1=sg_sb[sdx][:, :],
                                          op0=ALU.add, op1=ALU.mult),
                                      inc=[("dv_sg", 1)],
                                      waits=[("pe_y", v_y_s[(t, sdx)])])["dv_sg"]
            for u in range(4):
                bank = 4 * h2 + u % 2
                for si in range(4):
                    sdx = h2 * 4 + si
                    wl = None
                    if si == 0:
                        wl = [("dv_sg", v_sg[(t, h2 * 4 + 3)])]
                        if u >= 2:
                            wl.append(("dv_res", v_res[(t, u - 2, h2)]))
                    S.op("tensor",
                         lambda e, sdx=sdx, u=u, bank=bank, si=si: e.transpose(
                             pb[bank][:, si * 128:(si + 1) * 128],
                             sg_sb[sdx][:, u * 128:(u + 1) * 128], ident[:, :]),
                         inc=([("pe_tr", 1)] if si == 3 else None), waits=wl)
                vtr = S.val("pe_tr")
                wl = [("pe_tr", vtr), v_ldx3[(t, u)]]
                if t == 0 and u == 0 and h2 == 0:
                    wl.append(("pe_proj", v_proj[NT - 1]))
                    wl.append(("pe_mm1", NT * FC))
                if t >= 2 and u == 0 and h2 == 0:
                    wl.append(v_sto[t - 2])
                v_res[(t, u, h2)] = S.op(
                    "vector",
                    lambda e, t=t, u=u, h2=h2, bank=bank: e.tensor_tensor(
                        out=outb[t % 2][:, u * 1024 + h2 * 512:
                                        u * 1024 + (h2 + 1) * 512],
                        in0=pb[bank],
                        in1=xr3[u][:, h2 * 512:(h2 + 1) * 512],
                        op=ALU.add),
                    inc=[("dv_res", 1)], waits=wl)["dv_res"]

        emit_half(0)
        v_trA[t] = S.val("pe_tr")
        emit_half(1)
        v_trB[t] = S.val("pe_tr")
        v_sto[t] = dma(
            "gpsimd", "sto",
            lambda e, t=t: e.dma_start(
                out=dout[t * T:(t + 1) * T, :].rearrange("(u p) d -> p u d", p=128),
                in_=outb[t % 2][:, :].rearrange("p (u d) -> p u d", u=4)),
            waits=[("dv_res", v_res[(t, 3, 1)])])

    for (sm, v) in dtotal("sto"):
        S.wait("gpsimd", sm, v)
    return finish(nc, ctx, S, semh)


def finish(nc, ctx, S, semh):
    with nc.Block() as block:
        def mk(ename):
            def emitter(eng):
                for it in S.items:
                    if it[1] != ename:
                        continue
                    if it[0] == "w":
                        eng.wait_ge(semh[it[2]], it[3])
                    elif it[0] == "m":
                        eng.sem_inc(semh[it[2]], 1)
                    else:
                        inst = it[2](eng)
                        for (s, a) in it[3]:
                            if a == "dflt":
                                inst.then_inc(semh[s])
                            else:
                                inst.then_inc(semh[s], a)
            return emitter
        block.sync(mk("sync"))
        block.tensor(mk("tensor"))
        block.vector(mk("vector"))
        block.scalar(mk("scalar"))
        block.gpsimd(mk("gpsimd"))
    ctx.close()
    return nc


def make_consts():
    ident = np.eye(128, dtype=np.float32)
    ident_bf = np.eye(128, dtype=ml_dtypes.bfloat16)
    ones = np.ones((1, 512), dtype=np.float32)
    R = np.zeros((80, 48), dtype=np.float32)
    for n in range(16):
        R[n, n] = 1.0
        R[16 + 2 * n, 16 + n] = 0.5
        R[17 + 2 * n, 16 + n] = 0.5
        R[48 + 2 * n, 32 + n] = 0.5
        R[49 + 2 * n, 32 + n] = 0.5
    return ident, ident_bf, ones, R


def make_in_maps(inputs):
    ident, ident_bf, ones, R = make_consts()
    x = np.ascontiguousarray(np.asarray(inputs["L0_attn_x"], dtype=np.float32))
    common = {
        "conv_w": np.ascontiguousarray(
            np.asarray(inputs["L0_attn_conv_w"], np.float32).reshape(DC, 128, KC)
            .transpose(1, 0, 2).reshape(128, DC * KC)),
        "conv_b": np.ascontiguousarray(
            np.asarray(inputs["L0_attn_conv_b"], np.float32).reshape(DC, 128)),
        "Win": np.ascontiguousarray(
            np.asarray(inputs["L0_attn_Win"], np.float32).reshape(DC, 128, 80)
            .transpose(1, 0, 2).reshape(128, DC * 80)),
        "b_in": np.ascontiguousarray(
            np.asarray(inputs["L0_attn_b_in"], np.float32).reshape(1, 80)),
        "Wout_s": np.ascontiguousarray(np.asarray(inputs["L0_attn_Wout"], np.float32)),
        "b_out_s": np.ascontiguousarray(
            np.asarray(inputs["L0_attn_b_out"], np.float32).reshape(DC, 128)),
        "A_log": np.ascontiguousarray(
            np.asarray(inputs["L0_attn_A_log"], np.float32).reshape(1, NS)),
        "W_in_f": np.ascontiguousarray(np.asarray(inputs["L0_ffn_W_in"], np.float32)),
        "b_in_f": np.ascontiguousarray(
            np.asarray(inputs["L0_ffn_b_in"], np.float32).reshape(FC, 128)),
        "W_out_f": np.ascontiguousarray(np.asarray(inputs["L0_ffn_W_out"], np.float32)),
        "b_out_f": np.ascontiguousarray(
            np.asarray(inputs["L0_ffn_b_out"], np.float32).reshape(DC, 128)),
        "ident": ident, "ident_bf": ident_bf, "ones_r": ones, "Rmat": R,
    }
    return [dict(common, x=np.ascontiguousarray(x[c])) for c in range(B)]


def kernel(**inputs):
    in_maps = make_in_maps(inputs)
    nc = build_program(stage=4, debug_taps=False)
    res = run_bass_kernel_spmd(nc, in_maps, core_ids=list(range(B)))
    return np.stack([res.results[c]["out"] for c in range(B)], axis=0)


# version 14
# speedup vs baseline: 1.0339x; 1.0339x over previous
import numpy as np
import ml_dtypes
import concourse.bass as bass
import concourse.mybir as mybir
from concourse.bass_utils import run_bass_kernel_spmd

F32 = mybir.dt.float32
BF16 = mybir.dt.bfloat16
I32 = mybir.dt.int32
AF = mybir.ActivationFunctionType
ALU = mybir.AluOpType
AX = mybir.AxisListType

B = 8
L = 4096
D = 1024
DFF = 4096
NS = 16
KC = 4
T = 512
NT = L // T
HT = 256
NHT = L // HT
DC = D // 128
FC = DFF // 128
EW = T + 3
EPS_LN = 1e-5
EPS_A = 1e-4
RG = [[0, 1, 2, 3, 4, 5, 6, 7]]

ROUND_MODE = "rint"
SKIP_CC = False
PH1_HALVES = NHT
DO_WQ = 5
DO_AMAXF = True
WQ_NOQUANT = False


class Sched:

    def __init__(self):
        self.items = []
        self.counts = {}

    def val(self, sem):
        return self.counts.get(sem, 0)

    def wait(self, eng, sem, v):
        if v > 0:
            self.items.append(("w", eng, sem, v))

    def mark(self, eng, sem, semh_ref=None):
        self.counts[sem] = self.counts.get(sem, 0) + 1
        v = self.counts[sem]
        self.items.append(("m", eng, sem, v))
        return v

    def op(self, eng, fn, inc=None, waits=None):
        if waits:
            for (s, v) in waits:
                self.wait(eng, s, v)
        incs = []
        if inc:
            for (s, a) in inc:
                if a == "dflt":
                    self.counts[s] = self.counts.get(s, 0) + 1
                else:
                    assert a > 0
                    self.counts[s] = self.counts.get(s, 0) + a
                incs.append((s, a))
        self.items.append(("o", eng, fn, incs))
        out = {}
        for (s, a) in incs:
            out[s] = self.counts[s]
        return out


def build_program(stage=4, debug_taps=False):
    nc = bass.Bass("TRN2", target_bir_lowering=False)

    dx = nc.dram_tensor("x", [L, D], F32, kind="ExternalInput").ap()
    dconvw = nc.dram_tensor("conv_w", [128, DC * KC], F32, kind="ExternalInput").ap()
    dconvb = nc.dram_tensor("conv_b", [DC, 128], F32, kind="ExternalInput").ap()
    dwin = nc.dram_tensor("Win", [128, DC * 80], F32, kind="ExternalInput").ap()
    dbin = nc.dram_tensor("b_in", [1, 80], F32, kind="ExternalInput").ap()
    dwouts = nc.dram_tensor("Wout_s", [NS, D], F32, kind="ExternalInput").ap()
    dbouts = nc.dram_tensor("b_out_s", [DC, 128], F32, kind="ExternalInput").ap()
    dalog = nc.dram_tensor("A_log", [1, NS], F32, kind="ExternalInput").ap()
    dwinf = nc.dram_tensor("W_in_f", [D, DFF], F32, kind="ExternalInput").ap()
    dbinf = nc.dram_tensor("b_in_f", [FC, 128], F32, kind="ExternalInput").ap()
    dwoutf = nc.dram_tensor("W_out_f", [DFF, D], F32, kind="ExternalInput").ap()
    dboutf = nc.dram_tensor("b_out_f", [DC, 128], F32, kind="ExternalInput").ap()
    dident = nc.dram_tensor("ident", [128, 128], F32, kind="ExternalInput").ap()
    dident_bf = nc.dram_tensor("ident_bf", [128, 128], BF16, kind="ExternalInput").ap()
    dones = nc.dram_tensor("ones_r", [1, 512], F32, kind="ExternalInput").ap()
    dR = nc.dram_tensor("Rmat", [80, 48], F32, kind="ExternalInput").ap()

    dout = nc.dram_tensor("out", [L, D], F32, kind="ExternalOutput").ap()
    tap = "ExternalOutput" if debug_taps else "Internal"
    dxn = nc.dram_tensor("xn_spill", [NT, 128, DC * T], F32, kind=tap).ap()
    dhid = nc.dram_tensor("hid_spill", [NT, FC // 2, 128, 1024], F32, kind=tap).ap()
    dwsim = nc.dram_tensor("wsim_stage", [FC, 128, 1024], BF16, kind=tap).ap()
    dwinT = nc.dram_tensor("winq_T", [DC, DFF, 128], BF16).ap()
    dwq = nc.dram_tensor("winq_tap", [1, 128, 32768], BF16,
                         kind=tap if debug_taps else "Internal").ap()
    dccxi = nc.dram_tensor("ccx_in", [1, 128], F32)
    dccxo = nc.dram_tensor("ccx_out", [1, 128], F32)
    dcchi = nc.dram_tensor("cch_in", [1, 128], F32)
    dccho = nc.dram_tensor("cch_out", [1, 128], F32)
    dscal = nc.dram_tensor("scal_tap", [1, 8], F32,
                           kind=tap if debug_taps else "Internal").ap()
    dchb = nc.dram_tensor("chb_tap", [NS, L], mybir.dt.bfloat16,
                          kind=tap if debug_taps else "Internal").ap()

    import contextlib
    ctx = contextlib.ExitStack()
    sb = lambda name, shape, dt: ctx.enter_context(nc.sbuf_tensor(name, shape, dt))

    wbuf = sb("wbuf", [128, 32768], BF16)
    chb = sb("chb", [NS, L], BF16)
    xbuf = [sb(f"xbuf{i}", [128, 2048], F32) for i in range(2)]
    xnf = [sb(f"xnf{i}", [128, DC * T], F32) for i in range(2)]
    ar2 = sb("ar2", [128, 33024], mybir.dt.uint8)
    ar3 = sb("ar3", [128, 22528], mybir.dt.uint8)
    hidb = [sb(f"hidb{i}", [128, 1024], F32) for i in range(2)]
    wsim_bf = [sb(f"wsimbf{i}", [128, 1024], BF16) for i in range(2)]
    u_s = sb("u_s", [128, 1024], F32)
    i_s = sb("i_s", [128, 1024], I32)
    f_s = sb("f_s", [128, 1024], F32) if ROUND_MODE == "floor_cmp" else u_s
    ident = sb("identt", [128, 128], F32)
    ident_bf = sb("ident_bft", [128, 128], BF16)
    ones_r = sb("ones_rt", [1, 512], F32)
    rmat = sb("rmatt", [80, 48], F32)
    wt_bf = [sb(f"wt{k}", [128, DC * 80], BF16) for k in range(KC)]
    wc_sb = sb("wc_sbt", [128, DC * KC], F32)
    binf_sb = sb("binf_sbt", [128, FC], F32)
    boutf_sb = sb("boutf_sbt", [128, DC], F32)
    bouts_sb = sb("bouts_sbt", [128, DC], F32)
    cbf_sb = sb("cbf_sbt", [128, DC], F32)
    wouts_bf = sb("wouts_bft", [NS, D], BF16)
    binT = sb("binT", [80, 1], F32)
    bcomb = sb("bcomb", [80, 1], F32)
    bd = sb("bd", [NS, 6], F32)
    bdrow = sb("bdrow", [1, 3 * NS], F32)
    aset = sb("aset", [NS, 8], F32)
    bca = sb("bca", [128, 4], F32)
    scal = sb("scal", [1, 8], F32)
    amax_acc = sb("amax_acc", [128, 2], F32)
    stats = sb("stats", [128, 12], F32)
    statsW = sb("statsW", [128, 64], F32)
    statsW2 = sb("statsW2", [128, 32], F32)
    hstrip = sb("hstrip", [128, 128], F32)

    ar2b = ar2[:, :].bitcast(BF16)
    ext = [ar2b[:, i * (EW * DC):(i + 1) * (EW * DC)] for i in range(2)]
    qb = [ar2b[:, 2 * EW * DC + i * 4096: 2 * EW * DC + (i + 1) * 4096] for i in range(2)]
    ar2f = ar2[:, :].bitcast(F32)
    outb = [ar2f[:, i * 4096:(i + 1) * 4096] for i in range(2)]
    win_sb = ar2f[:, 0:640]
    sqs = ar2f[:, 640:1664]
    ld8a = ar2f[0:8, 1664:1792]
    ld8b = ar2f[0:8, 1920:2048]
    ld8c = ar2f[0:8, 2048:2176]
    ld32 = ar2f[0:32, 1792:1920]

    ar3f = ar3[:, :].bitcast(F32)
    wslc = [ar3f[:, i * 128:(i + 1) * 128] for i in range(2)]
    wtk = ar3f[:, 256:1280]
    wsimT = ar3f[:, 1280:1792].bitcast(BF16)
    p80_sb = ar3f[0:80, 0:512]
    _o = 2048
    dl_sb = ar3f[0:NS, _o:_o + 512]
    bi_sb = ar3f[0:NS, _o + 512:_o + 1024]
    c_sb = ar3f[0:NS, _o + 1024:_o + 1536]
    s1_sb = ar3f[0:NS, _o + 1536:_o + 2048]
    s2_sb = ar3f[0:NS, _o + 2048:_o + 2560]
    hb_sb = [ar3f[0:NS, _o + 2560 + i * 512:_o + 2560 + (i + 1) * 512] for i in range(2)]
    sg_sb = [ar3f[:, i * 512:(i + 1) * 512] for i in range(8)]
    g_sb = [ar3f[:, 4096 + i * 512:4096 + (i + 1) * 512] for i in range(2)]
    ccst = ar3f[0:1, 1792:1920]
    b80 = ar3f[0:1, 1920:2000]
    alog_row = ar3f[0:1, 2000:2016]

    wof = [xbuf[i][:, 0:1024] for i in range(2)]
    xr3 = [xbuf[0][:, 1024:2048], xbuf[1][:, 1024:2048],
           xbuf[0][:, 0:1024], xbuf[1][:, 0:1024]]
    qh = [xnf[i][:, :].bitcast(BF16) for i in range(2)]
    wtk_b = [None, None]
    wsimT_b = [None, None]

    wtk_b[0] = wtk
    wtk_b[1] = ar2f[:, 2176:3200]
    wsimT_b[0] = wsimT
    wsimT_b[1] = ar2f[:, 3200:3712].bitcast(BF16)
    wslc8 = [hidb[0][:, :], hidb[1][:, :]]
    pbk = [ctx.enter_context(nc.psum_tensor(f"pb{i}", [128, 512], F32)) for i in range(8)]
    pb = [p[:, :] for p in pbk]
    pb4bf = pbk[4][:, :].bitcast(BF16)
    pb5bf = pbk[5][:, :].bitcast(BF16)

    sem_names = [
        "cc", "cc1", "cc2", "dv_fl", "ac_fl", "dv_xn",
        "pe_t", "pe_w", "pe_wb", "pe_proj", "pe_r", "pe_mm1", "pe_mm2",
        "pe_y", "pe_tr", "pe_sc",
        "dv_ln", "dv_ax", "dv_q", "dv_w", "dv_wo", "dv_scan", "dv_sg",
        "dv_res", "dv_qh", "dv_h", "dv_sc", "dv_ext", "pq",
        "ac_xn", "ac_ev", "ac_g", "ac_sl", "ac_r", "ac_w", "ac_p80",
    ]
    streams = {"ldx": 2, "ldxn": 2, "ldh": 2, "ldw": 2, "ldwo": 2, "ldws": 1,
               "ldsm": 1, "ldx3": 4, "stxn": 2, "sth": 2, "stws": 2, "sto": 2,
               "ldwb": 1, "sttap": 1, "stwq": 2, "ldwq": 1}
    for nm, nb in streams.items():
        for b in range(nb):
            sem_names.append(f"{nm}{b}")
    semh = {n: ctx.enter_context(nc.semaphore(n)) for n in sem_names}
    S = Sched()
    dcount = {}

    def xop(eng, fn, marks, waits=None):

        sem = "dv_fl" if eng == "vector" else "ac_fl"
        r = S.op(eng, fn, inc=[(sem, 1)], waits=waits)
        S.wait(eng, sem, r[sem])
        out = {}
        for m in marks:
            out[m] = S.mark(eng, m)
        return out

    def warm_mm(bank):
        S.op("tensor", lambda e, bank=bank: e.matmul(out=bank[0:64, 448:512],
                                                     lhsT=ident[0:1, 0:64],
                                                     rhs=ident[0:1, 64:128],
                                                     start=True, stop=True))

    def fop(eng, fn, inc=None, waits=None):
        if inc:
            r = S.op(eng, fn, inc=inc, waits=waits)
            s0 = inc[0][0]
            S.wait(eng, s0, r[s0])
            return r
        sem = "dv_fl" if eng == "vector" else "ac_fl"
        r = S.op(eng, fn, inc=[(sem, 1)], waits=waits)
        S.wait(eng, sem, r[sem])
        return r

    def dma(eng, stream, fn, waits=None):
        nb = streams[stream]
        i = dcount.get(stream, 0)
        dcount[stream] = i + 1
        sem = f"{stream}{i % nb}"
        S.op(eng, fn, inc=[(sem, 16)], waits=waits)
        return (sem, 16 * (i // nb + 1))

    def dtotal(stream):
        nb = streams[stream]
        c = dcount.get(stream, 0)
        return [(f"{stream}{b}", 16 * ((c + nb - 1 - b) // nb))
                for b in range(nb) if (c + nb - 1 - b) // nb > 0]

    def _ld(dst, src):
        return lambda e, d=dst, s=src: e.dma_start(out=d, in_=s)

    for dst, src in [(ident[:, :], dident), (ident_bf[:, :], dident_bf),
                     (ones_r[:, :], dones), (rmat[:, :], dR),
                     (b80, dbin), (alog_row, dalog),
                     (win_sb, dwin), (wc_sb[:, :], dconvw),
                     (ld8a, dconvb), (ld8b, dbouts), (ld8c, dboutf),
                     (ld32, dbinf)]:
        dma("sync", "ldsm", _ld(dst, src))
    w_ldsm = dtotal("ldsm")
    v_wouts = dma("gpsimd", "ldwb",
                  lambda e: e.dma_start(out=wouts_bf[:, :], in_=dwouts))

    bias_jobs = [(ld8a, cbf_sb, 8), (ld8b, bouts_sb, 8), (ld8c, boutf_sb, 8),
                 (ld32, binf_sb, 32)]
    for (stg, dst, rows) in bias_jobs:
        S.op("tensor",
             lambda e, st=stg, r=rows: e.transpose(pb[5][0:128, 0:r], st[0:r, :],
                                                   ident[0:r, 0:r]),
             inc=[("pe_sc", 1)],
             waits=w_ldsm + [("ac_w", S.val("ac_w"))])
        xop("scalar",
            lambda e, d=dst, r=rows: e.copy(out=d[:, 0:r], in_=pb[5][0:128, 0:r]),
            marks=["ac_w"], waits=[("pe_sc", S.val("pe_sc"))])
    S.op("tensor", lambda e: e.transpose(pb[5][0:80, 0:1], b80,
                                         ident[0:1, 0:1]),
         inc=[("pe_sc", 1)], waits=w_ldsm + [("ac_w", S.val("ac_w"))])
    xop("scalar", lambda e: e.copy(out=binT[:, :], in_=pb[5][0:80, 0:1]),
        marks=["ac_w"], waits=[("pe_sc", S.val("pe_sc"))])
    S.op("tensor", lambda e: e.transpose(pb[5][0:NS, 0:1], alog_row,
                                         ident[0:1, 0:1]),
         inc=[("pe_sc", 1)], waits=[("ac_w", S.val("ac_w"))])
    xop("scalar", lambda e: e.copy(out=aset[:, 0:1], in_=pb[5][0:NS, 0:1]),
        marks=["ac_w"], waits=[("pe_sc", S.val("pe_sc"))])
    v_ac_bias = S.val("ac_w")

    for c in range(DC):
        for kk in range(KC):
            S.op("vector",
                 lambda e, c=c, kk=kk: e.tensor_scalar(
                     out=wt_bf[kk][:, c * 80:(c + 1) * 80],
                     in0=win_sb[:, c * 80:(c + 1) * 80],
                     scalar1=wc_sb[:, c * KC + kk:c * KC + kk + 1], scalar2=None,
                     op0=ALU.mult),
                 waits=(w_ldsm if (c == 0 and kk == 0) else None))

    xop("scalar", lambda e: e.activation(out=aset[:, 1:2], in_=aset[:, 0:1],
                                         func=AF.Exp),
        marks=["ac_w"], waits=[("ac_w", v_ac_bias)])
    vA = S.val("ac_w")
    fop("vector", lambda e: e.tensor_scalar(out=aset[:, 2:3], in0=aset[:, 1:2],
                                            scalar1=-1.0, scalar2=None, op0=ALU.mult),
        waits=[("ac_w", vA)])
    fop("vector", lambda e: e.tensor_scalar(out=aset[:, 3:4], in0=aset[:, 1:2],
                                            scalar1=EPS_A, scalar2=None, op0=ALU.is_lt))
    fop("vector", lambda e: e.tensor_scalar(out=aset[:, 4:5], in0=aset[:, 3:4],
                                            scalar1=-1.0, scalar2=1.0,
                                            op0=ALU.mult, op1=ALU.add))
    vfl = S.op("vector", lambda e: e.scalar_tensor_tensor(out=aset[:, 5:6],
                                                          in0=aset[:, 2:3],
                                                          scalar=aset[:, 4:5],
                                                          in1=aset[:, 3:4],
                                                          op0=ALU.mult, op1=ALU.add),
               inc=[("dv_fl", 1)])["dv_fl"]
    vfl = S.op("vector", lambda e: e.reciprocal(out=aset[:, 6:7], in_=aset[:, 5:6]),
               inc=[("dv_fl", 1)], waits=[("dv_fl", vfl)])["dv_fl"]
    fop("vector", lambda e: e.tensor_scalar(out=aset[:, 6:7], in0=aset[:, 6:7],
                                            scalar1=aset[:, 4:5], scalar2=None,
                                            op0=ALU.mult),
        waits=[("dv_fl", vfl)])

    for c in range(DC):
        S.op("tensor",
             lambda e, c=c: e.matmul(out=pb[5][0:80, 0:1],
                                     lhsT=win_sb[:, c * 80:(c + 1) * 80],
                                     rhs=cbf_sb[:, c:c + 1],
                                     start=(c == 0), stop=(c == DC - 1)),
             inc=([("pe_sc", 1)] if c == DC - 1 else None),
             waits=(w_ldsm + [("ac_w", v_ac_bias)] if c == 0 else None))
    fop("vector", lambda e: e.tensor_tensor(out=bcomb[:, :], in0=pb[5][0:80, 0:1],
                                            in1=binT[:, :], op=ALU.add),
        inc=[("dv_sc", 1)], waits=[("pe_sc", S.val("pe_sc")), ("ac_w", S.val("ac_w"))])
    v_bcomb = S.val("dv_sc")
    for (j3, rhs_t) in ((0, bcomb), (3, binT)):
        for i3 in range(3):
            S.op("tensor",
                 lambda e, i3=i3, rhs_t=rhs_t: e.matmul(
                     out=pb[5][0:NS, 0:1], lhsT=rmat[:, i3 * NS:(i3 + 1) * NS],
                     rhs=rhs_t[:, :], start=True, stop=True),
                 inc=[("pe_sc", 1)],
                 waits=[("dv_sc", v_bcomb), ("ac_w", S.val("ac_w"))])
            xop("scalar",
                lambda e, j3=j3, i3=i3: e.copy(out=bd[:, j3 + i3:j3 + i3 + 1],
                                               in_=pb[5][0:NS, 0:1]),
                marks=["ac_w"], waits=[("pe_sc", S.val("pe_sc"))])

    for i3 in range(3):
        S.op("tensor",
             lambda e, i3=i3: e.transpose(pb[5][0:1, 0:NS], bd[:, i3:i3 + 1],
                                          ident[0:NS, 0:NS]),
             inc=[("pe_sc", 1)], waits=[("ac_w", S.val("ac_w"))])
        xop("scalar",
            lambda e, i3=i3: e.copy(out=bdrow[0:1, i3 * NS:(i3 + 1) * NS],
                                    in_=pb[5][0:1, 0:NS]),
            marks=["ac_w"], waits=[("pe_sc", S.val("pe_sc"))])
    fop("vector", lambda e: e.memset(stats[:, 8:9], EPS_LN))
    S.op("vector", lambda e: e.memset(ccst[:, :], 0.0))
    S.op("vector", lambda e: e.memset(amax_acc[:, :], 0.0), inc=[("dv_sc", 1)])
    v_init = S.val("dv_sc")

    def round_scaled(pref, width, in_ap, out_ap, scale_ap, post_scale_ap,
                     first_waits, first_inc=None):
        vsem, psem = pref
        w = width
        fi = list(first_inc) if first_inc else [(vsem, 1)]
        fsem = fi[0][0]
        if ROUND_MODE == "floor_cmp":
            r = S.op("vector",
                     lambda e: e.tensor_scalar(out=u_s[:, 0:w], in0=in_ap,
                                               scalar1=scale_ap, scalar2=0.5,
                                               op0=ALU.mult, op1=ALU.add),
                     inc=fi, waits=first_waits)
            va = r[fsem]
            S.op("gpsimd", lambda e: e.tensor_copy(out=i_s[:, 0:w], in_=u_s[:, 0:w]),
                 waits=[(fsem, va)])
            vb = S.op("gpsimd", lambda e: e.tensor_copy(out=f_s[:, 0:w],
                                                        in_=i_s[:, 0:w]),
                      inc=[(psem, 1)])[psem]
            S.op("vector", lambda e: e.tensor_tensor(out=u_s[:, 0:w], in0=f_s[:, 0:w],
                                                     in1=u_s[:, 0:w], op=ALU.is_gt),
                 waits=[(psem, vb)])
            S.op("vector", lambda e: e.tensor_tensor(out=f_s[:, 0:w], in0=f_s[:, 0:w],
                                                     in1=u_s[:, 0:w], op=ALU.subtract))
            ps = post_scale_ap if post_scale_ap is not None else 1.0
            vz = S.op("vector",
                      lambda e, ps=ps: e.tensor_scalar(out=out_ap, in0=f_s[:, 0:w],
                                                       scalar1=ps, scalar2=None,
                                                       op0=ALU.mult),
                      inc=[(vsem, 1)])[vsem]
        else:
            S.op("vector",
                 lambda e: e.tensor_scalar(out=u_s[:, 0:w], in0=in_ap,
                                           scalar1=scale_ap, scalar2=None,
                                           op0=ALU.mult),
                 inc=fi if first_inc else None, waits=first_waits)
            S.op("vector", lambda e: e.tensor_copy(out=i_s[:, 0:w], in_=u_s[:, 0:w]))
            ps = post_scale_ap if post_scale_ap is not None else 1.0
            vz = S.op("vector",
                      lambda e, ps=ps: e.tensor_scalar(out=out_ap, in0=i_s[:, 0:w],
                                                       scalar1=ps, scalar2=None,
                                                       op0=ALU.mult),
                      inc=[(vsem, 1)])[vsem]
        return vz

    v_ldw_pc = {}
    v_pe_w_pc = {}
    v_wqst = {}
    v_ac_wtk = {}
    v_dvw_u = {}
    v_wq_done = {}

    def sched_winq_loads(k):
        v_ldw_pc[k] = dma(
            "sync", "ldw",
            lambda e, k=k: e.dma_start(
                out=wslc8[k % 2][:, :].rearrange("p (c w) -> p c w", w=128),
                in_=dwinf[:, k * 128:(k + 1) * 128].rearrange(
                    "(c p) w -> p c w", p=128)),
            waits=[("pe_w", max(0, (k - 1) * DC))])

    def sched_winq_pair(k):
        if DO_WQ < 2:
            return
        banks = {k: (pb[2], pb[3], pb4bf), k + 1: (pb[6], pb[7], pb5bf)}
        wtks = {k: wtk_b[0], k + 1: wtk_b[1]}
        wsims = {k: wsimT_b[0], k + 1: wsimT_b[1]}
        for kk in (k, k + 1):
            bA, bB, _ = banks[kk]
            for c in range(DC):
                g = kk * DC + c
                wl = [v_ldw_pc[kk]]
                if c == 0:
                    wl.append(("ac_w", S.val("ac_w")))
                S.op("tensor",
                     lambda e, kk=kk, c=c, bA=bA, bB=bB: e.transpose(
                         (bA if c < 4 else bB)[:, (c % 4) * 128:(c % 4 + 1) * 128],
                         wslc8[kk % 2][:, c * 128:(c + 1) * 128], ident[:, :]),
                     inc=[("pe_w", 1)], waits=wl)
                v_pe_w_pc[g] = S.val("pe_w")
                if c == 1:
                    warm_mm(bA)
        for kk in (k, k + 1):
            bA, bB, _ = banks[kk]
            xop("scalar", lambda e, bA=bA, kk=kk: e.copy(out=wtks[kk][:, 0:512], in_=bA),
                marks=["ac_w"],
                waits=[("pe_w", v_pe_w_pc[kk * DC + 3]), ("dv_w", S.val("dv_w"))])
            xop("scalar", lambda e, bB=bB, kk=kk: e.copy(out=wtks[kk][:, 512:1024], in_=bB),
                marks=["ac_w"], waits=[("pe_w", v_pe_w_pc[kk * DC + 7])])
        vac = S.val("ac_w")
        if DO_WQ < 3:
            return
        S.op("vector", lambda e: e.tensor_reduce(out=stats[:, 6:7], in_=wtks[k][:, :],
                                                 axis=AX.X, op=ALU.max,
                                                 apply_absolute_value=True),
             waits=[("ac_w", vac)])
        vfl = xop("vector", lambda e: e.tensor_reduce(out=stats[:, 7:8],
                                                      in_=wtks[k + 1][:, :],
                                                      axis=AX.X, op=ALU.max,
                                                      apply_absolute_value=True),
                  marks=["dv_fl"])["dv_fl"]
        vfl = xop("vector", lambda e: e.reciprocal(out=stats[:, 9:11],
                                                   in_=stats[:, 6:8]),
                  marks=["dv_fl"], waits=[("dv_fl", vfl)])["dv_fl"]
        fop("vector", lambda e: e.tensor_scalar(out=stats[:, 9:11], in0=stats[:, 9:11],
                                                scalar1=127.0, scalar2=None,
                                                op0=ALU.mult),
            waits=[("dv_fl", vfl)])
        fop("vector", lambda e: e.tensor_scalar(out=stats[:, 6:8], in0=stats[:, 6:8],
                                                scalar1=1.0 / 127.0, scalar2=None,
                                                op0=ALU.mult),
            inc=[("dv_w", 1)])
        for kk in (k, k + 1):
            j = kk - k
            v_dvw_u[kk] = round_scaled(
                ("dv_w", "pq"), 1024, wtks[kk][:, :], wsims[kk][:, :],
                stats[:, 9 + j:10 + j], stats[:, 6 + j:7 + j],
                first_waits=([v_wqst[kk - 2]] if kk - 2 in v_wqst else []))
        if DO_WQ < 4:
            return
        for kk in (k, k + 1):
            v_wqst[kk] = dma(
                "gpsimd", "stwq",
                lambda e, kk=kk: e.dma_start(
                    out=dwinT[:, kk * 128:(kk + 1) * 128, :].rearrange(
                        "c p w -> p c w"),
                    in_=wsims[kk][:, :].rearrange("p (c w) -> p c w", w=128)),
                waits=[("dv_w", v_dvw_u[kk])])
        v_wq_done[k] = S.val("ac_w")

    v_ldx = [0] * (2 * NHT)
    v_spill = [0] * NT
    v_ax = [0] * NHT
    ev_pair_vals = []
    v_sumq = {}

    def sched_ph1_passA(h):
        hb = h % 2
        wl = [("dv_ln", v_sumq[(h - 2, 1)] if h >= 2 else 0)]
        v_ldx[h] = dma(
            "sync", "ldx",
            lambda e, h=h, hb=hb: e.dma_start(
                out=xbuf[hb][:, :].rearrange("p (u d) -> p u d", u=2),
                in_=dx[h * HT:(h + 1) * HT, :].rearrange("(u p) d -> p u d", p=128)),
            waits=wl)
        for u in range(2):
            j = 2 * h + u
            xin = xbuf[hb][:, u * 1024:(u + 1) * 1024]
            S.op("vector",
                 lambda e, xin=xin, j=j: e.tensor_reduce(
                     out=statsW[:, j:j + 1], in_=xin, axis=AX.X, op=ALU.add),
                 waits=[v_ldx[h]])
            wl_sq = [v_ldx[h]]
            prev = (h, 0) if u == 1 else (h - 1, 1)
            if prev in v_sumq:
                wl_sq.append(("dv_ln", v_sumq[prev]))
            vsq = S.op("scalar",
                       lambda e, xin=xin: e.activation(out=sqs[:, :], in_=xin,
                                                       func=AF.Square),
                       inc=[("ac_xn", 1)], waits=wl_sq)["ac_xn"]
            v_sumq[(h, u)] = S.op(
                "vector",
                lambda e, j=j: e.tensor_reduce(out=statsW[:, 32 + j:33 + j],
                                               in_=sqs[:, :], axis=AX.X, op=ALU.add),
                inc=[("dv_ln", 1)], waits=[("ac_xn", vsq)])["dv_ln"]

    def sched_ph1_wide():
        S.op("vector", lambda e: e.tensor_scalar(out=statsW[:, 0:32], in0=statsW[:, 0:32],
                                                 scalar1=1.0 / D, scalar2=None,
                                                 op0=ALU.mult))
        S.op("vector", lambda e: e.tensor_scalar(out=statsW[:, 32:64], in0=statsW[:, 32:64],
                                                 scalar1=1.0 / D, scalar2=None,
                                                 op0=ALU.mult))
        S.op("vector", lambda e: e.tensor_tensor(out=statsW2[:, :], in0=statsW[:, 0:32],
                                                 in1=statsW[:, 0:32], op=ALU.mult))
        vv = xop("vector", lambda e: e.tensor_tensor(out=statsW[:, 32:64],
                                                     in0=statsW[:, 32:64],
                                                     in1=statsW2[:, :], op=ALU.subtract),
                 marks=["dv_ln"])["dv_ln"]
        fop("scalar", lambda e: e.activation(out=statsW[:, 32:64], in_=statsW[:, 32:64],
                                             func=AF.Ln, bias=stats[:, 8:9]),
            waits=[("dv_ln", vv)])
        vr = xop("scalar", lambda e: e.activation(out=statsW[:, 32:64],
                                                  in_=statsW[:, 32:64],
                                                  func=AF.Exp, scale=-0.5),
                 marks=["ac_xn"])["ac_xn"]
        return S.op("vector",
                    lambda e: e.scalar_tensor_tensor(out=statsW[:, 0:32],
                                                     in0=statsW[:, 0:32], scalar=-1.0,
                                                     in1=statsW[:, 32:64],
                                                     op0=ALU.mult, op1=ALU.mult),
                    inc=[("dv_ln", 1)], waits=[("ac_xn", vr)])["dv_ln"]

    def sched_ph1_passB(h, v_wide):
        t = h // 2
        hb = h % 2
        wl = [("dv_ax", v_ax[h - 2] if h >= 2 else 0),
              ("pe_t", max(0, (h - 1) * 4))]
        if h < 2:
            wl.append(("dv_ln", v_sumq[(NHT - 2 + h, 1)]))
        v_ldx[NHT + h] = dma(
            "sync", "ldx",
            lambda e, h=h, hb=hb: e.dma_start(
                out=xbuf[hb][:, :].rearrange("p (u d) -> p u d", u=2),
                in_=dx[h * HT:(h + 1) * HT, :].rearrange("(u p) d -> p u d", p=128)),
            waits=wl)
        for u in range(2):
            j = 2 * h + u
            xin = xbuf[hb][:, u * 1024:(u + 1) * 1024]
            S.op("vector",
                 lambda e, xin=xin, j=j: e.tensor_scalar(
                     out=xin, in0=xin, scalar1=statsW[:, 32 + j:33 + j],
                     scalar2=statsW[:, j:j + 1], op0=ALU.mult, op1=ALU.add),
                 waits=[v_ldx[NHT + h]] + ([("dv_ln", v_wide)] if h == 0 and u == 0 else []))
        S.op("vector", lambda e, hb=hb: e.tensor_reduce(out=stats[:, 6:7],
                                                        in_=xbuf[hb][:, :],
                                                        axis=AX.X, op=ALU.max,
                                                        apply_absolute_value=True))
        v_ax[h] = xop("vector", lambda e: e.tensor_tensor(out=amax_acc[:, 0:1],
                                                          in0=amax_acc[:, 0:1],
                                                          in1=stats[:, 6:7], op=ALU.max),
                      marks=["dv_ax"])["dv_ax"]
        v_xn = S.mark("vector", "dv_xn")
        for pr in range(4):
            gpair = h * 4 + pr
            bank = pr % 2
            for ci in range(2):
                c = pr * 2 + ci
                for u in range(2):
                    wl = []
                    if ci == 0 and u == 0:
                        wl = [("dv_xn", v_xn)]
                        if gpair >= 2:
                            wl.append(("ac_ev", ev_pair_vals[gpair - 2]))
                    S.op("tensor",
                         lambda e, hb=hb, c=c, u=u, bank=bank: e.transpose(
                             pb[bank][:, (c % 2) * 256 + u * 128:
                                      (c % 2) * 256 + (u + 1) * 128],
                             xbuf[hb][:, u * 1024 + c * 128:u * 1024 + (c + 1) * 128],
                             ident[:, :]),
                         inc=([("pe_t", 1)] if (ci, u) == (1, 1) else None),
                         waits=wl)
                    if (ci, u) == (0, 0):
                        warm_mm(pb[bank])
            vpe = S.val("pe_t")
            c0 = pr * 2
            wl = [("pe_t", vpe)]
            if h % 2 == 0 and pr == 0 and t >= 2:
                wl.append(v_spill[t - 2])
            S.op("scalar",
                 lambda e, t=t, h=h, c0=c0, bank=bank: e.copy(
                     out=xnf[t % 2][:, :].rearrange("p (c w) -> p c w", w=T)
                         [:, c0:c0 + 2, (h % 2) * 256:(h % 2) * 256 + 256],
                     in_=pb[bank].rearrange("p (c w) -> p c w", w=256)),
                 inc=[("ac_ev", 1)], waits=wl)
            ev_pair_vals.append(S.val("ac_ev"))
        if h % 2 == 1:
            v_spill[t] = dma(
                "gpsimd", "stxn",
                lambda e, t=t: e.dma_start(out=dxn[t], in_=xnf[t % 2][:, :]),
                waits=[("ac_ev", ev_pair_vals[-1])])

    wqk = 0
    for h in range(max(PH1_HALVES, NHT if DO_WQ else 0)):
        if h < PH1_HALVES:
            sched_ph1_passA(h)
        if wqk < FC and DO_WQ:
            sched_winq_loads(wqk)
            sched_winq_loads(wqk + 1)
            sched_winq_pair(wqk)
            wqk += 2
    v_wide = sched_ph1_wide() if PH1_HALVES else 0
    for h in range(PH1_HALVES):
        sched_ph1_passB(h, v_wide)
    v_wq_all = S.val("ac_w")
    w_stxn_all = dtotal("stxn")

    def sched_amax_final(col, cc_in_d, cc_out_d, sidx, bcol, in_sem, in_val,
                         ccd="cc1"):
        S.op("tensor",
             lambda e, col=col: e.transpose(pb[5][0:1, 0:128], amax_acc[:, col:col + 1],
                                            ident[:, :]),
             inc=[("pe_sc", 1)], waits=[(in_sem, in_val), ("dv_sc", S.val("dv_sc"))])
        xop("vector",
            lambda e: e.tensor_reduce(out=ccst[0:1, 0:1], in_=pb[5][0:1, 0:128],
                                      axis=AX.X, op=ALU.max),
            marks=["dv_sc"], waits=[("pe_sc", S.val("pe_sc"))])
        S.op("gpsimd", lambda e, d=cc_in_d: e.dma_start(out=d[:, :], in_=ccst[:, :]),
             inc=[("cc", 16)], waits=[("dv_sc", S.val("dv_sc"))])
        if SKIP_CC:
            S.op("gpsimd", lambda e, a=cc_in_d, b=cc_out_d: e.dma_start(
                 out=b[:, :], in_=a[:, :]),
                 inc=[(ccd, "dflt")], waits=[("cc", S.val("cc"))])
        else:
            S.op("gpsimd",
                 lambda e, a=cc_in_d, b=cc_out_d: e.collective_compute(
                     "AllReduce", ALU.max, replica_groups=RG,
                     ins=[a.ap().opt()], outs=[b.ap().opt()]),
                 inc=[(ccd, "dflt")], waits=[("cc", S.val("cc"))])
        S.op("gpsimd", lambda e, d=cc_out_d: e.dma_start(out=ccst[:, :], in_=d[:, :]),
             inc=[("cc", 16)], waits=[(ccd, 1)])
        vcc = S.val("cc")
        vfl = S.op("vector", lambda e, sidx=sidx: e.reciprocal(
             out=scal[:, sidx:sidx + 1], in_=ccst[0:1, 0:1]),
             inc=[("dv_fl", 1)], waits=[("cc", vcc)])["dv_fl"]
        fop("vector", lambda e, sidx=sidx: e.tensor_scalar(
            out=scal[:, sidx:sidx + 1], in0=scal[:, sidx:sidx + 1],
            scalar1=32767.0, scalar2=None, op0=ALU.mult),
            waits=[("dv_fl", vfl)])
        vds = xop("vector", lambda e, sidx=sidx: e.tensor_scalar(
            out=scal[:, sidx + 1:sidx + 2], in0=ccst[0:1, 0:1],
            scalar1=1.0 / 32767.0, scalar2=None, op0=ALU.mult),
            marks=["dv_sc"])["dv_sc"]
        S.op("tensor", lambda e, sidx=sidx: e.matmul(out=pb[5][0:128, 0:2],
                                                     lhsT=ones_r[0:1, 0:128],
                                                     rhs=scal[:, sidx:sidx + 2],
                                                     start=True, stop=True),
             inc=[("pe_sc", 1)], waits=[("dv_sc", vds)])
        vb = xop("vector", lambda e, bcol=bcol: e.tensor_copy(
            out=bca[:, bcol:bcol + 2], in_=pb[5][0:128, 0:2]),
            marks=["dv_sc"], waits=[("pe_sc", S.val("pe_sc"))])["dv_sc"]
        return vb

    w_stwq = dtotal("stwq")
    if DO_WQ:
        for c in range(DC):
            dma("sync", "ldwq",
                lambda e, c=c: e.dma_start(
                    out=wbuf[:, c * 4096:(c + 1) * 4096],
                    in_=dwinT[c], transpose=True),
                waits=(w_stwq if c == 0 else None))
    w_ldwq = dtotal("ldwq")
    v_sx = 0
    if DO_AMAXF:
        v_sx = sched_amax_final(0, dccxi, dccxo, 0, 0, "dv_ax", S.val("dv_ax"))
    if debug_taps:
        dma("gpsimd", "sttap", lambda e: e.dma_start(out=dscal, in_=scal[:, :]),
            waits=[("dv_sc", v_sx)])
        dma("gpsimd", "sttap", lambda e: e.dma_start(out=dwq[0], in_=wbuf[:, :]),
            waits=[("ac_w", v_wq_all)])
        for (sm, v) in dtotal("sttap"):
            S.wait("gpsimd", sm, v)

    if stage <= 1:
        for (sm, v) in w_stxn_all:
            S.wait("gpsimd", sm, v)
        S.wait("gpsimd", "cc", S.val("cc"))
        return finish(nc, ctx, S, semh)

    v_ldxn = [0] * NT
    v_q = [0] * NT
    v_ext = [0] * NT
    v_proj = [0] * NT
    v_mm1_t = [0] * NT
    v_gel = {}
    v_scan = [0] * NT
    v_dvh_pair = {}
    v_sth_pair = {}

    def sched_ssm_front(tf, ext_ready_val):
        first = [True]
        for kk in range(KC):
            for c in range(DC):
                wl = None
                if first[0]:
                    wl = [("dv_ext", ext_ready_val), ("ac_p80", S.val("ac_p80")),
                          ("dv_sc", v_init)]
                    first[0] = False
                S.op("tensor",
                     lambda e, kk=kk, c=c, tf=tf: e.matmul(
                         out=pb[2][0:80, 0:T],
                         lhsT=wt_bf[kk][:, c * 80:(c + 1) * 80],
                         rhs=ext[tf % 2][:, c * EW + kk:c * EW + kk + T],
                         start=(kk == 0 and c == 0),
                         stop=(kk == KC - 1 and c == DC - 1)),
                     inc=([("pe_proj", 1)] if (kk == KC - 1 and c == DC - 1) else None),
                     waits=wl)
        v_proj[tf] = S.val("pe_proj")
        S.op("scalar", lambda e: e.copy(out=p80_sb[:, :], in_=pb[2][0:80, 0:T]),
             inc=[("ac_p80", 1)],
             waits=[("pe_proj", v_proj[tf]), ("pe_r", S.val("pe_r")),
                    ("ac_w", v_wq_all)])
        vp80 = S.val("ac_p80")
        outs = [dl_sb, bi_sb, c_sb]
        for i3 in range(3):
            S.op("tensor",
                 lambda e, i3=i3: e.matmul(out=pb[3][0:NS, 0:T],
                                           lhsT=rmat[:, i3 * NS:(i3 + 1) * NS],
                                           rhs=p80_sb[:, :], start=True, stop=False),
                 waits=[("ac_p80", vp80), ("ac_r", S.val("ac_r"))] +
                       ([("dv_scan", S.val("dv_scan")),
                         ("dv_sc", v_init)] if i3 == 0 else []))
            S.op("tensor",
                 lambda e, i3=i3: e.matmul(out=pb[3][0:NS, 0:T],
                                           lhsT=bdrow[0:1, i3 * NS:(i3 + 1) * NS],
                                           rhs=ones_r[0:1, 0:T],
                                           start=False, stop=True),
                 inc=[("pe_r", 1)])
            S.op("scalar",
                 lambda e, i3=i3: e.copy(out=outs[i3][:, :], in_=pb[3][0:NS, 0:T]),
                 inc=[("ac_r", 1)], waits=[("pe_r", S.val("pe_r"))])
        vac_r = S.val("ac_r")
        if tf == NT - 1:
            for i3 in range(3):
                S.op("vector",
                     lambda e, i3=i3, o=outs[i3]: e.tensor_copy(
                         out=o[:, T - 1:T], in_=bd[:, 3 + i3:4 + i3]),
                     waits=([("ac_r", vac_r)] if i3 == 0 else None))
        S.op("scalar", lambda e: e.activation(out=s1_sb[:, :], in_=dl_sb[:, :],
                                              func=AF.Abs),
             waits=[("ac_r", vac_r), ("dv_scan", S.val("dv_scan"))])
        S.op("scalar", lambda e: e.activation(out=s1_sb[:, :], in_=s1_sb[:, :],
                                              func=AF.Exp, scale=-1.0))
        vsp = S.op("scalar", lambda e: e.activation(out=s1_sb[:, :], in_=s1_sb[:, :],
                                                    func=AF.Ln, bias=1.0),
                   inc=[("ac_r", 1)])["ac_r"]
        S.op("vector", lambda e: e.tensor_scalar(out=s2_sb[:, :], in0=dl_sb[:, :],
                                                 scalar1=0.0, scalar2=None,
                                                 op0=ALU.max),
             waits=[("ac_r", vsp)])
        S.op("vector", lambda e: e.tensor_tensor(out=s2_sb[:, :], in0=s2_sb[:, :],
                                                 in1=s1_sb[:, :], op=ALU.add))
        vda = S.op("vector", lambda e: e.tensor_scalar(out=s2_sb[:, :], in0=s2_sb[:, :],
                                                       scalar1=aset[:, 2:3],
                                                       scalar2=None, op0=ALU.mult),
                   inc=[("dv_scan", 1)])["dv_scan"]
        vab = S.op("scalar", lambda e: e.activation(out=s1_sb[:, :], in_=s2_sb[:, :],
                                                    func=AF.Exp),
                   inc=[("ac_r", 1)], waits=[("dv_scan", vda)])["ac_r"]
        S.op("vector", lambda e: e.tensor_scalar(out=dl_sb[:, :], in0=s2_sb[:, :],
                                                 scalar1=0.5, scalar2=1.0,
                                                 op0=ALU.mult, op1=ALU.add))
        S.op("vector", lambda e: e.tensor_tensor(out=s2_sb[:, :], in0=s2_sb[:, :],
                                                 in1=s2_sb[:, :], op=ALU.mult))
        S.op("vector", lambda e: e.scalar_tensor_tensor(out=dl_sb[:, :],
                                                        in0=s2_sb[:, :],
                                                        scalar=1.0 / 6.0,
                                                        in1=dl_sb[:, :],
                                                        op0=ALU.mult, op1=ALU.add))
        S.op("vector", lambda e: e.tensor_scalar(out=s2_sb[:, :], in0=s1_sb[:, :],
                                                 scalar1=-1.0, scalar2=aset[:, 6:7],
                                                 op0=ALU.add, op1=ALU.mult),
             waits=[("ac_r", vab)])
        S.op("vector", lambda e: e.scalar_tensor_tensor(out=s2_sb[:, :],
                                                        in0=dl_sb[:, :],
                                                        scalar=aset[:, 3:4],
                                                        in1=s2_sb[:, :],
                                                        op0=ALU.mult, op1=ALU.add))
        vfl = S.op("vector", lambda e: e.tensor_tensor(out=s2_sb[:, :],
                                                       in0=s2_sb[:, :],
                                                       in1=bi_sb[:, :], op=ALU.mult),
                   inc=[("dv_fl", 1)])["dv_fl"]
        hcur = hb_sb[tf % 2]
        S.wait("vector", "dv_fl", vfl)
        if tf == 0:
            vfl = S.op("vector", lambda e, hcur=hcur: e.tensor_tensor_scan(
                 out=hcur[:, :], data0=s1_sb[:, :], data1=s2_sb[:, :], initial=0.0,
                 op0=ALU.mult, op1=ALU.add), inc=[("dv_fl", 1)])["dv_fl"]
        else:
            hprev = hb_sb[(tf - 1) % 2]
            vfl = S.op("vector", lambda e, hcur=hcur, hprev=hprev: e.tensor_tensor_scan(
                 out=hcur[:, :], data0=s1_sb[:, :], data1=s2_sb[:, :],
                 initial=hprev[:, T - 1:T], op0=ALU.mult, op1=ALU.add),
                 inc=[("dv_fl", 1)])["dv_fl"]
        S.op("vector", lambda e, tf=tf, hcur=hcur: e.tensor_tensor(
             out=chb[:, tf * T:(tf + 1) * T], in0=c_sb[:, :], in1=hcur[:, :],
             op=ALU.mult),
             inc=[("dv_scan", 1)], waits=[("dv_fl", vfl)])
        v_scan[tf] = S.val("dv_scan")

    v_wo_u = [0] * FC
    v_stws_k = {}

    def sched_woq(k):
        tok = dma(
            "sync", "ldwo",
            lambda e, k=k: e.dma_start(out=wof[k % 2][:, :],
                                       in_=dwoutf[k * 128:(k + 1) * 128, :]),
            waits=[("dv_wo", v_wo_u[k - 2] if k >= 2 else 0)] +
                  ([("dv_ax", 16), ("pe_t", 4 * NHT)] if k == 0 else []))
        vfl = S.op("vector", lambda e, k=k: e.tensor_reduce(out=stats[:, 6:7],
                                                            in_=wof[k % 2][:, :],
                                                            axis=AX.X, op=ALU.max,
                                                            apply_absolute_value=True),
                   inc=[("dv_fl", 1)],
                   waits=[tok] + ([v_stws_k[k - 2]] if k >= 2 else []))["dv_fl"]
        vfl = S.op("vector", lambda e: e.reciprocal(out=stats[:, 7:8],
                                                    in_=stats[:, 6:7]),
                   inc=[("dv_fl", 1)], waits=[("dv_fl", vfl)])["dv_fl"]
        fop("vector", lambda e: e.tensor_scalar(out=stats[:, 7:8], in0=stats[:, 7:8],
                                                scalar1=127.0, scalar2=None,
                                                op0=ALU.mult),
            waits=[("dv_fl", vfl)])
        fop("vector", lambda e: e.tensor_scalar(out=stats[:, 6:7], in0=stats[:, 6:7],
                                                scalar1=1.0 / 127.0, scalar2=None,
                                                op0=ALU.mult))
        vz = round_scaled(("dv_w", "pq"), 1024, wof[k % 2][:, :],
                          wsim_bf[k % 2][:, :], stats[:, 7:8], stats[:, 6:7],
                          first_waits=None, first_inc=[("dv_wo", 1)])
        v_wo_u[k] = S.val("dv_wo")
        v_stws_k[k] = dma(
            "gpsimd", "stws",
            lambda e, k=k: e.dma_start(out=dwsim[k], in_=wsim_bf[k % 2][:, :]),
            waits=[("dv_w", vz)])

    def ext_view(i):
        return ext[i].rearrange("p (c w) -> p c w", w=EW)

    def sched_ph2_front(t):
        v_ldxn[t] = dma(
            "sync", "ldxn",
            lambda e, t=t: e.dma_start(out=xnf[t % 2][:, :], in_=dxn[t]),
            waits=w_stxn_all + [("dv_q", v_q[t - 2] if t >= 2 else 0)])
        S.op("vector",
             lambda e, t=t: e.tensor_copy(
                 out=ext_view(t % 2)[:, :, 1:1 + T],
                 in_=xnf[t % 2][:, :].rearrange("p (c w) -> p c w", w=T)),
             waits=[v_ldxn[t],
                    ("pe_proj", v_proj[t - 2] if t >= 2 else 0)] +
                   ([("pe_sc", S.val("pe_sc")), ("ac_xn", S.val("ac_xn")),
                     ("dv_ln", S.val("dv_ln"))]
                    if t == 0 else []) +
                   ([("ac_w", v_wq_all), ("dv_w", S.val("dv_w")),
                     ("pe_wb", S.val("pe_wb"))] if t < 2 else []))
        if t == 0:
            S.op("vector", lambda e: e.memset(ext_view(0)[:, :, 0:1], 0.0),
                 inc=[("dv_ext", 1)])
        else:
            S.op("vector", lambda e, t=t: e.tensor_copy(
                 out=ext_view(t % 2)[:, :, 0:1],
                 in_=ext_view((t - 1) % 2)[:, :, T:T + 1]))
            S.op("vector", lambda e, t=t: e.tensor_copy(
                 out=ext_view((t - 1) % 2)[:, :, T + 1:T + 3],
                 in_=ext_view(t % 2)[:, :, 1:3]),
                 inc=[("dv_ext", 1)])
        if t == NT - 1:
            S.op("vector", lambda e: e.memset(
                 ext_view((NT - 1) % 2)[:, :, T + 1:T + 3], 0.0),
                 inc=[("dv_ext", 1)])
        v_ext[t] = S.val("dv_ext")

        for pc in range(4):
            w = 1024
            xin = xnf[t % 2][:, pc * w:(pc + 1) * w]
            qout = qb[t % 2][:, pc * w:(pc + 1) * w]
            wl = []
            if t == 0 and pc == 0:
                wl.append(("dv_sc", v_sx))
            if t >= 2 and pc == 0:
                wl.append(("pe_mm1", v_mm1_t[t - 2]))
            round_scaled(("dv_q", "pq"), w, xin, qout, bca[:, 0:1], None,
                         first_waits=wl)
        v_q[t] = S.val("dv_q")

    def sched_ph2_body(t):
        if t >= 1:
            sched_ssm_front(t - 1, v_ext[t])
        if t + 1 < NT:
            sched_ph2_front(t + 1)

        for s in range(FC):
            bank = (0, 1, 6, 7)[s % 4]
            gi = t * FC + s
            for c in range(DC):
                wl = None
                if c == 0:
                    wl = [("dv_q", v_q[t]),
                          ("ac_g", v_gel[gi - 4] if gi >= 4 else 0)]
                    if gi == 0:
                        wl.extend(w_ldwq)
                S.op("tensor",
                     lambda e, t=t, s=s, c=c, bank=bank: e.matmul(
                         out=pb[bank],
                         lhsT=wbuf[:, c * 4096 + s * 128:c * 4096 + (s + 1) * 128],
                         rhs=qb[t % 2][:, c * T:(c + 1) * T],
                         start=(c == 0), stop=(c == DC - 1)),
                     inc=([("pe_mm1", 1)] if c == DC - 1 else None), waits=wl)
            vmm = S.val("pe_mm1")
            hb2 = (gi // 2) % 2
            wl = [("pe_mm1", vmm)]
            if gi == 0:
                wl.append(("dv_sc", v_sx))
            if gi <= 3:
                wl.append(("pe_w", FC * DC))
            if gi >= 4:
                wl.append(v_sth_pair[gi // 2 - 2])
                wl.append(("dv_h", v_dvh_pair[gi // 2 - 2]))
            v_gel[gi] = S.op("scalar",
                             lambda e, s=s, bank=bank, hb2=hb2: e.activation(
                                 out=hidb[hb2][:, (s % 2) * T:(s % 2 + 1) * T],
                                 in_=pb[bank], func=AF.Gelu,
                                 bias=binf_sb[:, s:s + 1], scale=bca[:, 1:2]),
                             inc=[("ac_g", 1)], waits=wl)["ac_g"]
            if s % 2 == 1:
                pc2 = gi // 2
                S.op("vector",
                     lambda e, hb2=hb2, pc2=pc2: e.tensor_reduce(
                         out=hstrip[:, pc2:pc2 + 1], in_=hidb[hb2][:, :],
                         axis=AX.X, op=ALU.max, apply_absolute_value=True),
                     inc=[("dv_h", 1)], waits=[("ac_g", v_gel[gi])])
                v_dvh_pair[gi // 2] = S.val("dv_h")
                v_sth_pair[gi // 2] = dma(
                    "gpsimd", "sth",
                    lambda e, t=t, s=s, hb2=hb2: e.dma_start(out=dhid[t, s // 2],
                                                             in_=hidb[hb2][:, :]),
                    waits=[("ac_g", v_gel[gi])])
        v_mm1_t[t] = S.val("pe_mm1")
        for kq in range(4 * t, min(4 * t + 4, FC)):
            sched_woq(kq)

    sched_ph2_front(0)
    for t in range(NT):
        sched_ph2_body(t)
    sched_ssm_front(NT - 1, v_ext[NT - 1])

    xop("vector", lambda e: e.tensor_reduce(out=amax_acc[:, 1:2],
                                            in_=hstrip[:, :], axis=AX.X,
                                            op=ALU.max),
        marks=["dv_h"], waits=[("dv_h", S.val("dv_h"))])
    v_sh = sched_amax_final(1, dcchi, dccho, 3, 2, "dv_h", S.val("dv_h"), ccd="cc2")
    if debug_taps:
        dma("gpsimd", "sttap", lambda e: e.dma_start(out=dscal, in_=scal[:, :]),
            waits=[("dv_sc", v_sh)])
        dma("gpsimd", "sttap", lambda e: e.dma_start(out=dchb, in_=chb[:, :]),
            waits=[("dv_scan", v_scan[NT - 1])])
        for (sm, v) in dtotal("sttap"):
            S.wait("gpsimd", sm, v)
    w_sth_all = dtotal("sth")
    w_stws_all = dtotal("stws")

    if stage <= 2:
        for (sm, v) in w_sth_all + w_stws_all:
            S.wait("gpsimd", sm, v)
        S.wait("gpsimd", "cc", S.val("cc"))
        return finish(nc, ctx, S, semh)

    for g4 in range(8):
        dma("sync", "ldws",
            lambda e, g4=g4: e.dma_start(
                out=wbuf[:, g4 * 4096:(g4 + 1) * 4096].rearrange(
                    "p (k w) -> p k w", k=4),
                in_=dwsim[g4 * 4:(g4 + 1) * 4].rearrange("k p w -> p k w")),
            waits=([("pe_mm1", NT * FC)] + w_stws_all if g4 == 0 else None))
    w_ldws = dtotal("ldws")

    v_qh_k = {}
    v_mm2_k = {}
    v_sto = {}
    v_y_s = {}
    v_tr_tile = {}
    v_sg = {}
    v_res = {}
    v_sl = {}
    v_ldx3 = {}

    v_ac_g_all = v_gel[NT * FC - 1]
    v_mm2a_k = {}
    v_trA = {}
    v_trB = {}
    for t in range(NT):
        for u in range(4):
            if t == 0:
                if u < 2:
                    wl = [("dv_ax", 16), ("pe_t", 4 * NHT),
                          ("dv_ln", S.val("dv_ln"))]
                else:
                    wl = [("dv_wo", v_wo_u[FC - 4 + u]),
                          ("dv_ln", S.val("dv_ln"))]
            else:
                wl = [("dv_res", v_res[(t - 1, u, 1)])]
            v_ldx3[(t, u)] = dma(
                "sync", "ldx3",
                lambda e, t=t, u=u: e.dma_start(
                    out=xr3[u][:, :],
                    in_=dx[t * T + u * 128:t * T + (u + 1) * 128, :]),
                waits=wl)

        def emit_quant(tq, g_lo, g_hi):
            for g in range(g_lo, g_hi):
                if 2 * g >= 4:
                    wl = [("dv_qh", v_qh_k[(tq, 2 * g - 4 + 1)])]
                elif tq >= 1:
                    wl = [("dv_qh", v_qh_k[(tq - 1, FC + 2 * g - 4 + 1)])]
                else:
                    wl = w_sth_all + [("dv_h", S.val("dv_h"))]
                tok_h = dma(
                    "sync", "ldh",
                    lambda e, tq=tq, g=g: e.dma_start(out=hidb[g % 2][:, :],
                                                      in_=dhid[tq, g]),
                    waits=wl)
                for kk2 in range(2):
                    k = 2 * g + kk2
                    qdst = qh[k // 16][:, (k % 16) * T:(k % 16 + 1) * T]
                    hin = hidb[g % 2][:, kk2 * T:(kk2 + 1) * T]
                    wl = [tok_h]
                    if tq == 0 and k == 0:
                        wl.append(("dv_sc", v_sh))
                        wl.append(("dv_q", v_q[NT - 1]))
                    if tq >= 1:
                        wl.append(("pe_mm2", v_mm2_k[(tq - 1, k)]))
                    v_qh_k[(tq, k)] = round_scaled(("dv_qh", "pq"), T, hin, qdst,
                                                   bca[:, 2:3], None,
                                                   first_waits=wl)

        if t == 0:
            emit_quant(0, 0, FC // 2)
        for k in range(FC):
            for sdx in range(4):
                wl = None
                if sdx == 0:
                    wl = [("dv_qh", v_qh_k[(t, k)])]
                    if k == 0:
                        wl.extend(w_ldws)
                        if t == 0:
                            wl.append(("dv_scan", v_scan[NT - 1]))
                            wl.append(("ac_g", v_ac_g_all))
                            wl.append(("ac_p80", S.val("ac_p80")))
                            wl.append(("ac_r", S.val("ac_r")))
                            wl.append(("ac_w", v_wq_all))
                            wl.append(("dv_sc", v_sh))
                        else:
                            wl.append(("ac_sl", v_sl[(t - 1, 3)]))
                            wl.append(("dv_sg", v_sg[(t - 1, 3)]))
                            wl.append(("dv_res", v_res[(t - 1, 3, 0)]))
                S.op("tensor",
                     lambda e, t=t, k=k, sdx=sdx: e.matmul(
                         out=pb[sdx],
                         lhsT=wbuf[:, k * 1024 + sdx * 128:k * 1024 + (sdx + 1) * 128],
                         rhs=qh[k // 16][:, (k % 16) * T:(k % 16 + 1) * T],
                         start=(k == 0), stop=(k == FC - 1)),
                     inc=([("pe_mm2", 1)] if sdx == 3 else None), waits=wl)
            v_mm2a_k[(t, k)] = S.val("pe_mm2")
        for k in range(FC):
            for sdx in range(4, 8):
                wl = None
                if sdx == 4 and k == 0:
                    if t == 0:
                        wl = [("ac_g", v_ac_g_all)]
                    else:
                        wl = [("ac_sl", v_sl[(t - 1, 7)]),
                              ("dv_sg", v_sg[(t - 1, 7)]),
                              ("dv_res", v_res[(t - 1, 3, 1)])]
                S.op("tensor",
                     lambda e, t=t, k=k, sdx=sdx: e.matmul(
                         out=pb[sdx],
                         lhsT=wbuf[:, k * 1024 + sdx * 128:k * 1024 + (sdx + 1) * 128],
                         rhs=qh[k // 16][:, (k % 16) * T:(k % 16 + 1) * T],
                         start=(k == 0), stop=(k == FC - 1)),
                     inc=([("pe_mm2", 1)] if sdx == 7 else None), waits=wl)
            v_mm2_k[(t, k)] = S.val("pe_mm2")
        if t + 1 < NT:
            emit_quant(t + 1, 0, 4)
        for sdx in range(8):
            half_last = v_mm2a_k[(t, FC - 1)] if sdx < 4 else v_mm2_k[(t, FC - 1)]
            wl = [("pe_mm2", half_last)]
            if sdx == 0:
                if t >= 1:
                    wl.append(("pe_tr", v_trA[t - 1]))
                else:
                    wl.append(("dv_sc", v_sh))
                    wl.append(("dv_scan", v_scan[NT - 1]))
            if sdx == 4 and t >= 1:
                wl.append(("pe_tr", v_trB[t - 1]))
            v_sl[(t, sdx)] = S.op("scalar",
                                  lambda e, sdx=sdx: e.activation(
                                      out=sg_sb[sdx][:, :], in_=pb[sdx],
                                      func=AF.Silu,
                                      bias=boutf_sb[:, sdx:sdx + 1],
                                      scale=bca[:, 3:4]),
                                  inc=[("ac_sl", 1)], waits=wl)["ac_sl"]

        def emit_half(h2):
            for si in range(4):
                sdx = h2 * 4 + si
                S.op("tensor",
                     lambda e, t=t, sdx=sdx: e.matmul(
                         out=pb[sdx],
                         lhsT=wouts_bf[0:NS, sdx * 128:(sdx + 1) * 128],
                         rhs=chb[0:NS, t * T:(t + 1) * T], start=True, stop=True),
                     inc=[("pe_y", 1)],
                     waits=[("ac_sl", v_sl[(t, sdx)])] +
                           ([v_wouts] if (t == 0 and sdx == 0) else []))
                v_y_s[(t, sdx)] = S.val("pe_y")
            for si in range(4):
                sdx = h2 * 4 + si
                v_sg[(t, sdx)] = S.op("vector",
                                      lambda e, sdx=sdx: e.scalar_tensor_tensor(
                                          out=sg_sb[sdx][:, :], in0=pb[sdx],
                                          scalar=bouts_sb[:, sdx:sdx + 1],
                                          in1=sg_sb[sdx][:, :],
                                          op0=ALU.add, op1=ALU.mult),
                                      inc=[("dv_sg", 1)],
                                      waits=[("pe_y", v_y_s[(t, sdx)])])["dv_sg"]
            for u in range(4):
                bank = 4 * h2 + u % 2
                for si in range(4):
                    sdx = h2 * 4 + si
                    wl = None
                    if si == 0:
                        wl = [("dv_sg", v_sg[(t, h2 * 4 + 3)])]
                        if u >= 2:
                            wl.append(("dv_res", v_res[(t, u - 2, h2)]))
                    S.op("tensor",
                         lambda e, sdx=sdx, u=u, bank=bank, si=si: e.transpose(
                             pb[bank][:, si * 128:(si + 1) * 128],
                             sg_sb[sdx][:, u * 128:(u + 1) * 128], ident[:, :]),
                         inc=([("pe_tr", 1)] if si == 3 else None), waits=wl)
                vtr = S.val("pe_tr")
                wl = [("pe_tr", vtr), v_ldx3[(t, u)]]
                if t == 0 and u == 0 and h2 == 0:
                    wl.append(("pe_proj", v_proj[NT - 1]))
                    wl.append(("pe_mm1", NT * FC))
                if t >= 2 and u == 0 and h2 == 0:
                    wl.append(v_sto[t - 2])
                v_res[(t, u, h2)] = S.op(
                    "vector",
                    lambda e, t=t, u=u, h2=h2, bank=bank: e.tensor_tensor(
                        out=outb[t % 2][:, u * 1024 + h2 * 512:
                                        u * 1024 + (h2 + 1) * 512],
                        in0=pb[bank],
                        in1=xr3[u][:, h2 * 512:(h2 + 1) * 512],
                        op=ALU.add),
                    inc=[("dv_res", 1)], waits=wl)["dv_res"]

        emit_half(0)
        v_trA[t] = S.val("pe_tr")
        emit_half(1)
        v_trB[t] = S.val("pe_tr")
        if t + 1 < NT:
            emit_quant(t + 1, 4, FC // 2)
        v_sto[t] = dma(
            "gpsimd", "sto",
            lambda e, t=t: e.dma_start(
                out=dout[t * T:(t + 1) * T, :].rearrange("(u p) d -> p u d", p=128),
                in_=outb[t % 2][:, :].rearrange("p (u d) -> p u d", u=4)),
            waits=[("dv_res", v_res[(t, 3, 1)])])

    for (sm, v) in dtotal("sto"):
        S.wait("gpsimd", sm, v)
    return finish(nc, ctx, S, semh)


def finish(nc, ctx, S, semh):
    with nc.Block() as block:
        def mk(ename):
            def emitter(eng):
                for it in S.items:
                    if it[1] != ename:
                        continue
                    if it[0] == "w":
                        eng.wait_ge(semh[it[2]], it[3])
                    elif it[0] == "m":
                        eng.sem_inc(semh[it[2]], 1)
                    else:
                        inst = it[2](eng)
                        for (s, a) in it[3]:
                            if a == "dflt":
                                inst.then_inc(semh[s])
                            else:
                                inst.then_inc(semh[s], a)
            return emitter
        block.sync(mk("sync"))
        block.tensor(mk("tensor"))
        block.vector(mk("vector"))
        block.scalar(mk("scalar"))
        block.gpsimd(mk("gpsimd"))
    ctx.close()
    return nc


def make_consts():
    ident = np.eye(128, dtype=np.float32)
    ident_bf = np.eye(128, dtype=ml_dtypes.bfloat16)
    ones = np.ones((1, 512), dtype=np.float32)
    R = np.zeros((80, 48), dtype=np.float32)
    for n in range(16):
        R[n, n] = 1.0
        R[16 + 2 * n, 16 + n] = 0.5
        R[17 + 2 * n, 16 + n] = 0.5
        R[48 + 2 * n, 32 + n] = 0.5
        R[49 + 2 * n, 32 + n] = 0.5
    return ident, ident_bf, ones, R


def make_in_maps(inputs):
    ident, ident_bf, ones, R = make_consts()
    x = np.ascontiguousarray(np.asarray(inputs["L0_attn_x"], dtype=np.float32))
    common = {
        "conv_w": np.ascontiguousarray(
            np.asarray(inputs["L0_attn_conv_w"], np.float32).reshape(DC, 128, KC)
            .transpose(1, 0, 2).reshape(128, DC * KC)),
        "conv_b": np.ascontiguousarray(
            np.asarray(inputs["L0_attn_conv_b"], np.float32).reshape(DC, 128)),
        "Win": np.ascontiguousarray(
            np.asarray(inputs["L0_attn_Win"], np.float32).reshape(DC, 128, 80)
            .transpose(1, 0, 2).reshape(128, DC * 80)),
        "b_in": np.ascontiguousarray(
            np.asarray(inputs["L0_attn_b_in"], np.float32).reshape(1, 80)),
        "Wout_s": np.ascontiguousarray(np.asarray(inputs["L0_attn_Wout"], np.float32)),
        "b_out_s": np.ascontiguousarray(
            np.asarray(inputs["L0_attn_b_out"], np.float32).reshape(DC, 128)),
        "A_log": np.ascontiguousarray(
            np.asarray(inputs["L0_attn_A_log"], np.float32).reshape(1, NS)),
        "W_in_f": np.ascontiguousarray(np.asarray(inputs["L0_ffn_W_in"], np.float32)),
        "b_in_f": np.ascontiguousarray(
            np.asarray(inputs["L0_ffn_b_in"], np.float32).reshape(FC, 128)),
        "W_out_f": np.ascontiguousarray(np.asarray(inputs["L0_ffn_W_out"], np.float32)),
        "b_out_f": np.ascontiguousarray(
            np.asarray(inputs["L0_ffn_b_out"], np.float32).reshape(DC, 128)),
        "ident": ident, "ident_bf": ident_bf, "ones_r": ones, "Rmat": R,
    }
    return [dict(common, x=np.ascontiguousarray(x[c])) for c in range(B)]


def kernel(**inputs):
    in_maps = make_in_maps(inputs)
    nc = build_program(stage=4, debug_taps=False)
    res = run_bass_kernel_spmd(nc, in_maps, core_ids=list(range(B)))
    return np.stack([res.results[c]["out"] for c in range(B)], axis=0)


# version 15 (final)
# speedup vs baseline: 1.0764x; 1.0411x over previous
import numpy as np
import ml_dtypes
import concourse.bass as bass
import concourse.mybir as mybir
from concourse.bass_utils import run_bass_kernel_spmd

F32 = mybir.dt.float32
BF16 = mybir.dt.bfloat16
I32 = mybir.dt.int32
AF = mybir.ActivationFunctionType
ALU = mybir.AluOpType
AX = mybir.AxisListType

B = 8
L = 4096
D = 1024
DFF = 4096
NS = 16
KC = 4
T = 512
NT = L // T
HT = 256
NHT = L // HT
DC = D // 128
FC = DFF // 128
EW = T + 3
EPS_LN = 1e-5
EPS_A = 1e-4
RG = [[0, 1, 2, 3, 4, 5, 6, 7]]

ROUND_MODE = "rint"
SKIP_CC = False
PH1_HALVES = NHT
DO_WQ = 5
DO_AMAXF = True
WQ_NOQUANT = False


class Sched:

    def __init__(self):
        self.items = []
        self.counts = {}

    def val(self, sem):
        return self.counts.get(sem, 0)

    def wait(self, eng, sem, v):
        if v > 0:
            self.items.append(("w", eng, sem, v))

    def mark(self, eng, sem, semh_ref=None):
        self.counts[sem] = self.counts.get(sem, 0) + 1
        v = self.counts[sem]
        self.items.append(("m", eng, sem, v))
        return v

    def op(self, eng, fn, inc=None, waits=None):
        if waits:
            for (s, v) in waits:
                self.wait(eng, s, v)
        incs = []
        if inc:
            for (s, a) in inc:
                if a == "dflt":
                    self.counts[s] = self.counts.get(s, 0) + 1
                else:
                    assert a > 0
                    self.counts[s] = self.counts.get(s, 0) + a
                incs.append((s, a))
        self.items.append(("o", eng, fn, incs))
        out = {}
        for (s, a) in incs:
            out[s] = self.counts[s]
        return out


def build_program(stage=4, debug_taps=False):
    nc = bass.Bass("TRN2", target_bir_lowering=False)

    dx = nc.dram_tensor("x", [L, D], F32, kind="ExternalInput").ap()
    dconvw = nc.dram_tensor("conv_w", [128, DC * KC], F32, kind="ExternalInput").ap()
    dconvb = nc.dram_tensor("conv_b", [DC, 128], F32, kind="ExternalInput").ap()
    dwin = nc.dram_tensor("Win", [128, DC * 80], F32, kind="ExternalInput").ap()
    dbin = nc.dram_tensor("b_in", [1, 80], F32, kind="ExternalInput").ap()
    dwouts = nc.dram_tensor("Wout_s", [NS, D], F32, kind="ExternalInput").ap()
    dbouts = nc.dram_tensor("b_out_s", [DC, 128], F32, kind="ExternalInput").ap()
    dalog = nc.dram_tensor("A_log", [1, NS], F32, kind="ExternalInput").ap()
    dwinf = nc.dram_tensor("W_in_f", [D, DFF], F32, kind="ExternalInput").ap()
    dbinf = nc.dram_tensor("b_in_f", [FC, 128], F32, kind="ExternalInput").ap()
    dwoutf = nc.dram_tensor("W_out_f", [DFF, D], F32, kind="ExternalInput").ap()
    dboutf = nc.dram_tensor("b_out_f", [DC, 128], F32, kind="ExternalInput").ap()
    dident = nc.dram_tensor("ident", [128, 128], F32, kind="ExternalInput").ap()
    dident_bf = nc.dram_tensor("ident_bf", [128, 128], BF16, kind="ExternalInput").ap()
    dones = nc.dram_tensor("ones_r", [1, 512], F32, kind="ExternalInput").ap()
    dR = nc.dram_tensor("Rmat", [80, 48], F32, kind="ExternalInput").ap()

    dout = nc.dram_tensor("out", [L, D], F32, kind="ExternalOutput").ap()
    tap = "ExternalOutput" if debug_taps else "Internal"
    dxn = nc.dram_tensor("xn_spill", [NT, 128, DC * T], F32, kind=tap).ap()
    dhid = nc.dram_tensor("hid_spill", [NT, FC // 2, 128, 1024], F32, kind=tap).ap()
    dwsim = nc.dram_tensor("wsim_stage", [FC, 128, 1024], BF16, kind=tap).ap()
    dwinT = nc.dram_tensor("winq_T", [DC, DFF, 128], BF16).ap()
    dwq = nc.dram_tensor("winq_tap", [1, 128, 32768], BF16,
                         kind=tap if debug_taps else "Internal").ap()
    dccxi = nc.dram_tensor("ccx_in", [1, 128], F32)
    dccxo = nc.dram_tensor("ccx_out", [1, 128], F32)
    dcchi = nc.dram_tensor("cch_in", [1, 128], F32)
    dccho = nc.dram_tensor("cch_out", [1, 128], F32)
    dscal = nc.dram_tensor("scal_tap", [1, 8], F32,
                           kind=tap if debug_taps else "Internal").ap()
    dchb = nc.dram_tensor("chb_tap", [NS, L], mybir.dt.bfloat16,
                          kind=tap if debug_taps else "Internal").ap()

    import contextlib
    ctx = contextlib.ExitStack()
    sb = lambda name, shape, dt: ctx.enter_context(nc.sbuf_tensor(name, shape, dt))

    wbuf = sb("wbuf", [128, 32768], BF16)
    chb = sb("chb", [NS, L], BF16)
    xbuf = [sb(f"xbuf{i}", [128, 2048], F32) for i in range(2)]
    xnf = [sb(f"xnf{i}", [128, DC * T], F32) for i in range(2)]
    ar2 = sb("ar2", [128, 33024], mybir.dt.uint8)
    ar3 = sb("ar3", [128, 22528], mybir.dt.uint8)
    hidb = [sb(f"hidb{i}", [128, 1024], F32) for i in range(2)]
    wsim_bf = [sb(f"wsimbf{i}", [128, 1024], BF16) for i in range(2)]
    u_s = sb("u_s", [128, 1024], F32)
    i_s = sb("i_s", [128, 1024], I32)
    f_s = sb("f_s", [128, 1024], F32) if ROUND_MODE == "floor_cmp" else u_s
    ident = sb("identt", [128, 128], F32)
    ident_bf = sb("ident_bft", [128, 128], BF16)
    ones_r = sb("ones_rt", [1, 512], F32)
    rmat = sb("rmatt", [80, 48], F32)
    wt_bf = [sb(f"wt{k}", [128, DC * 80], BF16) for k in range(KC)]
    wc_sb = sb("wc_sbt", [128, DC * KC], F32)
    binf_sb = sb("binf_sbt", [128, FC], F32)
    boutf_sb = sb("boutf_sbt", [128, DC], F32)
    bouts_sb = sb("bouts_sbt", [128, DC], F32)
    cbf_sb = sb("cbf_sbt", [128, DC], F32)
    wouts_bf = sb("wouts_bft", [NS, D], BF16)
    binT = sb("binT", [80, 1], F32)
    bcomb = sb("bcomb", [80, 1], F32)
    bd = sb("bd", [NS, 6], F32)
    bdrow = sb("bdrow", [1, 3 * NS], F32)
    aset = sb("aset", [NS, 8], F32)
    bca = sb("bca", [128, 4], F32)
    scal = sb("scal", [1, 8], F32)
    amax_acc = sb("amax_acc", [128, 2], F32)
    stats = sb("stats", [128, 12], F32)
    statsW = sb("statsW", [128, 64], F32)
    statsW2 = sb("statsW2", [128, 32], F32)
    hstrip = sb("hstrip", [128, 128], F32)

    ar2b = ar2[:, :].bitcast(BF16)
    ext = [ar2b[:, i * (EW * DC):(i + 1) * (EW * DC)] for i in range(2)]
    qb = [ar2b[:, 2 * EW * DC + i * 4096: 2 * EW * DC + (i + 1) * 4096] for i in range(2)]
    ar2f = ar2[:, :].bitcast(F32)
    outb = [ar2f[:, i * 4096:(i + 1) * 4096] for i in range(2)]
    win_sb = ar2f[:, 0:640]
    sqs = ar2f[:, 640:1664]
    ld8a = ar2f[0:8, 1664:1792]
    ld8b = ar2f[0:8, 1920:2048]
    ld8c = ar2f[0:8, 2048:2176]
    ld32 = ar2f[0:32, 1792:1920]

    ar3f = ar3[:, :].bitcast(F32)
    wslc = [ar3f[:, i * 128:(i + 1) * 128] for i in range(2)]
    wtk = ar3f[:, 256:1280]
    wsimT = ar3f[:, 1280:1792].bitcast(BF16)
    p80_sb = ar3f[0:80, 0:512]
    _o = 2048
    dl_sb = ar3f[0:NS, _o:_o + 512]
    bi_sb = ar3f[0:NS, _o + 512:_o + 1024]
    c_sb = ar3f[0:NS, _o + 1024:_o + 1536]
    s1_sb = ar3f[0:NS, _o + 1536:_o + 2048]
    s2_sb = ar3f[0:NS, _o + 2048:_o + 2560]
    hb_sb = [ar3f[0:NS, _o + 2560 + i * 512:_o + 2560 + (i + 1) * 512] for i in range(2)]
    sg_sb = [ar3f[:, i * 512:(i + 1) * 512] for i in range(8)]
    g_sb = [ar3f[:, 4096 + i * 512:4096 + (i + 1) * 512] for i in range(2)]
    ccst = ar3f[0:1, 1792:1920]
    b80 = ar3f[0:1, 1920:2000]
    alog_row = ar3f[0:1, 2000:2016]

    wof = [xbuf[i][:, 0:1024] for i in range(2)]
    xr3 = [xbuf[0][:, 1024:2048], xbuf[1][:, 1024:2048],
           xbuf[0][:, 0:1024], xbuf[1][:, 0:1024]]
    qh = [xnf[i][:, :].bitcast(BF16) for i in range(2)]
    wtk_b = [None, None]
    wsimT_b = [None, None]

    wtk_b[0] = wtk
    wtk_b[1] = ar2f[:, 2176:3200]
    wsimT_b[0] = wsimT
    wsimT_b[1] = ar2f[:, 3200:3712].bitcast(BF16)
    wslc8 = [hidb[0][:, :], hidb[1][:, :]]
    pbk = [ctx.enter_context(nc.psum_tensor(f"pb{i}", [128, 512], F32)) for i in range(8)]
    pb = [p[:, :] for p in pbk]
    pb4bf = pbk[4][:, :].bitcast(BF16)
    pb5bf = pbk[5][:, :].bitcast(BF16)

    sem_names = [
        "cc", "cc1", "cc2", "dv_fl", "ac_fl", "dv_xn",
        "pe_t", "pe_w", "pe_wb", "pe_proj", "pe_r", "pe_mm1", "pe_mm2",
        "pe_y", "pe_tr", "pe_sc",
        "dv_ln", "dv_ax", "dv_q", "dv_w", "dv_wo", "dv_scan", "dv_sg",
        "dv_res", "dv_qh", "dv_h", "dv_sc", "dv_ext", "pq",
        "ac_xn", "ac_ev", "ac_g", "ac_sl", "ac_r", "ac_w", "ac_p80",
    ]
    streams = {"ldx": 2, "ldxn": 2, "ldh": 2, "ldw": 2, "ldwo": 2, "ldws": 1,
               "ldsm": 1, "ldx3": 4, "stxn": 2, "sth": 2, "stws": 2, "sto": 2,
               "ldwb": 1, "sttap": 1, "stwq": 2, "ldwq": 1}
    for nm, nb in streams.items():
        for b in range(nb):
            sem_names.append(f"{nm}{b}")
    semh = {n: ctx.enter_context(nc.semaphore(n)) for n in sem_names}
    S = Sched()
    dcount = {}

    def xop(eng, fn, marks, waits=None):

        sem = "dv_fl" if eng == "vector" else "ac_fl"
        r = S.op(eng, fn, inc=[(sem, 1)], waits=waits)
        S.wait(eng, sem, r[sem])
        out = {}
        for m in marks:
            out[m] = S.mark(eng, m)
        return out

    def warm_mm(bank):
        S.op("tensor", lambda e, bank=bank: e.matmul(out=bank[0:64, 448:512],
                                                     lhsT=ident[0:1, 0:64],
                                                     rhs=ident[0:1, 64:128],
                                                     start=True, stop=True))

    def fop(eng, fn, inc=None, waits=None):
        if inc:
            r = S.op(eng, fn, inc=inc, waits=waits)
            s0 = inc[0][0]
            S.wait(eng, s0, r[s0])
            return r
        sem = "dv_fl" if eng == "vector" else "ac_fl"
        r = S.op(eng, fn, inc=[(sem, 1)], waits=waits)
        S.wait(eng, sem, r[sem])
        return r

    def dma(eng, stream, fn, waits=None):
        nb = streams[stream]
        i = dcount.get(stream, 0)
        dcount[stream] = i + 1
        sem = f"{stream}{i % nb}"
        S.op(eng, fn, inc=[(sem, 16)], waits=waits)
        return (sem, 16 * (i // nb + 1))

    def dtotal(stream):
        nb = streams[stream]
        c = dcount.get(stream, 0)
        return [(f"{stream}{b}", 16 * ((c + nb - 1 - b) // nb))
                for b in range(nb) if (c + nb - 1 - b) // nb > 0]

    def _ld(dst, src):
        return lambda e, d=dst, s=src: e.dma_start(out=d, in_=s)

    for dst, src in [(ident[:, :], dident), (ident_bf[:, :], dident_bf),
                     (ones_r[:, :], dones), (rmat[:, :], dR),
                     (b80, dbin), (alog_row, dalog),
                     (win_sb, dwin), (wc_sb[:, :], dconvw),
                     (ld8a, dconvb), (ld8b, dbouts), (ld8c, dboutf),
                     (ld32, dbinf)]:
        dma("sync", "ldsm", _ld(dst, src))
    w_ldsm = dtotal("ldsm")
    v_wouts = dma("gpsimd", "ldwb",
                  lambda e: e.dma_start(out=wouts_bf[:, :], in_=dwouts))

    bias_jobs = [(ld8a, cbf_sb, 8), (ld8b, bouts_sb, 8), (ld8c, boutf_sb, 8),
                 (ld32, binf_sb, 32)]
    for (stg, dst, rows) in bias_jobs:
        S.op("tensor",
             lambda e, st=stg, r=rows: e.transpose(pb[5][0:128, 0:r], st[0:r, :],
                                                   ident[0:r, 0:r]),
             inc=[("pe_sc", 1)],
             waits=w_ldsm + [("ac_w", S.val("ac_w"))])
        xop("scalar",
            lambda e, d=dst, r=rows: e.copy(out=d[:, 0:r], in_=pb[5][0:128, 0:r]),
            marks=["ac_w"], waits=[("pe_sc", S.val("pe_sc"))])
    S.op("tensor", lambda e: e.transpose(pb[5][0:80, 0:1], b80,
                                         ident[0:1, 0:1]),
         inc=[("pe_sc", 1)], waits=w_ldsm + [("ac_w", S.val("ac_w"))])
    xop("scalar", lambda e: e.copy(out=binT[:, :], in_=pb[5][0:80, 0:1]),
        marks=["ac_w"], waits=[("pe_sc", S.val("pe_sc"))])
    S.op("tensor", lambda e: e.transpose(pb[5][0:NS, 0:1], alog_row,
                                         ident[0:1, 0:1]),
         inc=[("pe_sc", 1)], waits=[("ac_w", S.val("ac_w"))])
    xop("scalar", lambda e: e.copy(out=aset[:, 0:1], in_=pb[5][0:NS, 0:1]),
        marks=["ac_w"], waits=[("pe_sc", S.val("pe_sc"))])
    v_ac_bias = S.val("ac_w")

    for c in range(DC):
        for kk in range(KC):
            S.op("vector",
                 lambda e, c=c, kk=kk: e.tensor_scalar(
                     out=wt_bf[kk][:, c * 80:(c + 1) * 80],
                     in0=win_sb[:, c * 80:(c + 1) * 80],
                     scalar1=wc_sb[:, c * KC + kk:c * KC + kk + 1], scalar2=None,
                     op0=ALU.mult),
                 waits=(w_ldsm if (c == 0 and kk == 0) else None))

    xop("scalar", lambda e: e.activation(out=aset[:, 1:2], in_=aset[:, 0:1],
                                         func=AF.Exp),
        marks=["ac_w"], waits=[("ac_w", v_ac_bias)])
    vA = S.val("ac_w")
    fop("vector", lambda e: e.tensor_scalar(out=aset[:, 2:3], in0=aset[:, 1:2],
                                            scalar1=-1.0, scalar2=None, op0=ALU.mult),
        waits=[("ac_w", vA)])
    fop("vector", lambda e: e.tensor_scalar(out=aset[:, 3:4], in0=aset[:, 1:2],
                                            scalar1=EPS_A, scalar2=None, op0=ALU.is_lt))
    fop("vector", lambda e: e.tensor_scalar(out=aset[:, 4:5], in0=aset[:, 3:4],
                                            scalar1=-1.0, scalar2=1.0,
                                            op0=ALU.mult, op1=ALU.add))
    vfl = S.op("vector", lambda e: e.scalar_tensor_tensor(out=aset[:, 5:6],
                                                          in0=aset[:, 2:3],
                                                          scalar=aset[:, 4:5],
                                                          in1=aset[:, 3:4],
                                                          op0=ALU.mult, op1=ALU.add),
               inc=[("dv_fl", 1)])["dv_fl"]
    vfl = S.op("vector", lambda e: e.reciprocal(out=aset[:, 6:7], in_=aset[:, 5:6]),
               inc=[("dv_fl", 1)], waits=[("dv_fl", vfl)])["dv_fl"]
    fop("vector", lambda e: e.tensor_scalar(out=aset[:, 6:7], in0=aset[:, 6:7],
                                            scalar1=aset[:, 4:5], scalar2=None,
                                            op0=ALU.mult),
        waits=[("dv_fl", vfl)])

    for c in range(DC):
        S.op("tensor",
             lambda e, c=c: e.matmul(out=pb[5][0:80, 0:1],
                                     lhsT=win_sb[:, c * 80:(c + 1) * 80],
                                     rhs=cbf_sb[:, c:c + 1],
                                     start=(c == 0), stop=(c == DC - 1)),
             inc=([("pe_sc", 1)] if c == DC - 1 else None),
             waits=(w_ldsm + [("ac_w", v_ac_bias)] if c == 0 else None))
    fop("vector", lambda e: e.tensor_tensor(out=bcomb[:, :], in0=pb[5][0:80, 0:1],
                                            in1=binT[:, :], op=ALU.add),
        inc=[("dv_sc", 1)], waits=[("pe_sc", S.val("pe_sc")), ("ac_w", S.val("ac_w"))])
    v_bcomb = S.val("dv_sc")
    for (j3, rhs_t) in ((0, bcomb), (3, binT)):
        for i3 in range(3):
            S.op("tensor",
                 lambda e, i3=i3, rhs_t=rhs_t: e.matmul(
                     out=pb[5][0:NS, 0:1], lhsT=rmat[:, i3 * NS:(i3 + 1) * NS],
                     rhs=rhs_t[:, :], start=True, stop=True),
                 inc=[("pe_sc", 1)],
                 waits=[("dv_sc", v_bcomb), ("ac_w", S.val("ac_w"))])
            xop("scalar",
                lambda e, j3=j3, i3=i3: e.copy(out=bd[:, j3 + i3:j3 + i3 + 1],
                                               in_=pb[5][0:NS, 0:1]),
                marks=["ac_w"], waits=[("pe_sc", S.val("pe_sc"))])

    for i3 in range(3):
        S.op("tensor",
             lambda e, i3=i3: e.transpose(pb[5][0:1, 0:NS], bd[:, i3:i3 + 1],
                                          ident[0:NS, 0:NS]),
             inc=[("pe_sc", 1)], waits=[("ac_w", S.val("ac_w"))])
        xop("scalar",
            lambda e, i3=i3: e.copy(out=bdrow[0:1, i3 * NS:(i3 + 1) * NS],
                                    in_=pb[5][0:1, 0:NS]),
            marks=["ac_w"], waits=[("pe_sc", S.val("pe_sc"))])
    fop("vector", lambda e: e.memset(stats[:, 8:9], EPS_LN))
    S.op("vector", lambda e: e.memset(ccst[:, :], 0.0))
    S.op("vector", lambda e: e.memset(amax_acc[:, :], 0.0), inc=[("dv_sc", 1)])
    v_init = S.val("dv_sc")

    def round_scaled(pref, width, in_ap, out_ap, scale_ap, post_scale_ap,
                     first_waits, first_inc=None):
        vsem, psem = pref
        w = width
        fi = list(first_inc) if first_inc else [(vsem, 1)]
        fsem = fi[0][0]
        if ROUND_MODE == "floor_cmp":
            r = S.op("vector",
                     lambda e: e.tensor_scalar(out=u_s[:, 0:w], in0=in_ap,
                                               scalar1=scale_ap, scalar2=0.5,
                                               op0=ALU.mult, op1=ALU.add),
                     inc=fi, waits=first_waits)
            va = r[fsem]
            S.op("gpsimd", lambda e: e.tensor_copy(out=i_s[:, 0:w], in_=u_s[:, 0:w]),
                 waits=[(fsem, va)])
            vb = S.op("gpsimd", lambda e: e.tensor_copy(out=f_s[:, 0:w],
                                                        in_=i_s[:, 0:w]),
                      inc=[(psem, 1)])[psem]
            S.op("vector", lambda e: e.tensor_tensor(out=u_s[:, 0:w], in0=f_s[:, 0:w],
                                                     in1=u_s[:, 0:w], op=ALU.is_gt),
                 waits=[(psem, vb)])
            S.op("vector", lambda e: e.tensor_tensor(out=f_s[:, 0:w], in0=f_s[:, 0:w],
                                                     in1=u_s[:, 0:w], op=ALU.subtract))
            ps = post_scale_ap if post_scale_ap is not None else 1.0
            vz = S.op("vector",
                      lambda e, ps=ps: e.tensor_scalar(out=out_ap, in0=f_s[:, 0:w],
                                                       scalar1=ps, scalar2=None,
                                                       op0=ALU.mult),
                      inc=[(vsem, 1)])[vsem]
        else:
            S.op("vector",
                 lambda e: e.tensor_scalar(out=u_s[:, 0:w], in0=in_ap,
                                           scalar1=scale_ap, scalar2=None,
                                           op0=ALU.mult),
                 inc=fi if first_inc else None, waits=first_waits)
            S.op("vector", lambda e: e.tensor_copy(out=i_s[:, 0:w], in_=u_s[:, 0:w]))
            ps = post_scale_ap if post_scale_ap is not None else 1.0
            vz = S.op("vector",
                      lambda e, ps=ps: e.tensor_scalar(out=out_ap, in0=i_s[:, 0:w],
                                                       scalar1=ps, scalar2=None,
                                                       op0=ALU.mult),
                      inc=[(vsem, 1)])[vsem]
        return vz

    v_ldw_pc = {}
    v_pe_w_pc = {}
    v_wqst = {}
    v_ac_wtk = {}
    v_dvw_u = {}
    v_wq_done = {}

    def sched_winq_loads(k):
        v_ldw_pc[k] = dma(
            "sync", "ldw",
            lambda e, k=k: e.dma_start(
                out=wslc8[k % 2][:, :].rearrange("p (c w) -> p c w", w=128),
                in_=dwinf[:, k * 128:(k + 1) * 128].rearrange(
                    "(c p) w -> p c w", p=128)),
            waits=[("pe_w", max(0, (k - 1) * DC))])

    def sched_winq_pair(k):
        if DO_WQ < 2:
            return
        banks = {k: (pb[2], pb[3], pb4bf), k + 1: (pb[6], pb[7], pb5bf)}
        wtks = {k: wtk_b[0], k + 1: wtk_b[1]}
        wsims = {k: wsimT_b[0], k + 1: wsimT_b[1]}
        for kk in (k, k + 1):
            bA, bB, _ = banks[kk]
            for c in range(DC):
                g = kk * DC + c
                wl = [v_ldw_pc[kk]]
                if c == 0:
                    wl.append(("ac_w", S.val("ac_w")))
                S.op("tensor",
                     lambda e, kk=kk, c=c, bA=bA, bB=bB: e.transpose(
                         (bA if c < 4 else bB)[:, (c % 4) * 128:(c % 4 + 1) * 128],
                         wslc8[kk % 2][:, c * 128:(c + 1) * 128], ident[:, :]),
                     inc=[("pe_w", 1)], waits=wl)
                v_pe_w_pc[g] = S.val("pe_w")
                if c == 1:
                    warm_mm(bA)
        for kk in (k, k + 1):
            bA, bB, _ = banks[kk]
            xop("scalar", lambda e, bA=bA, kk=kk: e.copy(out=wtks[kk][:, 0:512], in_=bA),
                marks=["ac_w"],
                waits=[("pe_w", v_pe_w_pc[kk * DC + 3]), ("dv_w", S.val("dv_w"))])
            xop("scalar", lambda e, bB=bB, kk=kk: e.copy(out=wtks[kk][:, 512:1024], in_=bB),
                marks=["ac_w"], waits=[("pe_w", v_pe_w_pc[kk * DC + 7])])
        vac = S.val("ac_w")
        if DO_WQ < 3:
            return
        S.op("vector", lambda e: e.tensor_reduce(out=stats[:, 6:7], in_=wtks[k][:, :],
                                                 axis=AX.X, op=ALU.max,
                                                 apply_absolute_value=True),
             waits=[("ac_w", vac)])
        vfl = xop("vector", lambda e: e.tensor_reduce(out=stats[:, 7:8],
                                                      in_=wtks[k + 1][:, :],
                                                      axis=AX.X, op=ALU.max,
                                                      apply_absolute_value=True),
                  marks=["dv_fl"])["dv_fl"]
        vfl = xop("vector", lambda e: e.reciprocal(out=stats[:, 9:11],
                                                   in_=stats[:, 6:8]),
                  marks=["dv_fl"], waits=[("dv_fl", vfl)])["dv_fl"]
        fop("vector", lambda e: e.tensor_scalar(out=stats[:, 9:11], in0=stats[:, 9:11],
                                                scalar1=127.0, scalar2=None,
                                                op0=ALU.mult),
            waits=[("dv_fl", vfl)])
        fop("vector", lambda e: e.tensor_scalar(out=stats[:, 6:8], in0=stats[:, 6:8],
                                                scalar1=1.0 / 127.0, scalar2=None,
                                                op0=ALU.mult),
            inc=[("dv_w", 1)])
        for kk in (k, k + 1):
            j = kk - k
            v_dvw_u[kk] = round_scaled(
                ("dv_w", "pq"), 1024, wtks[kk][:, :], wsims[kk][:, :],
                stats[:, 9 + j:10 + j], stats[:, 6 + j:7 + j],
                first_waits=([v_wqst[kk - 2]] if kk - 2 in v_wqst else []))
        if DO_WQ < 4:
            return
        for kk in (k, k + 1):
            v_wqst[kk] = dma(
                "gpsimd", "stwq",
                lambda e, kk=kk: e.dma_start(
                    out=dwinT[:, kk * 128:(kk + 1) * 128, :].rearrange(
                        "c p w -> p c w"),
                    in_=wsims[kk][:, :].rearrange("p (c w) -> p c w", w=128)),
                waits=[("dv_w", v_dvw_u[kk])])
        v_wq_done[k] = S.val("ac_w")

    v_ldx = [0] * (2 * NHT)
    v_spill = [0] * NT
    v_ax = [0] * NHT
    ev_pair_vals = []
    v_sumq = {}

    def sched_ph1_passA(h):
        hb = h % 2
        wl = [("dv_ln", v_sumq[(h - 2, 1)] if h >= 2 else 0)]
        v_ldx[h] = dma(
            "sync", "ldx",
            lambda e, h=h, hb=hb: e.dma_start(
                out=xbuf[hb][:, :].rearrange("p (u d) -> p u d", u=2),
                in_=dx[h * HT:(h + 1) * HT, :].rearrange("(u p) d -> p u d", p=128)),
            waits=wl)
        for u in range(2):
            j = 2 * h + u
            xin = xbuf[hb][:, u * 1024:(u + 1) * 1024]
            S.op("vector",
                 lambda e, xin=xin, j=j: e.tensor_reduce(
                     out=statsW[:, j:j + 1], in_=xin, axis=AX.X, op=ALU.add),
                 waits=[v_ldx[h]])
            wl_sq = [v_ldx[h]]
            prev = (h, 0) if u == 1 else (h - 1, 1)
            if prev in v_sumq:
                wl_sq.append(("dv_ln", v_sumq[prev]))
            vsq = S.op("scalar",
                       lambda e, xin=xin: e.activation(out=sqs[:, :], in_=xin,
                                                       func=AF.Square),
                       inc=[("ac_xn", 1)], waits=wl_sq)["ac_xn"]
            v_sumq[(h, u)] = S.op(
                "vector",
                lambda e, j=j: e.tensor_reduce(out=statsW[:, 32 + j:33 + j],
                                               in_=sqs[:, :], axis=AX.X, op=ALU.add),
                inc=[("dv_ln", 1)], waits=[("ac_xn", vsq)])["dv_ln"]

    def sched_ph1_wide():
        S.op("vector", lambda e: e.tensor_scalar(out=statsW[:, 0:32], in0=statsW[:, 0:32],
                                                 scalar1=1.0 / D, scalar2=None,
                                                 op0=ALU.mult))
        S.op("vector", lambda e: e.tensor_scalar(out=statsW[:, 32:64], in0=statsW[:, 32:64],
                                                 scalar1=1.0 / D, scalar2=None,
                                                 op0=ALU.mult))
        S.op("vector", lambda e: e.tensor_tensor(out=statsW2[:, :], in0=statsW[:, 0:32],
                                                 in1=statsW[:, 0:32], op=ALU.mult))
        vv = xop("vector", lambda e: e.tensor_tensor(out=statsW[:, 32:64],
                                                     in0=statsW[:, 32:64],
                                                     in1=statsW2[:, :], op=ALU.subtract),
                 marks=["dv_ln"])["dv_ln"]
        fop("scalar", lambda e: e.activation(out=statsW[:, 32:64], in_=statsW[:, 32:64],
                                             func=AF.Ln, bias=stats[:, 8:9]),
            waits=[("dv_ln", vv)])
        vr = xop("scalar", lambda e: e.activation(out=statsW[:, 32:64],
                                                  in_=statsW[:, 32:64],
                                                  func=AF.Exp, scale=-0.5),
                 marks=["ac_xn"])["ac_xn"]
        return S.op("vector",
                    lambda e: e.scalar_tensor_tensor(out=statsW[:, 0:32],
                                                     in0=statsW[:, 0:32], scalar=-1.0,
                                                     in1=statsW[:, 32:64],
                                                     op0=ALU.mult, op1=ALU.mult),
                    inc=[("dv_ln", 1)], waits=[("ac_xn", vr)])["dv_ln"]

    def sched_ph1_passB(h, v_wide):
        t = h // 2
        hb = h % 2
        wl = [("dv_ax", v_ax[h - 2] if h >= 2 else 0),
              ("pe_t", max(0, (h - 1) * 4))]
        if h < 2:
            wl.append(("dv_ln", v_sumq[(NHT - 2 + h, 1)]))
        v_ldx[NHT + h] = dma(
            "sync", "ldx",
            lambda e, h=h, hb=hb: e.dma_start(
                out=xbuf[hb][:, :].rearrange("p (u d) -> p u d", u=2),
                in_=dx[h * HT:(h + 1) * HT, :].rearrange("(u p) d -> p u d", p=128)),
            waits=wl)
        for u in range(2):
            j = 2 * h + u
            xin = xbuf[hb][:, u * 1024:(u + 1) * 1024]
            S.op("vector",
                 lambda e, xin=xin, j=j: e.tensor_scalar(
                     out=xin, in0=xin, scalar1=statsW[:, 32 + j:33 + j],
                     scalar2=statsW[:, j:j + 1], op0=ALU.mult, op1=ALU.add),
                 waits=[v_ldx[NHT + h]] + ([("dv_ln", v_wide)] if h == 0 and u == 0 else []))
        S.op("vector", lambda e, hb=hb: e.tensor_reduce(out=stats[:, 6:7],
                                                        in_=xbuf[hb][:, :],
                                                        axis=AX.X, op=ALU.max,
                                                        apply_absolute_value=True))
        v_ax[h] = xop("vector", lambda e: e.tensor_tensor(out=amax_acc[:, 0:1],
                                                          in0=amax_acc[:, 0:1],
                                                          in1=stats[:, 6:7], op=ALU.max),
                      marks=["dv_ax"])["dv_ax"]
        v_xn = S.mark("vector", "dv_xn")
        for pr in range(4):
            gpair = h * 4 + pr
            bank = pr % 2
            for ci in range(2):
                c = pr * 2 + ci
                for u in range(2):
                    wl = []
                    if ci == 0 and u == 0:
                        wl = [("dv_xn", v_xn)]
                        if gpair >= 2:
                            wl.append(("ac_ev", ev_pair_vals[gpair - 2]))
                    S.op("tensor",
                         lambda e, hb=hb, c=c, u=u, bank=bank: e.transpose(
                             pb[bank][:, (c % 2) * 256 + u * 128:
                                      (c % 2) * 256 + (u + 1) * 128],
                             xbuf[hb][:, u * 1024 + c * 128:u * 1024 + (c + 1) * 128],
                             ident[:, :]),
                         inc=([("pe_t", 1)] if (ci, u) == (1, 1) else None),
                         waits=wl)
                    if (ci, u) == (0, 0):
                        warm_mm(pb[bank])
            vpe = S.val("pe_t")
            c0 = pr * 2
            wl = [("pe_t", vpe)]
            if h % 2 == 0 and pr == 0 and t >= 2:
                wl.append(v_spill[t - 2])
            S.op("scalar",
                 lambda e, t=t, h=h, c0=c0, bank=bank: e.copy(
                     out=xnf[t % 2][:, :].rearrange("p (c w) -> p c w", w=T)
                         [:, c0:c0 + 2, (h % 2) * 256:(h % 2) * 256 + 256],
                     in_=pb[bank].rearrange("p (c w) -> p c w", w=256)),
                 inc=[("ac_ev", 1)], waits=wl)
            ev_pair_vals.append(S.val("ac_ev"))
        if h % 2 == 1:
            v_spill[t] = dma(
                "gpsimd", "stxn",
                lambda e, t=t: e.dma_start(out=dxn[t], in_=xnf[t % 2][:, :]),
                waits=[("ac_ev", ev_pair_vals[-1])])

    wqk = 0
    for h in range(max(PH1_HALVES, NHT if DO_WQ else 0)):
        if h < PH1_HALVES:
            sched_ph1_passA(h)
        if wqk < FC and DO_WQ:
            sched_winq_loads(wqk)
            sched_winq_loads(wqk + 1)
            sched_winq_pair(wqk)
            wqk += 2
    v_wide = sched_ph1_wide() if PH1_HALVES else 0
    for h in range(PH1_HALVES):
        sched_ph1_passB(h, v_wide)
    v_wq_all = S.val("ac_w")
    w_stxn_all = dtotal("stxn")

    def sched_amax_final(col, cc_in_d, cc_out_d, sidx, bcol, in_sem, in_val,
                         ccd="cc1"):
        S.op("tensor",
             lambda e, col=col: e.transpose(pb[5][0:1, 0:128], amax_acc[:, col:col + 1],
                                            ident[:, :]),
             inc=[("pe_sc", 1)], waits=[(in_sem, in_val), ("dv_sc", S.val("dv_sc"))])
        xop("vector",
            lambda e: e.tensor_reduce(out=ccst[0:1, 0:1], in_=pb[5][0:1, 0:128],
                                      axis=AX.X, op=ALU.max),
            marks=["dv_sc"], waits=[("pe_sc", S.val("pe_sc"))])
        S.op("gpsimd", lambda e, d=cc_in_d: e.dma_start(out=d[:, :], in_=ccst[:, :]),
             inc=[("cc", 16)], waits=[("dv_sc", S.val("dv_sc"))])
        if SKIP_CC:
            S.op("gpsimd", lambda e, a=cc_in_d, b=cc_out_d: e.dma_start(
                 out=b[:, :], in_=a[:, :]),
                 inc=[(ccd, "dflt")], waits=[("cc", S.val("cc"))])
        else:
            S.op("gpsimd",
                 lambda e, a=cc_in_d, b=cc_out_d: e.collective_compute(
                     "AllReduce", ALU.max, replica_groups=RG,
                     ins=[a.ap().opt()], outs=[b.ap().opt()]),
                 inc=[(ccd, "dflt")], waits=[("cc", S.val("cc"))])
        S.op("gpsimd", lambda e, d=cc_out_d: e.dma_start(out=ccst[:, :], in_=d[:, :]),
             inc=[("cc", 16)], waits=[(ccd, 1)])
        vcc = S.val("cc")
        vfl = S.op("vector", lambda e, sidx=sidx: e.reciprocal(
             out=scal[:, sidx:sidx + 1], in_=ccst[0:1, 0:1]),
             inc=[("dv_fl", 1)], waits=[("cc", vcc)])["dv_fl"]
        fop("vector", lambda e, sidx=sidx: e.tensor_scalar(
            out=scal[:, sidx:sidx + 1], in0=scal[:, sidx:sidx + 1],
            scalar1=32767.0, scalar2=None, op0=ALU.mult),
            waits=[("dv_fl", vfl)])
        vds = xop("vector", lambda e, sidx=sidx: e.tensor_scalar(
            out=scal[:, sidx + 1:sidx + 2], in0=ccst[0:1, 0:1],
            scalar1=1.0 / 32767.0, scalar2=None, op0=ALU.mult),
            marks=["dv_sc"])["dv_sc"]
        S.op("tensor", lambda e, sidx=sidx: e.matmul(out=pb[5][0:128, 0:2],
                                                     lhsT=ones_r[0:1, 0:128],
                                                     rhs=scal[:, sidx:sidx + 2],
                                                     start=True, stop=True),
             inc=[("pe_sc", 1)], waits=[("dv_sc", vds)])
        vb = xop("vector", lambda e, bcol=bcol: e.tensor_copy(
            out=bca[:, bcol:bcol + 2], in_=pb[5][0:128, 0:2]),
            marks=["dv_sc"], waits=[("pe_sc", S.val("pe_sc"))])["dv_sc"]
        return vb

    w_stwq = dtotal("stwq")
    if DO_WQ:
        for c in range(DC):
            dma("sync", "ldwq",
                lambda e, c=c: e.dma_start(
                    out=wbuf[:, c * 4096:(c + 1) * 4096],
                    in_=dwinT[c], transpose=True),
                waits=(w_stwq if c == 0 else None))
    w_ldwq = dtotal("ldwq")
    v_sx = 0
    if DO_AMAXF:
        v_sx = sched_amax_final(0, dccxi, dccxo, 0, 0, "dv_ax", S.val("dv_ax"))
    if debug_taps:
        dma("gpsimd", "sttap", lambda e: e.dma_start(out=dscal, in_=scal[:, :]),
            waits=[("dv_sc", v_sx)])
        dma("gpsimd", "sttap", lambda e: e.dma_start(out=dwq[0], in_=wbuf[:, :]),
            waits=[("ac_w", v_wq_all)])
        for (sm, v) in dtotal("sttap"):
            S.wait("gpsimd", sm, v)

    if stage <= 1:
        for (sm, v) in w_stxn_all:
            S.wait("gpsimd", sm, v)
        S.wait("gpsimd", "cc", S.val("cc"))
        return finish(nc, ctx, S, semh)

    v_ldxn = [0] * NT
    v_q = [0] * NT
    v_ext = [0] * NT
    v_proj = [0] * NT
    v_mm1_t = [0] * NT
    v_gel = {}
    v_scan = [0] * NT
    v_dvh_pair = {}
    v_sth_pair = {}

    def sched_ssm_front(tf, ext_ready_val):
        first = [True]
        for kk in range(KC):
            for c in range(DC):
                wl = None
                if first[0]:
                    wl = [("dv_ext", ext_ready_val), ("ac_p80", S.val("ac_p80")),
                          ("dv_sc", v_init)]
                    first[0] = False
                S.op("tensor",
                     lambda e, kk=kk, c=c, tf=tf: e.matmul(
                         out=pb[2][0:80, 0:T],
                         lhsT=wt_bf[kk][:, c * 80:(c + 1) * 80],
                         rhs=ext[tf % 2][:, c * EW + kk:c * EW + kk + T],
                         start=(kk == 0 and c == 0),
                         stop=(kk == KC - 1 and c == DC - 1)),
                     inc=([("pe_proj", 1)] if (kk == KC - 1 and c == DC - 1) else None),
                     waits=wl)
        v_proj[tf] = S.val("pe_proj")
        S.op("scalar", lambda e: e.copy(out=p80_sb[:, :], in_=pb[2][0:80, 0:T]),
             inc=[("ac_p80", 1)],
             waits=[("pe_proj", v_proj[tf]), ("pe_r", S.val("pe_r")),
                    ("ac_w", v_wq_all)])
        vp80 = S.val("ac_p80")
        outs = [dl_sb, bi_sb, c_sb]
        for i3 in range(3):
            S.op("tensor",
                 lambda e, i3=i3: e.matmul(out=pb[3][0:NS, 0:T],
                                           lhsT=rmat[:, i3 * NS:(i3 + 1) * NS],
                                           rhs=p80_sb[:, :], start=True, stop=False),
                 waits=[("ac_p80", vp80), ("ac_r", S.val("ac_r"))] +
                       ([("dv_scan", S.val("dv_scan")),
                         ("dv_sc", v_init)] if i3 == 0 else []))
            S.op("tensor",
                 lambda e, i3=i3: e.matmul(out=pb[3][0:NS, 0:T],
                                           lhsT=bdrow[0:1, i3 * NS:(i3 + 1) * NS],
                                           rhs=ones_r[0:1, 0:T],
                                           start=False, stop=True),
                 inc=[("pe_r", 1)])
            S.op("scalar",
                 lambda e, i3=i3: e.copy(out=outs[i3][:, :], in_=pb[3][0:NS, 0:T]),
                 inc=[("ac_r", 1)], waits=[("pe_r", S.val("pe_r"))])
        vac_r = S.val("ac_r")
        if tf == NT - 1:
            for i3 in range(3):
                S.op("vector",
                     lambda e, i3=i3, o=outs[i3]: e.tensor_copy(
                         out=o[:, T - 1:T], in_=bd[:, 3 + i3:4 + i3]),
                     waits=([("ac_r", vac_r)] if i3 == 0 else None))
        S.op("scalar", lambda e: e.activation(out=s1_sb[:, :], in_=dl_sb[:, :],
                                              func=AF.Abs),
             waits=[("ac_r", vac_r), ("dv_scan", S.val("dv_scan"))])
        S.op("scalar", lambda e: e.activation(out=s1_sb[:, :], in_=s1_sb[:, :],
                                              func=AF.Exp, scale=-1.0))
        vsp = S.op("scalar", lambda e: e.activation(out=s1_sb[:, :], in_=s1_sb[:, :],
                                                    func=AF.Ln, bias=1.0),
                   inc=[("ac_r", 1)])["ac_r"]
        S.op("vector", lambda e: e.tensor_scalar(out=s2_sb[:, :], in0=dl_sb[:, :],
                                                 scalar1=0.0, scalar2=None,
                                                 op0=ALU.max),
             waits=[("ac_r", vsp)])
        S.op("vector", lambda e: e.tensor_tensor(out=s2_sb[:, :], in0=s2_sb[:, :],
                                                 in1=s1_sb[:, :], op=ALU.add))
        vda = S.op("vector", lambda e: e.tensor_scalar(out=s2_sb[:, :], in0=s2_sb[:, :],
                                                       scalar1=aset[:, 2:3],
                                                       scalar2=None, op0=ALU.mult),
                   inc=[("dv_scan", 1)])["dv_scan"]
        vab = S.op("scalar", lambda e: e.activation(out=s1_sb[:, :], in_=s2_sb[:, :],
                                                    func=AF.Exp),
                   inc=[("ac_r", 1)], waits=[("dv_scan", vda)])["ac_r"]
        S.op("vector", lambda e: e.tensor_scalar(out=dl_sb[:, :], in0=s2_sb[:, :],
                                                 scalar1=0.5, scalar2=1.0,
                                                 op0=ALU.mult, op1=ALU.add))
        S.op("vector", lambda e: e.tensor_tensor(out=s2_sb[:, :], in0=s2_sb[:, :],
                                                 in1=s2_sb[:, :], op=ALU.mult))
        S.op("vector", lambda e: e.scalar_tensor_tensor(out=dl_sb[:, :],
                                                        in0=s2_sb[:, :],
                                                        scalar=1.0 / 6.0,
                                                        in1=dl_sb[:, :],
                                                        op0=ALU.mult, op1=ALU.add))
        S.op("vector", lambda e: e.tensor_scalar(out=s2_sb[:, :], in0=s1_sb[:, :],
                                                 scalar1=-1.0, scalar2=aset[:, 6:7],
                                                 op0=ALU.add, op1=ALU.mult),
             waits=[("ac_r", vab)])
        S.op("vector", lambda e: e.scalar_tensor_tensor(out=s2_sb[:, :],
                                                        in0=dl_sb[:, :],
                                                        scalar=aset[:, 3:4],
                                                        in1=s2_sb[:, :],
                                                        op0=ALU.mult, op1=ALU.add))
        vfl = S.op("vector", lambda e: e.tensor_tensor(out=s2_sb[:, :],
                                                       in0=s2_sb[:, :],
                                                       in1=bi_sb[:, :], op=ALU.mult),
                   inc=[("dv_fl", 1)])["dv_fl"]
        hcur = hb_sb[tf % 2]
        S.wait("vector", "dv_fl", vfl)
        if tf == 0:
            vfl = S.op("vector", lambda e, hcur=hcur: e.tensor_tensor_scan(
                 out=hcur[:, :], data0=s1_sb[:, :], data1=s2_sb[:, :], initial=0.0,
                 op0=ALU.mult, op1=ALU.add), inc=[("dv_fl", 1)])["dv_fl"]
        else:
            hprev = hb_sb[(tf - 1) % 2]
            vfl = S.op("vector", lambda e, hcur=hcur, hprev=hprev: e.tensor_tensor_scan(
                 out=hcur[:, :], data0=s1_sb[:, :], data1=s2_sb[:, :],
                 initial=hprev[:, T - 1:T], op0=ALU.mult, op1=ALU.add),
                 inc=[("dv_fl", 1)])["dv_fl"]
        S.op("vector", lambda e, tf=tf, hcur=hcur: e.tensor_tensor(
             out=chb[:, tf * T:(tf + 1) * T], in0=c_sb[:, :], in1=hcur[:, :],
             op=ALU.mult),
             inc=[("dv_scan", 1)], waits=[("dv_fl", vfl)])
        v_scan[tf] = S.val("dv_scan")

    v_wo_u = [0] * FC
    v_stws_k = {}

    def sched_woq(k):
        tok = dma(
            "sync", "ldwo",
            lambda e, k=k: e.dma_start(out=wof[k % 2][:, :],
                                       in_=dwoutf[k * 128:(k + 1) * 128, :]),
            waits=[("dv_wo", v_wo_u[k - 2] if k >= 2 else 0)] +
                  ([("dv_ax", 16), ("pe_t", 4 * NHT)] if k == 0 else []))
        vfl = S.op("vector", lambda e, k=k: e.tensor_reduce(out=stats[:, 6:7],
                                                            in_=wof[k % 2][:, :],
                                                            axis=AX.X, op=ALU.max,
                                                            apply_absolute_value=True),
                   inc=[("dv_fl", 1)],
                   waits=[tok] + ([v_stws_k[k - 2]] if k >= 2 else []))["dv_fl"]
        vfl = S.op("vector", lambda e: e.reciprocal(out=stats[:, 7:8],
                                                    in_=stats[:, 6:7]),
                   inc=[("dv_fl", 1)], waits=[("dv_fl", vfl)])["dv_fl"]
        fop("vector", lambda e: e.tensor_scalar(out=stats[:, 7:8], in0=stats[:, 7:8],
                                                scalar1=127.0, scalar2=None,
                                                op0=ALU.mult),
            waits=[("dv_fl", vfl)])
        fop("vector", lambda e: e.tensor_scalar(out=stats[:, 6:7], in0=stats[:, 6:7],
                                                scalar1=1.0 / 127.0, scalar2=None,
                                                op0=ALU.mult))
        vz = round_scaled(("dv_w", "pq"), 1024, wof[k % 2][:, :],
                          wsim_bf[k % 2][:, :], stats[:, 7:8], stats[:, 6:7],
                          first_waits=None, first_inc=[("dv_wo", 1)])
        v_wo_u[k] = S.val("dv_wo")
        v_stws_k[k] = dma(
            "gpsimd", "stws",
            lambda e, k=k: e.dma_start(out=dwsim[k], in_=wsim_bf[k % 2][:, :]),
            waits=[("dv_w", vz)])

    def ext_view(i):
        return ext[i].rearrange("p (c w) -> p c w", w=EW)

    def sched_ph2_front_q(t):
        v_ldxn[t] = dma(
            "sync", "ldxn",
            lambda e, t=t: e.dma_start(out=xnf[t % 2][:, :], in_=dxn[t]),
            waits=w_stxn_all + [("dv_q", v_q[t - 2] if t >= 2 else 0),
                                ("dv_ext", v_ext[t - 2] if t >= 2 else 0)])
        for pc in range(4):
            w = 1024
            xin = xnf[t % 2][:, pc * w:(pc + 1) * w]
            qout = qb[t % 2][:, pc * w:(pc + 1) * w]
            wl = []
            if pc == 0:
                wl.append(v_ldxn[t])
            if t == 0 and pc == 0:
                wl.append(("dv_sc", v_sx))
            if t >= 2 and pc == 0:
                wl.append(("pe_mm1", v_mm1_t[t - 2]))
            round_scaled(("dv_q", "pq"), w, xin, qout, bca[:, 0:1], None,
                         first_waits=wl)
        v_q[t] = S.val("dv_q")

    def sched_ph2_front_e(t):
        S.op("vector",
             lambda e, t=t: e.tensor_copy(
                 out=ext_view(t % 2)[:, :, 1:1 + T],
                 in_=xnf[t % 2][:, :].rearrange("p (c w) -> p c w", w=T)),
             waits=[v_ldxn[t],
                    ("pe_proj", v_proj[t - 2] if t >= 2 else 0)] +
                   ([("pe_sc", S.val("pe_sc")), ("ac_xn", S.val("ac_xn")),
                     ("dv_ln", S.val("dv_ln"))]
                    if t == 0 else []) +
                   ([("ac_w", v_wq_all), ("dv_w", S.val("dv_w")),
                     ("pe_wb", S.val("pe_wb"))] if t < 2 else []))
        if t == 0:
            S.op("vector", lambda e: e.memset(ext_view(0)[:, :, 0:1], 0.0),
                 inc=[("dv_ext", 1)])
        else:
            S.op("vector", lambda e, t=t: e.tensor_copy(
                 out=ext_view(t % 2)[:, :, 0:1],
                 in_=ext_view((t - 1) % 2)[:, :, T:T + 1]))
            S.op("vector", lambda e, t=t: e.tensor_copy(
                 out=ext_view((t - 1) % 2)[:, :, T + 1:T + 3],
                 in_=ext_view(t % 2)[:, :, 1:3]),
                 inc=[("dv_ext", 1)])
        if t == NT - 1:
            S.op("vector", lambda e: e.memset(
                 ext_view((NT - 1) % 2)[:, :, T + 1:T + 3], 0.0),
                 inc=[("dv_ext", 1)])
        v_ext[t] = S.val("dv_ext")

    def sched_ph2_body(t):
        if t + 1 < NT:
            sched_ph2_front_q(t + 1)
        if t >= 1:
            sched_ssm_front(t - 1, v_ext[t])
        if t + 1 < NT:
            sched_ph2_front_e(t + 1)

        for s in range(FC):
            bank = (0, 1, 6, 7)[s % 4]
            gi = t * FC + s
            for c in range(DC):
                wl = None
                if c == 0:
                    wl = [("dv_q", v_q[t]),
                          ("ac_g", v_gel[gi - 4] if gi >= 4 else 0)]
                    if gi == 0:
                        wl.extend(w_ldwq)
                S.op("tensor",
                     lambda e, t=t, s=s, c=c, bank=bank: e.matmul(
                         out=pb[bank],
                         lhsT=wbuf[:, c * 4096 + s * 128:c * 4096 + (s + 1) * 128],
                         rhs=qb[t % 2][:, c * T:(c + 1) * T],
                         start=(c == 0), stop=(c == DC - 1)),
                     inc=([("pe_mm1", 1)] if c == DC - 1 else None), waits=wl)
            vmm = S.val("pe_mm1")
            hb2 = (gi // 2) % 2
            wl = [("pe_mm1", vmm)]
            if gi == 0:
                wl.append(("dv_sc", v_sx))
            if gi <= 3:
                wl.append(("pe_w", FC * DC))
            if gi >= 4:
                wl.append(v_sth_pair[gi // 2 - 2])
                wl.append(("dv_h", v_dvh_pair[gi // 2 - 2]))
            v_gel[gi] = S.op("scalar",
                             lambda e, s=s, bank=bank, hb2=hb2: e.activation(
                                 out=hidb[hb2][:, (s % 2) * T:(s % 2 + 1) * T],
                                 in_=pb[bank], func=AF.Gelu,
                                 bias=binf_sb[:, s:s + 1], scale=bca[:, 1:2]),
                             inc=[("ac_g", 1)], waits=wl)["ac_g"]
            if s % 2 == 1:
                pc2 = gi // 2
                S.op("vector",
                     lambda e, hb2=hb2, pc2=pc2: e.tensor_reduce(
                         out=hstrip[:, pc2:pc2 + 1], in_=hidb[hb2][:, :],
                         axis=AX.X, op=ALU.max, apply_absolute_value=True),
                     inc=[("dv_h", 1)], waits=[("ac_g", v_gel[gi])])
                v_dvh_pair[gi // 2] = S.val("dv_h")
                v_sth_pair[gi // 2] = dma(
                    "gpsimd", "sth",
                    lambda e, t=t, s=s, hb2=hb2: e.dma_start(out=dhid[t, s // 2],
                                                             in_=hidb[hb2][:, :]),
                    waits=[("ac_g", v_gel[gi])])
        v_mm1_t[t] = S.val("pe_mm1")
        for kq in range(4 * t, min(4 * t + 4, FC)):
            sched_woq(kq)

    sched_ph2_front_q(0)
    sched_ph2_front_e(0)
    for t in range(NT):
        sched_ph2_body(t)
    sched_ssm_front(NT - 1, v_ext[NT - 1])

    xop("vector", lambda e: e.tensor_reduce(out=amax_acc[:, 1:2],
                                            in_=hstrip[:, :], axis=AX.X,
                                            op=ALU.max),
        marks=["dv_h"], waits=[("dv_h", S.val("dv_h"))])
    v_sh = sched_amax_final(1, dcchi, dccho, 3, 2, "dv_h", S.val("dv_h"), ccd="cc2")
    if debug_taps:
        dma("gpsimd", "sttap", lambda e: e.dma_start(out=dscal, in_=scal[:, :]),
            waits=[("dv_sc", v_sh)])
        dma("gpsimd", "sttap", lambda e: e.dma_start(out=dchb, in_=chb[:, :]),
            waits=[("dv_scan", v_scan[NT - 1])])
        for (sm, v) in dtotal("sttap"):
            S.wait("gpsimd", sm, v)
    w_sth_all = dtotal("sth")
    w_stws_all = dtotal("stws")

    if stage <= 2:
        for (sm, v) in w_sth_all + w_stws_all:
            S.wait("gpsimd", sm, v)
        S.wait("gpsimd", "cc", S.val("cc"))
        return finish(nc, ctx, S, semh)

    for g4 in range(8):
        dma("sync", "ldws",
            lambda e, g4=g4: e.dma_start(
                out=wbuf[:, g4 * 4096:(g4 + 1) * 4096].rearrange(
                    "p (k w) -> p k w", k=4),
                in_=dwsim[g4 * 4:(g4 + 1) * 4].rearrange("k p w -> p k w")),
            waits=([("pe_mm1", NT * FC)] + w_stws_all if g4 == 0 else None))
    w_ldws = dtotal("ldws")

    v_qh_k = {}
    v_mm2_k = {}
    v_sto = {}
    v_y_s = {}
    v_tr_tile = {}
    v_sg = {}
    v_res = {}
    v_sl = {}
    v_ldx3 = {}

    v_ac_g_all = v_gel[NT * FC - 1]
    v_mm2a_k = {}
    v_trA = {}
    v_trB = {}
    for t in range(NT):
        for u in range(4):
            if t == 0:
                if u < 2:
                    wl = [("dv_ax", 16), ("pe_t", 4 * NHT),
                          ("dv_ln", S.val("dv_ln"))]
                else:
                    wl = [("dv_wo", v_wo_u[FC - 4 + u]),
                          ("dv_ln", S.val("dv_ln"))]
            else:
                wl = [("dv_res", v_res[(t - 1, u, 1)])]
            v_ldx3[(t, u)] = dma(
                "sync", "ldx3",
                lambda e, t=t, u=u: e.dma_start(
                    out=xr3[u][:, :],
                    in_=dx[t * T + u * 128:t * T + (u + 1) * 128, :]),
                waits=wl)

        def emit_quant(tq, g_lo, g_hi):
            for g in range(g_lo, g_hi):
                if 2 * g >= 4:
                    wl = [("dv_qh", v_qh_k[(tq, 2 * g - 4 + 1)])]
                elif tq >= 1:
                    wl = [("dv_qh", v_qh_k[(tq - 1, FC + 2 * g - 4 + 1)])]
                else:
                    wl = w_sth_all + [("dv_h", S.val("dv_h"))]
                tok_h = dma(
                    "sync", "ldh",
                    lambda e, tq=tq, g=g: e.dma_start(out=hidb[g % 2][:, :],
                                                      in_=dhid[tq, g]),
                    waits=wl)
                for kk2 in range(2):
                    k = 2 * g + kk2
                    qdst = qh[k // 16][:, (k % 16) * T:(k % 16 + 1) * T]
                    hin = hidb[g % 2][:, kk2 * T:(kk2 + 1) * T]
                    wl = [tok_h]
                    if tq == 0 and k == 0:
                        wl.append(("dv_sc", v_sh))
                        wl.append(("dv_q", v_q[NT - 1]))
                        wl.append(("dv_ext", v_ext[NT - 1]))
                    if tq >= 1:
                        wl.append(("pe_mm2", v_mm2_k[(tq - 1, k)]))
                    v_qh_k[(tq, k)] = round_scaled(("dv_qh", "pq"), T, hin, qdst,
                                                   bca[:, 2:3], None,
                                                   first_waits=wl)

        if t == 0:
            emit_quant(0, 0, FC // 2)
        for k in range(FC):
            for sdx in range(4):
                wl = None
                if sdx == 0:
                    wl = [("dv_qh", v_qh_k[(t, k)])]
                    if k == 0:
                        wl.extend(w_ldws)
                        if t == 0:
                            wl.append(("dv_scan", v_scan[NT - 1]))
                            wl.append(("ac_g", v_ac_g_all))
                            wl.append(("ac_p80", S.val("ac_p80")))
                            wl.append(("ac_r", S.val("ac_r")))
                            wl.append(("ac_w", v_wq_all))
                            wl.append(("dv_sc", v_sh))
                        else:
                            wl.append(("ac_sl", v_sl[(t - 1, 3)]))
                            wl.append(("dv_sg", v_sg[(t - 1, 3)]))
                            wl.append(("dv_res", v_res[(t - 1, 3, 0)]))
                S.op("tensor",
                     lambda e, t=t, k=k, sdx=sdx: e.matmul(
                         out=pb[sdx],
                         lhsT=wbuf[:, k * 1024 + sdx * 128:k * 1024 + (sdx + 1) * 128],
                         rhs=qh[k // 16][:, (k % 16) * T:(k % 16 + 1) * T],
                         start=(k == 0), stop=(k == FC - 1)),
                     inc=([("pe_mm2", 1)] if sdx == 3 else None), waits=wl)
            v_mm2a_k[(t, k)] = S.val("pe_mm2")
        for k in range(FC):
            for sdx in range(4, 8):
                wl = None
                if sdx == 4 and k == 0:
                    if t == 0:
                        wl = [("ac_g", v_ac_g_all)]
                    else:
                        wl = [("ac_sl", v_sl[(t - 1, 7)]),
                              ("dv_sg", v_sg[(t - 1, 7)]),
                              ("dv_res", v_res[(t - 1, 3, 1)])]
                S.op("tensor",
                     lambda e, t=t, k=k, sdx=sdx: e.matmul(
                         out=pb[sdx],
                         lhsT=wbuf[:, k * 1024 + sdx * 128:k * 1024 + (sdx + 1) * 128],
                         rhs=qh[k // 16][:, (k % 16) * T:(k % 16 + 1) * T],
                         start=(k == 0), stop=(k == FC - 1)),
                     inc=([("pe_mm2", 1)] if sdx == 7 else None), waits=wl)
            v_mm2_k[(t, k)] = S.val("pe_mm2")
        if t + 1 < NT:
            emit_quant(t + 1, 0, 4)
        for sdx in range(8):
            half_last = v_mm2a_k[(t, FC - 1)] if sdx < 4 else v_mm2_k[(t, FC - 1)]
            wl = [("pe_mm2", half_last)]
            if sdx == 0:
                if t >= 1:
                    wl.append(("pe_tr", v_trA[t - 1]))
                else:
                    wl.append(("dv_sc", v_sh))
                    wl.append(("dv_scan", v_scan[NT - 1]))
            if sdx == 4 and t >= 1:
                wl.append(("pe_tr", v_trB[t - 1]))
            v_sl[(t, sdx)] = S.op("scalar",
                                  lambda e, sdx=sdx: e.activation(
                                      out=sg_sb[sdx][:, :], in_=pb[sdx],
                                      func=AF.Silu,
                                      bias=boutf_sb[:, sdx:sdx + 1],
                                      scale=bca[:, 3:4]),
                                  inc=[("ac_sl", 1)], waits=wl)["ac_sl"]

        def emit_half(h2):
            for si in range(4):
                sdx = h2 * 4 + si
                S.op("tensor",
                     lambda e, t=t, sdx=sdx: e.matmul(
                         out=pb[sdx],
                         lhsT=wouts_bf[0:NS, sdx * 128:(sdx + 1) * 128],
                         rhs=chb[0:NS, t * T:(t + 1) * T], start=True, stop=True),
                     inc=[("pe_y", 1)],
                     waits=[("ac_sl", v_sl[(t, sdx)])] +
                           ([v_wouts] if (t == 0 and sdx == 0) else []))
                v_y_s[(t, sdx)] = S.val("pe_y")
            for si in range(4):
                sdx = h2 * 4 + si
                v_sg[(t, sdx)] = S.op("vector",
                                      lambda e, sdx=sdx: e.scalar_tensor_tensor(
                                          out=sg_sb[sdx][:, :], in0=pb[sdx],
                                          scalar=bouts_sb[:, sdx:sdx + 1],
                                          in1=sg_sb[sdx][:, :],
                                          op0=ALU.add, op1=ALU.mult),
                                      inc=[("dv_sg", 1)],
                                      waits=[("pe_y", v_y_s[(t, sdx)])])["dv_sg"]
            for u in range(4):
                bank = 4 * h2 + u % 2
                for si in range(4):
                    sdx = h2 * 4 + si
                    wl = None
                    if si == 0:
                        wl = [("dv_sg", v_sg[(t, h2 * 4 + 3)])]
                        if u >= 2:
                            wl.append(("dv_res", v_res[(t, u - 2, h2)]))
                    S.op("tensor",
                         lambda e, sdx=sdx, u=u, bank=bank, si=si: e.transpose(
                             pb[bank][:, si * 128:(si + 1) * 128],
                             sg_sb[sdx][:, u * 128:(u + 1) * 128], ident[:, :]),
                         inc=([("pe_tr", 1)] if si == 3 else None), waits=wl)
                vtr = S.val("pe_tr")
                wl = [("pe_tr", vtr), v_ldx3[(t, u)]]
                if t == 0 and u == 0 and h2 == 0:
                    wl.append(("pe_proj", v_proj[NT - 1]))
                    wl.append(("pe_mm1", NT * FC))
                if t >= 2 and u == 0 and h2 == 0:
                    wl.append(v_sto[t - 2])
                v_res[(t, u, h2)] = S.op(
                    "vector",
                    lambda e, t=t, u=u, h2=h2, bank=bank: e.tensor_tensor(
                        out=outb[t % 2][:, u * 1024 + h2 * 512:
                                        u * 1024 + (h2 + 1) * 512],
                        in0=pb[bank],
                        in1=xr3[u][:, h2 * 512:(h2 + 1) * 512],
                        op=ALU.add),
                    inc=[("dv_res", 1)], waits=wl)["dv_res"]

        emit_half(0)
        v_trA[t] = S.val("pe_tr")
        emit_half(1)
        v_trB[t] = S.val("pe_tr")
        if t + 1 < NT:
            emit_quant(t + 1, 4, FC // 2)
        v_sto[t] = dma(
            "gpsimd", "sto",
            lambda e, t=t: e.dma_start(
                out=dout[t * T:(t + 1) * T, :].rearrange("(u p) d -> p u d", p=128),
                in_=outb[t % 2][:, :].rearrange("p (u d) -> p u d", u=4)),
            waits=[("dv_res", v_res[(t, 3, 1)])])

    for (sm, v) in dtotal("sto"):
        S.wait("gpsimd", sm, v)
    return finish(nc, ctx, S, semh)


def finish(nc, ctx, S, semh):
    with nc.Block() as block:
        def mk(ename):
            def emitter(eng):
                for it in S.items:
                    if it[1] != ename:
                        continue
                    if it[0] == "w":
                        eng.wait_ge(semh[it[2]], it[3])
                    elif it[0] == "m":
                        eng.sem_inc(semh[it[2]], 1)
                    else:
                        inst = it[2](eng)
                        for (s, a) in it[3]:
                            if a == "dflt":
                                inst.then_inc(semh[s])
                            else:
                                inst.then_inc(semh[s], a)
            return emitter
        block.sync(mk("sync"))
        block.tensor(mk("tensor"))
        block.vector(mk("vector"))
        block.scalar(mk("scalar"))
        block.gpsimd(mk("gpsimd"))
    ctx.close()
    return nc


def make_consts():
    ident = np.eye(128, dtype=np.float32)
    ident_bf = np.eye(128, dtype=ml_dtypes.bfloat16)
    ones = np.ones((1, 512), dtype=np.float32)
    R = np.zeros((80, 48), dtype=np.float32)
    for n in range(16):
        R[n, n] = 1.0
        R[16 + 2 * n, 16 + n] = 0.5
        R[17 + 2 * n, 16 + n] = 0.5
        R[48 + 2 * n, 32 + n] = 0.5
        R[49 + 2 * n, 32 + n] = 0.5
    return ident, ident_bf, ones, R


def make_in_maps(inputs):
    ident, ident_bf, ones, R = make_consts()
    x = np.ascontiguousarray(np.asarray(inputs["L0_attn_x"], dtype=np.float32))
    common = {
        "conv_w": np.ascontiguousarray(
            np.asarray(inputs["L0_attn_conv_w"], np.float32).reshape(DC, 128, KC)
            .transpose(1, 0, 2).reshape(128, DC * KC)),
        "conv_b": np.ascontiguousarray(
            np.asarray(inputs["L0_attn_conv_b"], np.float32).reshape(DC, 128)),
        "Win": np.ascontiguousarray(
            np.asarray(inputs["L0_attn_Win"], np.float32).reshape(DC, 128, 80)
            .transpose(1, 0, 2).reshape(128, DC * 80)),
        "b_in": np.ascontiguousarray(
            np.asarray(inputs["L0_attn_b_in"], np.float32).reshape(1, 80)),
        "Wout_s": np.ascontiguousarray(np.asarray(inputs["L0_attn_Wout"], np.float32)),
        "b_out_s": np.ascontiguousarray(
            np.asarray(inputs["L0_attn_b_out"], np.float32).reshape(DC, 128)),
        "A_log": np.ascontiguousarray(
            np.asarray(inputs["L0_attn_A_log"], np.float32).reshape(1, NS)),
        "W_in_f": np.ascontiguousarray(np.asarray(inputs["L0_ffn_W_in"], np.float32)),
        "b_in_f": np.ascontiguousarray(
            np.asarray(inputs["L0_ffn_b_in"], np.float32).reshape(FC, 128)),
        "W_out_f": np.ascontiguousarray(np.asarray(inputs["L0_ffn_W_out"], np.float32)),
        "b_out_f": np.ascontiguousarray(
            np.asarray(inputs["L0_ffn_b_out"], np.float32).reshape(DC, 128)),
        "ident": ident, "ident_bf": ident_bf, "ones_r": ones, "Rmat": R,
    }
    return [dict(common, x=np.ascontiguousarray(x[c])) for c in range(B)]


def kernel(**inputs):
    in_maps = make_in_maps(inputs)
    nc = build_program(stage=4, debug_taps=False)
    res = run_bass_kernel_spmd(nc, in_maps, core_ids=list(range(B)))
    return np.stack([res.results[c]["out"] for c in range(B)], axis=0)
